# version 17
# baseline (speedup 1.0000x reference)
import contextlib
import numpy as np
import concourse.bass as bass
import concourse.mybir as mybir
from concourse.bass import IndirectOffsetOnAxis
from concourse.bass_utils import run_bass_kernel_spmd

F32 = mybir.dt.float32
BF16 = mybir.dt.bfloat16
AF = mybir.ActivationFunctionType
ALU = mybir.AluOpType
AX = mybir.AxisListType

D, T, H, E, NH, HD = 1024, 2048, 4096, 8, 16, 64
DC, TQ, TC, TH, HC = 8, 4, 16, 2, 32
NCORES = 8
RMS_EPS = 1e-6
LN_EPS = 1e-5
CAP = 384
I32 = mybir.dt.int32


def ts(i, n):
    return slice(i * n, (i + 1) * n)


class Ctr:
    def __init__(self, h):
        self.h = h
        self.n = 0


class B:
    ENGINES = ("tensor", "vector", "scalar", "gpsimd", "sync")

    def __init__(self, nc, stk):
        self.nc = nc
        self.stk = stk
        self.q = {e: [] for e in self.ENGINES}
        self.ctr = {}
        for name in ("pe", "v", "sc", "gp", "dg", "ds"):
            self.ctr[name] = Ctr(stk.enter_context(nc.semaphore("s_" + name)))
        self.eng2ctr = {"tensor": "pe", "vector": "v", "scalar": "sc",
                        "gpsimd": "gp", "sync": None}
        self.serial_dma = {id(self.ctr["dg"]), id(self.ctr["ds"])}
        self.bank = {i: {"w": None, "r": {}} for i in range(8)}

    def dsem(self, name):
        if name not in self.ctr:
            self.ctr[name] = Ctr(self.stk.enter_context(self.nc.semaphore(name)))
        return self.ctr[name]

    def op(self, eng, builder, waits=(), inc=None, dma=False, dma_sem=None):
        ww = {}
        waits = list(waits)
        if eng in ("vector", "scalar") and not dma:
            cs = self.ctr[self.eng2ctr[eng]]
            if cs.n > 0:
                waits.append((cs, cs.n))
        for w in waits:
            if w is None:
                continue
            c, v = w
            if v is None:
                continue
            if id(c) in self.serial_dma:
                v = c.n
            ww[id(c)] = (c.h, max(v, ww[id(c)][1]) if id(c) in ww else v)
        ww = list(ww.values())
        incs = []
        if dma:
            if dma_sem is not None:
                c = self.dsem(dma_sem)
            else:
                c = self.ctr["dg" if eng == "gpsimd" else "ds"]
            c.n += 16
            incs = [(c.h, 16)]
            mark = (c, c.n)
        elif inc is False:
            mark = None
        else:
            c = self.ctr[self.eng2ctr[eng]]
            assert c is not None
            c.n += 1
            incs = [(c.h, 1)]
            mark = (c, c.n)

        def go(engine, ww=ww, incs=incs, builder=builder):
            for h, v in ww:
                engine.wait_ge(h, v)
            ins = builder(engine)
            for h, k in incs:
                ins.then_inc(h, k)
        self.q[eng].append(go)
        return mark

    def wait_only(self, eng, waits):
        ww = []
        for w in waits:
            if w is None or w[1] is None:
                continue
            vv_ = w[0].n if id(w[0]) in self.serial_dma else w[1]
            ww.append((w[0].h, vv_))

        def go(engine, ww=ww):
            for h, v in ww:
                engine.wait_ge(h, v)
        self.q[eng].append(go)

    def marks_all(self):
        return [(c, c.n) for c in self.ctr.values()]

    def bar(self):
        tgt = self.marks_all()
        for eng in self.ENGINES:
            self.wait_only(eng, tgt)
        for b in self.bank.values():
            b["w"] = None
            b["r"] = {}

    def bank_waits_for_write(self, banks):
        w = []
        for b in banks:
            w += list(self.bank[b]["r"].values())
        return w

    def note_write(self, banks, mark):
        for b in banks:
            self.bank[b]["w"] = mark
            self.bank[b]["r"] = {}

    def read_waits(self, banks):
        return [self.bank[b]["w"] for b in banks if self.bank[b]["w"] is not None]

    def note_read(self, banks, eng, mark):
        for b in banks:
            self.bank[b]["r"][eng] = mark

    def mm(self, banks, builder, extra=(), first=True, last=True, force_inc=False):
        waits = list(extra)
        if first:
            waits += self.bank_waits_for_write(banks)
        mark = self.op("tensor", builder,
                       waits=waits, inc=(None if (last or force_inc) else False))
        if last:
            self.note_write(banks, mark)
        return mark if (last or force_inc) else None

    def rd(self, eng, banks, builder, extra=()):
        waits = list(extra) + self.read_waits(banks)
        mark = self.op(eng, builder, waits=waits)
        self.note_read(banks, eng, mark)
        return mark

    def replay(self, block):
        for eng in self.ENGINES:
            items = self.q[eng]
            if not items:
                continue

            def body(engine, items=items):
                for fn in items:
                    fn(engine)
            getattr(block, eng)(body)


def build(dbg=()):
    nc = bass.Bass("TRN2", target_bir_lowering=False, debug=False)

    def din(name, shape, dtype=F32):
        return nc.dram_tensor(name, shape, dtype, kind="ExternalInput").ap()

    x0 = din("x0", [D, T])
    ident = din("ident", [128, 128])
    ones = din("ones", [128, 128])
    Wqkv = din("Wqkv", [48, D, HD])
    bqkvT = din("bqkvT", [HD, 48])
    bqkv_flat = din("bqkv_flat", [1, 48 * HD])
    WpT = din("WpT", [D, D])
    bp_cols = din("bp_cols", [128, DC])
    Wg_l = [din("Wg0", [D, E]), din("Wg1", [D, E])]
    bg_l = [din("bg0", [1, E]), din("bg1", [1, E])]
    W1_l = [din("W1_0", [E, D, H]), din("W1_1", [E, D, H])]
    W2_l = [din("W2_0", [E, H, D]), din("W2_1", [E, H, D])]
    b1c = din("b1c", [E, 128, 32])
    b2c = din("b2c", [E, 128, DC])
    opwT = din("opwT", [D, 2016])
    opb = din("opb", [1, 2016])
    oowT = din("oowT", [HD, D])
    oobc = din("oobc", [128, DC])
    vpiwT = din("vpiwT", [D, HD])
    vpib = din("vpib", [1, HD])
    vpowT = din("vpowT", [HD, D])
    vpobc = din("vpobc", [128, DC])
    lt128 = din("lt128", [128, 128])
    ecm1 = din("ecm1", [128, E])
    b2r = din("b2r", [E, 1, D])
    cbp = din("cbp", [128, 128])
    mixes = din("mixes", [1, 2])
    nwc = din("nwc", [128, DC])

    ok = lambda n: "ExternalOutput" if n in dbg else "Internal"
    xn_d = nc.dram_tensor("xn_d", [128, DC, T], BF16, kind=ok("xn_d")).ap()
    x_d = nc.dram_tensor("x_d", [128, DC, T], F32, kind=ok("x_d")).ap()
    sg_d = nc.dram_tensor("sg_dump", [128, DC * T], BF16, kind="ExternalOutput").ap() if "sg" in dbg else None
    sg_d1 = nc.dram_tensor("sg_dump1", [128, DC * T], BF16, kind="ExternalOutput").ap() if "sg1" in dbg else None
    out_fm = nc.dram_tensor("out_fm", [128, DC, T], F32, kind="ExternalOutput").ap()
    ind_d = nc.dram_tensor("ind_d", [8, T], F32, kind="Internal").ap()
    xg_d = nc.dram_tensor("xg_d", [E * CAP, D], BF16, kind="Internal").ap()
    w1bf_d = nc.dram_tensor("w1bf_d", [E, D, H], BF16, kind="Internal").ap()
    w2bf_d = nc.dram_tensor("w2bf_d", [E, H, D], BF16, kind="Internal").ap()
    ytm_d = nc.dram_tensor("ytm_d", [E * CAP, D], BF16, kind="Internal").ap()
    gtm_d = nc.dram_tensor("gtm_d", [T, D], BF16, kind="Internal").ap()

    stk = contextlib.ExitStack()
    sbuf = lambda name, shape, dtype: stk.enter_context(nc.sbuf_tensor(name, shape, dtype))
    K = B(nc, stk)
    pe, v, sc, dg, ds = (K.ctr[k] for k in ("pe", "v", "sc", "dg", "ds"))

    pb01 = [stk.enter_context(nc.psum_tensor(f"pb{i}", [128, 512], F32)) for i in range(2)]
    p4 = stk.enter_context(nc.psum_tensor("p4", [128, 2048], F32))
    pbig1_t = stk.enter_context(nc.psum_tensor("pbig1", [128, 1024], F32))
    pb = [pb01[0].ap() if hasattr(pb01[0], "ap") else pb01[0], None, None, None]
    pb[0] = pb01[0]
    pb[1] = pb01[1]
    pb[2] = p4[:, 0:512]
    pb[3] = p4[:, 512:1024]
    pbig = [p4[:, 1024:2048], pbig1_t]

    sg = stk.enter_context(nc.sbuf_tensor("sg", [128, DC * T], BF16, is_transpose_dest=True))
    lt_sb = sbuf("lt_sb", [128, 128], F32)
    ecm1_sb = sbuf("ecm1_sb", [128, E], F32)
    ind_all = sbuf("ind_all", [128, TC * E], F32)
    cs_rows = sbuf("cs_rows", [1, 128], F32)
    bases = sbuf("bases", [1, 128], F32)
    tmp8 = sbuf("tmp8", [128, E], F32)
    tcol = sbuf("tcol", [128, 1], F32)
    tsl_sb = sbuf("tsl_sb", [128, TC], I32)
    pc_stage = [sbuf(f"pc_stage{i}", [128, 2048], BF16) for i in range(3)]
    ident_sb = sbuf("ident_sb", [128, 128], F32)
    ones_sb = sbuf("ones_sb", [128, 128], F32)
    ind_fm = sbuf("ind_fm", [8, T], F32)
    mrow_sb = sbuf("mrow_sb", [1, 1024], F32)
    hm_cols = sbuf("hm_cols", [128, DC], F32)
    hm_bf = sbuf("hm_bf", [128, DC], BF16)
    bg_rep = sbuf("bg_rep", [128, E], F32)
    rowC = sbuf("rowC", [1, 512], F32)
    ltm = sbuf("ltm", [128, E], F32)
    mxcol = sbuf("mxcol", [128, 1], F32)
    ind_tm = sbuf("ind_tm", [128, E], F32)
    wg_bf = sbuf("wg_bf", [128, DC * E], F32)
    rstdc_sb = sbuf("rstdc_sb", [128, 4], F32)
    bgrow = sbuf("bgrow", [1, E], F32)
    bp_sb = sbuf("bp_sb", [128, DC], F32)
    bqkvT_sb = sbuf("bqkvT_sb", [HD, 48], F32)
    oobc_sb = sbuf("oobc_sb", [128, DC], F32)
    vpobc_sb = sbuf("vpobc_sb", [128, DC], F32)
    nwc_sb = sbuf("nwc_sb", [128, DC], F32)
    opb_sb = sbuf("opb_sb", [1, 2016], F32)
    vpib_sb = sbuf("vpib_sb", [1, HD], F32)
    mixes_sb = sbuf("mixes_sb", [1, 2], F32)
    cb_sb = sbuf("cb_sb", [128, 128], F32)
    b1c_sb = [sbuf(f"b1c_sb{i}", [128, 32], F32) for i in range(2)]
    b2c_sb = [sbuf(f"b2c_sb{i}", [128, DC], F32) for i in range(2)]
    ffn_tmp = sbuf("ffn_tmp", [128, 512], F32)

    state = {}

    for dst, src in [
        (ident_sb[:], ident), (ones_sb[:], ones), (bp_sb[:], bp_cols),
        (bqkvT_sb[:], bqkvT), (oobc_sb[:], oobc),
        (vpobc_sb[:], vpobc), (nwc_sb[:], nwc), (opb_sb[:], opb),
        (vpib_sb[:], vpib), (mixes_sb[:], mixes), (cb_sb[:], cbp),
        (lt_sb[:], lt128), (ecm1_sb[:], ecm1),
    ]:
        ev_consts = K.op("sync", lambda e, d=dst, s=src: e.dma_start(out=d, in_=s), dma=True)

    def phase_ln(l, xsrc, xs, xn_sb, want_hm, ln_sq, rowA, rowB):
        ev_x = K.op("sync", lambda e: e.dma_start(
            out=xs[:].rearrange("p (dc t) -> p dc t", dc=DC), in_=xsrc), dma=True)
        ev_wg = K.op("sync", lambda g: g.dma_start(
            out=wg_bf[:].rearrange("p (dc e) -> p dc e", dc=DC),
            in_=Wg_l[l].rearrange("(dc p) e -> p dc e", p=128)), dma=True)
        ev_bg = K.op("sync", lambda e: e.dma_start(out=bgrow[:], in_=bg_l[l]), dma=True)
        m = K.mm([0], lambda t: t.matmul(pb[0][:, 0:E], ones_sb[0:1, :], bgrow[:],
                                         start=True, stop=True),
                 extra=[ev_bg, ev_consts])
        ev_bgrep = K.rd("scalar", [0], lambda s: s.copy(bg_rep[:], pb[0][:, 0:E]))

        for tau in range(TQ):
            for dc in range(DC):
                par = dc % 2
                xt = xs[:, dc * T + tau * 512: (dc * T + tau * 512) + 512]
                sq_ev = K.op("scalar", lambda s, a=ln_sq[par], b=xt: s.activation(a[:], b, AF.Square),
                             waits=[ev_x, state.get(("lnsq", par))])
                K.mm([0], lambda t, b=xt, dc=dc: t.matmul(
                    pb[0][0:1, 0:512], ones_sb[:, 0:1], b, start=(dc == 0), stop=(dc == DC - 1)),
                    extra=[ev_x, ev_consts], first=(dc == 0), last=(dc == DC - 1))
                m2 = K.mm([1], lambda t, a=ln_sq[par], dc=dc: t.matmul(
                    pb[1][0:1, 0:512], ones_sb[:, 0:1], a[:], start=(dc == 0), stop=(dc == DC - 1)),
                    extra=[sq_ev], first=(dc == 0), last=True)
                state[("lnsq", par)] = m2
            mrow = rowA[0:1, 0:512]
            rrow = rowB[0:1, 0:512]
            e1 = K.rd("vector", [0], lambda vv, mr=mrow: vv.tensor_scalar(
                mr, pb[0][0:1, 0:512], 1.0 / D, None, op0=ALU.mult),
                extra=[state.get("ln_mrow_free")])
            e2 = K.rd("vector", [1], lambda vv: vv.tensor_scalar(
                rowC[:], pb[1][0:1, 0:512], 1.0 / D, LN_EPS, op0=ALU.mult, op1=ALU.add),
                extra=[e1])
            e3 = K.op("vector", lambda vv, mr=mrow, rr=rrow: vv.tensor_tensor(rr, mr, mr, op=ALU.mult),
                      waits=[e2, state.get("ln_rrow_free")])
            e4 = K.op("vector", lambda vv, rr=rrow: vv.tensor_tensor(rr, rowC[:], rr, op=ALU.subtract),
                      waits=[e3])
            e5 = K.op("scalar", lambda s, rr=rrow: s.activation(rr, rr, AF.Sqrt),
                      waits=[e4])
            e6 = K.op("vector", lambda vv, rr=rrow: vv.reciprocal(rr, rr), waits=[e5])
            mmm = K.mm([2], lambda t, mr=mrow: t.matmul(
                pb[2][:, 0:512], ones_sb[0:1, :], mr, start=True, stop=True), extra=[e4])
            state["ln_mrow_free"] = mmm
            mmr = K.mm([3], lambda t, rr=rrow: t.matmul(
                pb[3][:, 0:512], ones_sb[0:1, :], rr, start=True, stop=True), extra=[e6])
            for tc4 in range(4):
                mtr = K.mm([7], lambda t, tc4=tc4, rr=rrow: t.transpose(
                    pbig[1][:, 512:513], rr[0:1, ts(tc4, 128)], ident_sb[0:1, 0:1]),
                    extra=[e6])
                K.rd("scalar", [7], lambda s2, tc4=tc4: s2.copy(
                    rstdc_sb[:, tc4:tc4 + 1], pbig[1][:, 512:513]))
            state["ln_rrow_free"] = mtr
            for dc in range(DC):
                xt = xs[:, dc * T + tau * 512: (dc * T + tau * 512) + 512]
                xnt = xn_sb[:, dc * T + tau * 512: (dc * T + tau * 512) + 512]
                K.rd("vector", [2], lambda vv, a=xt: vv.tensor_tensor(a, a, pb[2][:, 0:512], op=ALU.subtract))
                ev_n = K.rd("vector", [3], lambda vv, a=xt, b=xnt: vv.tensor_tensor(
                    b, a, pb[3][:, 0:512], op=ALU.mult))
                K.op("sync", lambda e, b=xnt, dc=dc, tau=tau: e.dma_start(
                    out=xn_d[:, dc, ts(tau, 512)], in_=b), waits=[ev_n], dma=True)
            ev_norm = (v, v.n)
            for tc4 in range(4):
                g_ = tau * 4 + tc4
                for dc in range(DC):
                    K.mm([4], lambda t, dc=dc, g_=g_, tc4=tc4: t.matmul(
                        pbig[0][:, ts(tc4, 8)],
                        xs[:, dc * T + g_ * 128: dc * T + g_ * 128 + 128],
                        wg_bf[:, ts(dc, E)], start=(dc == 0), stop=(dc == DC - 1)),
                        extra=[ev_norm, ev_wg], first=(dc == 0), last=(dc == DC - 1))
                e1 = K.rd("vector", [4], lambda vv, tc4=tc4: vv.scalar_tensor_tensor(
                    ltm[:], pbig[0][:, ts(tc4, 8)], rstdc_sb[:, tc4:tc4 + 1], bg_rep[:],
                    op0=ALU.mult, op1=ALU.add),
                    extra=[ev_bgrep, state.get("ln_ltm_free"), (sc, sc.n)])
                e2 = K.op("vector", lambda vv: vv.reduce_max(mxcol[:], ltm[:], axis=AX.X), waits=[e1])
                K.wait_only("vector", [e2])
                e3 = K.op("vector", lambda vv: vv.tensor_scalar(
                    ind_tm[:], ltm[:], mxcol[:], None, op0=ALU.is_equal),
                    waits=[e2])
                K.op("vector", lambda vv, g_=g_: vv.tensor_copy(
                    ind_all[:, ts(g_, E)], ind_tm[:]))
                state["ln_ltm_free"] = e3
        ev_ind = (v, v.n)
        for tc in range(TC):
            K.mm([6], lambda t, tc=tc: t.matmul(
                pbig[1][0:1, ts(tc, E)], ones_sb[:, 0:1], ind_all[:, ts(tc, E)],
                start=True, stop=True),
                extra=[ev_ind] if tc == 0 else [], first=(tc == 0), last=(tc == TC - 1))
        K.rd("scalar", [6], lambda s: s.copy(cs_rows[:], pbig[1][0:1, 0:128]))
        K.op("vector", lambda vv: vv.memset(bases[0:1, 0:E], 0.0), waits=[(sc, sc.n)])
        for tc in range(1, TC):
            K.op("vector", lambda vv, tc=tc: vv.tensor_tensor(
                bases[0:1, ts(tc, E)], bases[0:1, ts(tc - 1, E)],
                cs_rows[0:1, ts(tc - 1, E)], op=ALU.add))
        ev_bases = (v, v.n)
        for tc in range(TC):
            cpar = tc % 2
            cum_sl = pbig[0][:, cpar * 512: cpar * 512 + E]
            cbank = [4 + cpar]
            K.mm(cbank, lambda t, tc=tc, cum_sl=cum_sl: t.matmul(
                cum_sl, lt_sb[:], ind_all[:, ts(tc, E)], start=True, stop=False),
                extra=[ev_bases, ev_consts], first=True, last=False)
            K.mm(cbank, lambda t, tc=tc, cum_sl=cum_sl: t.matmul(
                cum_sl, ones_sb[0:1, :], bases[0:1, ts(tc, E)], start=False, stop=True),
                first=False, last=True)
            K.rd("vector", cbank, lambda vv, cum_sl=cum_sl: vv.tensor_tensor(
                tmp8[:], cum_sl, ecm1_sb[:], op=ALU.add))
            K.op("vector", lambda vv, tc=tc: vv.tensor_tensor(
                tmp8[:], tmp8[:], ind_all[:, ts(tc, E)], op=ALU.mult))
            K.op("vector", lambda vv: vv.tensor_reduce(tcol[:], tmp8[:], AX.X, ALU.add))
            K.op("vector", lambda vv: vv.tensor_scalar(
                tcol[:], tcol[:], float(E * CAP - 1), None, op0=ALU.min))
            K.op("vector", lambda vv, tc=tc: vv.tensor_copy(tsl_sb[:, tc:tc + 1], tcol[:]))
        state["tsl"] = (v, v.n)
        if want_hm:
            prev = None
            for dc in range(DC):
                prev = K.op("vector", lambda vv, dc=dc: vv.tensor_reduce(
                    hm_cols[:, dc:dc + 1], xn_sb[:, ts(dc, T)], AX.X, ALU.add), waits=[(v, v.n)])
            K.wait_only("vector", [prev])
            state["hm"] = K.op("vector", lambda vv: vv.tensor_scalar(
                hm_bf[:], hm_cols[:], 1.0 / T, None, op0=ALU.mult))

    def phase_ffn(l, xnh, hact, w2buf, w1buf, maskrep, gelu):
        ev_memset = K.op("vector", lambda vv: vv.memset(sg[:], 0.0))
        a_evac = {}
        w1_free = [None] * 6
        w2_free = [None] * 6
        mask_free = None
        hps_free = [None, None]
        b1_free = [None, None]
        b2_free = [None, None]
        ev_b1 = [None, None]
        ev_b2 = [None, None]
        xnh_free = None
        ybanks_sl = [pb[0][:, 0:512], pb[1][:, 0:512], pb[2][:, 0:512], pb[3][:, 0:512],
                     pbig[0][:, 0:512], pbig[0][:, 512:1024],
                     pbig[1][:, 0:512], pbig[1][:, 512:1024]]
        ybank_ids = [[0], [1], [2], [3], [4], [5], [6], [7]]

        for th in range(TH):
            ev_xnh = K.op("sync", lambda e, th=th: e.dma_start(
                out=xnh[:].rearrange("p (dc t) -> p dc t", dc=DC),
                in_=xn_d[:, :, ts(th, 1024)]), waits=[xnh_free], dma=True)
            for e_ in range(E):
                bpar = e_ % 2
                if l == 1:
                    K.op("sync", lambda e, e_=e_, bp=bpar: e.dma_start(
                        out=b1c_sb[bp][:], in_=b1c[e_]), waits=[b1_free[bpar], b2_free[bpar]],
                        dma=True, dma_sem=f"d_b{bpar}")
                    mboth = K.op("sync", lambda e, e_=e_, bp=bpar: e.dma_start(
                        out=b2c_sb[bp][:], in_=b2c[e_]), waits=[],
                        dma=True, dma_sem=f"d_b{bpar}")
                    ev_b1[bpar] = ev_b2[bpar] = mboth
                ev_mrow = K.op("sync", lambda e, e_=e_, th=th: e.dma_start(
                    out=mrow_sb[:], in_=ind_d[e_:e_ + 1, ts(th, 1024)]),
                    waits=[state[("masks", l)], state.get("ffn_mrow_free")], dma=True)
                for seg in range(16):
                    par = seg % 4
                    ev_w1 = K.op("gpsimd", lambda g, e_=e_, seg=seg, par=par: g.dma_start(
                        out=w1buf[par][:].rearrange("p (dc h) -> p dc h", dc=DC),
                        in_=W1_l[l][e_, :, ts(seg, 256)].rearrange("(dc p) h -> p dc h", p=128)),
                        waits=[w1_free[par]], dma=True, dma_sem=f"d_w1{par}")
                    for hc2 in range(2):
                        hc = seg * 2 + hc2
                        for sub in range(2):
                            hb = sub
                            for dc in range(DC):
                                mmk = K.mm([hb], lambda t, par=par, dc=dc, hc2=hc2, sub=sub, hb=hb: t.matmul(
                                    pb[hb][:, 0:512],
                                    w1buf[par][:, dc * 256 + hc2 * 128: dc * 256 + hc2 * 128 + 128],
                                    xnh[:, dc * 1024 + sub * 512: dc * 1024 + sub * 512 + 512],
                                    start=(dc == 0), stop=(dc == DC - 1)),
                                    extra=([ev_w1, ev_xnh] if dc == 0 else []),
                                    first=(dc == 0), last=(dc == DC - 1))
                            if hc2 == 1 and sub == 1:
                                w1_free[par] = mmk
                            hsl = hact[:, hc * 1024 + sub * 512: hc * 1024 + sub * 512 + 512]
                            if gelu:
                                ev = K.rd("scalar", [hb], lambda s, h_=hsl, hb=hb: s.activation(
                                    h_, pb[hb][:, 0:512], AF.Gelu_apprx_tanh))
                            else:
                                ev = K.rd("scalar", [hb], lambda s, h_=hsl, hb=hb, hc=hc, bp=bpar: s.activation(
                                    h_, pb[hb][:, 0:512], AF.Relu, bias=b1c_sb[bp][:, hc:hc + 1]),
                                    extra=[ev_b1[bpar]])
                            a_evac[(hc, sub)] = ev
                xnh_free = (pe, pe.n)
                if l == 1:
                    b1_free[bpar] = (sc, sc.n)
                for q_ in range(2):
                    bq = 2 + q_
                    K.mm([bq], lambda t, q_=q_, bq=bq: t.matmul(
                        pb[bq][:, 0:512], ones_sb[0:1, :], mrow_sb[0:1, ts(q_, 512)],
                        start=True, stop=True), extra=[ev_mrow])
                state["ffn_mrow_free"] = (pe, pe.n)
                K.rd("scalar", [2], lambda s: s.copy(maskrep[:, 0:512], pb[2][:, 0:512]),
                     extra=[mask_free])
                ev_mask = K.rd("scalar", [3], lambda s: s.copy(maskrep[:, 512:1024],
                                                               pb[3][:, 0:512]),
                               extra=[mask_free])
                for dh in range(2):
                    for hc in range(HC):
                        w2p = hc % 6
                        ev_w2 = K.op("gpsimd", lambda g, e_=e_, hc=hc, dh=dh, w2p=w2p: g.dma_start(
                            out=w2buf[w2p][:], in_=W2_l[l][e_, ts(hc, 128), dh * 512:dh * 512 + 512]),
                            waits=[w2_free[w2p]], dma=True, dma_sem=f"d_w2{w2p}")
                        for sub in range(2):
                            for d4 in range(4):
                                yb = sub * 4 + d4
                                mmk = K.mm(ybank_ids[yb], lambda t, hc=hc, sub=sub, d4=d4, w2p=w2p, yb=yb: t.matmul(
                                    ybanks_sl[yb],
                                    w2buf[w2p][:, ts(d4, 128)],
                                    hact[:, hc * 1024 + sub * 512: hc * 1024 + sub * 512 + 512],
                                    start=(hc == 0), stop=(hc == HC - 1)),
                                    extra=([ev_w2, a_evac[(hc, sub)]] if d4 == 0 else []),
                                    first=(hc == 0), last=(hc == HC - 1),
                                    force_inc=(sub == 1 and d4 == 3))
                        w2_free[w2p] = mmk
                    for sub in range(2):
                        for d4 in range(4):
                            yb = sub * 4 + d4
                            dout = dh * 4 + d4
                            sgs = sg[:, dout * T + th * 1024 + sub * 512:
                                     dout * T + th * 1024 + sub * 512 + 512]
                            msl = maskrep[:, ts(sub, 512)]
                            if l == 0:
                                K.rd("vector", ybank_ids[yb], lambda vv, yb=yb, msl=msl: vv.scalar_tensor_tensor(
                                    ffn_tmp[:], ybanks_sl[yb], 0.0, msl, op0=ALU.add, op1=ALU.mult),
                                    extra=[ev_mask, ev_memset])
                            else:
                                K.rd("vector", ybank_ids[yb], lambda vv, yb=yb, msl=msl, dout=dout, bp=bpar: vv.scalar_tensor_tensor(
                                    ffn_tmp[:], ybanks_sl[yb], b2c_sb[bp][:, dout:dout + 1], msl,
                                    op0=ALU.add, op1=ALU.mult),
                                    extra=[ev_mask, ev_memset, ev_b2[bpar]])
                            K.op("vector", lambda vv, sgs=sgs: vv.tensor_tensor(
                                sgs, sgs, ffn_tmp[:], op=ALU.add))
                mask_free = (v, v.n)
                if l == 1:
                    b2_free[bpar] = (v, v.n)
        state[("ffn_v", l)] = (v, v.n)

    def phase_ffn_sparse(l, xgfm, hact_sp, w1buf, w2buf, ytm_sb, xntm, b2row, gelu,
                         precast=False):
        W1_ap = W1_l[l]
        W2_ap = W2_l[l]
        xntm_free = [None, None]
        ev_scat = None
        for tc in range(TC):
            par = tc % 2
            tr = None
            for dc in range(DC):
                tr = K.op("sync", lambda e, dc=dc, tc=tc, par=par: e.dma_start_transpose(
                    out=xntm[par][:, ts(dc, 128)], in_=xn_d[:, dc, ts(tc, 128)]),
                    waits=[xntm_free[par]], dma=True, dma_sem=f"d_xt{par}")
            ev_scat_p = K.op("gpsimd", lambda g, tc=tc, par=par: g.indirect_dma_start(
                out=xg_d, out_offset=IndirectOffsetOnAxis(tsl_sb[:, tc:tc + 1], 0),
                in_=xntm[par][:], in_offset=None),
                waits=[tr, state["tsl"]], dma=True, dma_sem=f"d_scat{par}")
            xntm_free[par] = ev_scat_p
            ev_scat = None
        a_evac = {}
        w1_free = [None] * 6
        w2_free = [None] * 6
        xg_free = [None, None]
        ytm_free = [None, None]
        b1_free = [None, None]
        b2_free = [None, None]
        ysl = [(p4[:, 0:512], [2]), (p4[:, 512:1024], [3]),
               (p4[:, 1024:1536], [4]), (p4[:, 1536:2048], [5]),
               (pbig[1][:, 0:512], [6]), (pbig[1][:, 512:1024], [7])]
        NSC = CAP // 128
        last_store = None
        xg_marks = [None] * E
        b1_marks = [None] * E
        b2_marks = [None] * E

        def emit_loads(e_):
            epar = e_ % 2
            m = None
            for dc in range(DC):
                m = K.op("sync", lambda e2, e_=e_, dc=dc, epar=epar: e2.dma_start_transpose(
                    out=xgfm[epar][:, ts(dc, CAP)], in_=xg_d[e_ * CAP:(e_ + 1) * CAP, ts(dc, 128)]),
                    waits=[xntm_free[0], xntm_free[1], xg_free[epar]],
                    dma=True, dma_sem=f"d_xg{epar}")
            xg_marks[e_] = m
            if l == 1:
                b1_marks[e_] = K.op("sync", lambda e, e_=e_, bp=epar: e.dma_start(
                    out=b1c_sb[bp][:], in_=b1c[e_]), waits=[b1_free[epar]],
                    dma=True, dma_sem=f"d_b{epar}")
                b2_marks[e_] = K.op("sync", lambda e, e_=e_, bp=epar: e.dma_start(
                    out=b2row[bp][:], in_=b2r[e_]), waits=[b2_free[epar]],
                    dma=True, dma_sem=f"d_b2{epar}")

        emit_loads(0)
        for e_ in range(E):
            epar = e_ % 2
            xg_m = xg_marks[e_]
            ev_b1 = b1_marks[e_]
            ev_b2 = b2_marks[e_]
            for seg in range(16):
                par4 = seg % 6
                w1src = (w1bf_d if precast else W1_ap)
                ev_w1 = K.op("gpsimd", lambda g, e_=e_, seg=seg, par4=par4, w1src=w1src: g.dma_start(
                    out=w1buf[par4][:].rearrange("p (dc h) -> p dc h", dc=DC),
                    in_=w1src[e_, :, ts(seg, 256)].rearrange("(dc p) h -> p dc h", p=128)),
                    waits=[w1_free[par4]] + ([state.get("precast_done"), state.get("precast_done2"), state.get("precast_done3"), state.get("precast_done4")] if precast else []),
                    dma=True, dma_sem=f"d_w1{par4}")
                for hc2 in range(2):
                    hc = seg * 2 + hc2
                    hb = hc % 2
                    for dc in range(DC):
                        mmk = K.mm([hb], lambda t, par4=par4, dc=dc, hc2=hc2, hb=hb, epar=epar: t.matmul(
                            pb[hb][:, 0:CAP],
                            w1buf[par4][:, dc * 256 + hc2 * 128: dc * 256 + hc2 * 128 + 128],
                            xgfm[epar][:, dc * CAP: dc * CAP + CAP],
                            start=(dc == 0), stop=(dc == DC - 1)),
                            extra=([ev_w1, xg_m] if dc == 0 else []),
                            first=(dc == 0), last=(dc == DC - 1))
                    if hc2 == 1:
                        w1_free[par4] = mmk
                    hsl = hact_sp[:, hc * CAP: hc * CAP + CAP]
                    if gelu:
                        a_evac[hc] = K.rd("scalar", [hb], lambda s, h_=hsl, hb=hb: s.activation(
                            h_, pb[hb][:, 0:CAP], AF.Gelu_apprx_tanh))
                    else:
                        a_evac[hc] = K.rd("scalar", [hb], lambda s, h_=hsl, hb=hb, hc=hc, bp=epar: s.activation(
                            h_, pb[hb][:, 0:CAP], AF.Relu, bias=b1c_sb[bp][:, hc:hc + 1]),
                            extra=[ev_b1])
            xg_free[epar] = (pe, pe.n)
            if l == 1:
                b1_free[epar] = (sc, sc.n)
            if e_ + 1 < E:
                emit_loads(e_ + 1)
            for hc in range(HC):
                w2p = hc % 6
                w2src = (w2bf_d if precast else W2_ap)
                ev_w2 = K.op("gpsimd", lambda g, e_=e_, hc=hc, w2p=w2p, w2src=w2src: g.dma_start(
                    out=w2buf[w2p][:], in_=w2src[e_, ts(hc, 128), :]),
                    waits=[w2_free[w2p]] + ([state.get("precast_done"), state.get("precast_done2"), state.get("precast_done3"), state.get("precast_done4")] if precast else []),
                    dma=True, dma_sem=f"d_w2{w2p}")
                for sc4 in range(NSC):
                    for half in range(2):
                        yi = sc4 * 2 + half
                        mmk = K.mm(ysl[yi][1], lambda t, hc=hc, sc4=sc4, half=half, w2p=w2p, yi=yi: t.matmul(
                            ysl[yi][0],
                            hact_sp[:, hc * CAP + sc4 * 128: hc * CAP + sc4 * 128 + 128],
                            w2buf[w2p][:, half * 512: half * 512 + 512],
                            start=(hc == 0), stop=(hc == HC - 1 and l == 0)),
                            extra=([ev_w2, a_evac[hc]] if (sc4 == 0 and half == 0) else [ev_w2] if half == 0 else []),
                            first=(hc == 0), last=(hc == HC - 1 and l == 0),
                            force_inc=(sc4 == NSC - 1 and half == 1))
                w2_free[w2p] = mmk
            if l == 1:
                for sc4 in range(NSC):
                    for half in range(2):
                        yi = sc4 * 2 + half
                        K.mm(ysl[yi][1], lambda t, sc4=sc4, half=half, yi=yi, bp=epar: t.matmul(
                            ysl[yi][0], ones_sb[0:1, :], b2row[bp][0:1, half * 512: half * 512 + 512],
                            start=False, stop=True),
                            extra=[ev_b2] if yi == 0 else [], first=False, last=True)
            for sc4 in range(NSC):
                for half in range(2):
                    yi = sc4 * 2 + half
                    K.rd("vector", ysl[yi][1], lambda vv, sc4=sc4, half=half, yi=yi, epar=epar: vv.tensor_copy(
                        ytm_sb[epar][:, sc4 * 1024 + half * 512: sc4 * 1024 + half * 512 + 512],
                        ysl[yi][0]), extra=[ytm_free[epar]])
            ev_yv = (v, v.n)
            last_store = K.op("sync", lambda e2, e_=e_, epar=epar: e2.dma_start(
                out=ytm_d[e_ * CAP:(e_ + 1) * CAP, :].rearrange("(sc p) d -> p sc d", p=128),
                in_=ytm_sb[epar][:].rearrange("p (sc d) -> p sc d", sc=NSC)),
                waits=[ev_yv], dma=True, dma_sem=f"d_yt{epar}")
            ytm_free[epar] = last_store
            if l == 1:
                b2_free[epar] = (v, v.n)
        ev_gat = None
        gst_free = [None, None]
        for tc in range(TC):
            gpar = tc % 2
            gm = K.op("gpsimd", lambda g, tc=tc, gpar=gpar: g.indirect_dma_start(
                out=xntm[gpar][:], out_offset=None,
                in_=ytm_d, in_offset=IndirectOffsetOnAxis(tsl_sb[:, tc:tc + 1], 0)),
                waits=[ytm_free[0], ytm_free[1], gst_free[gpar]],
                dma=True, dma_sem=f"d_gat{gpar}")
            ev_gat = K.op("sync", lambda e2, tc=tc, gpar=gpar: e2.dma_start(
                out=gtm_d[tc * 128:(tc + 1) * 128, :], in_=xntm[gpar][:]),
                waits=[gm], dma=True, dma_sem=f"d_gst{gpar}")
            gst_free[gpar] = ev_gat
        state["gst0"], state["gst1"] = gst_free[0], gst_free[1]
        ev_sg = None
        for dc in range(DC):
            ev_sg = K.op("sync", lambda e2, dc=dc: e2.dma_start_transpose(
                out=sg[:, dc * T:(dc + 1) * T], in_=gtm_d[:, ts(dc, 128)]),
                waits=[ev_gat, state.get("gst0"), state.get("gst1")], dma=True, dma_sem="d_sg")
        state[("ffn_v", l)] = ev_sg

    def phase_attn(xn_sb, av_sb, wpT_buf, attnT2, Qb2, Kb2, Vb2, qkv_bf, rrep2,
                   nmcols, rrow, x0t):
        ev_xn = K.op("sync", lambda e: e.dma_start(
            out=xn_sb[:].rearrange("p (dc t) -> p dc t", dc=DC), in_=xn_d[:, :, :]), dma=True)
        ev_g = None
        for i in range(DC * 4):
            ev_g = K.op("scalar", lambda s, i=i: s.activation(
                sg[:, ts(i, 512)], sg[:, ts(i, 512)], AF.Silu),
                waits=[state[("ffn_v", 0)]])
        for hd in range(2):
            for tc in range(TC):
                K.op("vector", lambda vv, hd=hd, tc=tc: vv.memset(
                    Vb2[hd][:, tc * 65 + 64: tc * 65 + 65], 1.0))
        ev_vones = (v, v.n)
        pc_free = [None, None, None]
        for e_ in range(E):
            for ch in range(16):
                par = ch % 3
                src = W1_l[1][e_, ts(ch // 2, 128), (ch % 2) * 2048:(ch % 2) * 2048 + 2048]
                dst = w1bf_d[e_, ts(ch // 2, 128), (ch % 2) * 2048:(ch % 2) * 2048 + 2048]
                ld = K.op("gpsimd", lambda g, src=src, par=par: g.dma_start(
                    out=pc_stage[par][:], in_=src),
                    waits=[pc_free[par]], dma=True, dma_sem=f"d_pc{par}")
                pc_free[par] = K.op("sync", lambda e2, dst=dst, par=par: e2.dma_start(
                    out=dst, in_=pc_stage[par][:]),
                    waits=[ld], dma=True, dma_sem=f"d_pcs{par}")
            for ch in range(16):
                par = ch % 3
                src = W2_l[1][e_, ts(ch, 256), :].rearrange("(a p) d -> p a d", p=128)
                dst = w2bf_d[e_, ts(ch, 256), :].rearrange("(a p) d -> p a d", p=128)
                ld = K.op("gpsimd", lambda g, src=src, par=par: g.dma_start(
                    out=pc_stage[par][:].rearrange("p (a d) -> p a d", a=2), in_=src),
                    waits=[pc_free[par]], dma=True, dma_sem=f"d_pc{par}")
                pc_free[par] = K.op("sync", lambda e2, dst=dst, par=par: e2.dma_start(
                    out=dst, in_=pc_stage[par][:].rearrange("p (a d) -> p a d", a=2)),
                    waits=[ld], dma=True, dma_sem=f"d_pcs{par}")
        state["precast_done"] = pc_free[0]
        state["precast_done2"] = pc_free[1]
        state["precast_done3"] = pc_free[2]
        state["precast_done4"] = None

        qkv_w = [None, None]
        qkv_wfree = [None, None]
        kones_free = [None, None]
        x0_free = [None, None]
        prev_av_tt = [None]
        tr_mark = [None, None]
        for hp in range(NH // 2):
            heads = (2 * hp, 2 * hp + 1)
            ev_qk = [None, None]
            ev_kones = [None, None]
            ev_v = [None, None]
            row64 = [[None] * TQ, [None] * TQ]
            for hd in range(2):
                h = heads[hd]
                for kk in range(3):
                    qkv_w[hd] = K.op("gpsimd", lambda g, h=h, hd=hd, kk=kk: g.dma_start(
                        out=qkv_bf[hd][:, ts(kk, 512)].rearrange("p (dc hd2) -> p dc hd2", dc=DC),
                        in_=Wqkv[kk * 16 + h].rearrange("(dc p) hd2 -> p dc hd2", p=128)),
                        waits=[qkv_wfree[hd]], dma=True, dma_sem=f"d_qkv{hd}")
            for hd in range(2):
                h = heads[hd]
                Qb, Kb, Vb = Qb2[hd], Kb2[hd], Vb2[hd]
                for which, buf, bi in (("q", Qb, h), ("k", Kb, 16 + h)):
                    koff = 0 if which == "q" else 1
                    for tau in range(TQ):
                        qpar = tau % 2
                        for dc in range(DC):
                            K.mm([qpar], lambda t, dc=dc, tau=tau, koff=koff, qpar=qpar, hd=hd: t.matmul(
                                pb[qpar][0:HD, 0:512],
                                qkv_bf[hd][:, koff * 512 + dc * HD: koff * 512 + dc * HD + HD],
                                xn_sb[:, dc * T + tau * 512: dc * T + tau * 512 + 512],
                                start=(dc == 0), stop=(dc == DC - 1)),
                                extra=([qkv_w[hd], ev_xn] if dc == 0 else []),
                                first=(dc == 0), last=(dc == DC - 1))
                        K.rd("vector", [qpar], lambda vv, buf=buf, tau=tau, bi=bi, qpar=qpar: vv.tensor_scalar(
                            buf[0:HD, ts(tau, 512)], pb[qpar][0:HD, 0:512],
                            bqkvT_sb[:, bi:bi + 1], None, op0=ALU.add),
                            extra=[ev_consts] + ([kones_free[hd]] if which == "k" else []))
                ev_qk[hd] = (v, v.n)
                ev_kones[hd] = K.op("vector", lambda vv, Kb=Kb: vv.memset(Kb[64:65, :], 1.0),
                                    waits=[ev_qk[hd]])
                for tc in range(TC):
                    vpar = tc % 2
                    for dc in range(DC):
                        K.mm([vpar], lambda t, dc=dc, tc=tc, vpar=vpar, hd=hd: t.matmul(
                            pb[vpar][:, 0:HD],
                            xn_sb[:, dc * T + tc * 128: dc * T + tc * 128 + 128],
                            qkv_bf[hd][:, 2 * 512 + dc * HD: 2 * 512 + dc * HD + HD],
                            start=(dc == 0), stop=(dc == DC - 1)),
                            extra=([qkv_w[hd], ev_xn] if dc == 0 else []),
                            first=(dc == 0), last=(dc == DC - 1))
                    K.rd("vector", [vpar], lambda vv, tc=tc, vpar=vpar, Vb=Vb: vv.tensor_copy(
                        Vb[:, tc * 65: tc * 65 + 64], pb[vpar][:, 0:HD]), extra=[ev_vones])
                ev_v[hd] = (v, v.n)
                qkv_wfree[hd] = (pe, pe.n)

            def emit_p1_scores(hd, qg, sp):
                Qb, Kb = Qb2[hd], Kb2[hd]
                for qt4 in range(4):
                    qt = qg * 4 + qt4
                    for half, bks in ((0, [2, 3]), (1, [4, 5])):
                        for kg in range(2):
                            K.mm(bks, lambda t, qt=qt, half=half, kg=kg, Qb=Qb, Kb=Kb: t.matmul(
                                p4[:, ts(half * 2 + kg, 512)], Qb[0:HD, ts(qt, 128)],
                                Kb[0:HD, ts(half * 2 + kg, 512)], start=True, stop=True),
                                extra=([ev_qk[hd]] if kg == 0 else []),
                                first=(kg == 0), last=(kg == 1))
                        K.rd("vector", bks, lambda vv, half=half: vv.reduce_max(
                            nmcols[:, half:half + 1], p4[:, half * 1024:half * 1024 + 1024],
                            axis=AX.X, negate=True))
                    col = 2 + (sp % 2) * 4 + qt4
                    K.op("vector", lambda vv, col=col: vv.tensor_reduce(
                        nmcols[:, col:col + 1], nmcols[:, 0:2], AX.X, ALU.min),
                        waits=[tr_mark[sp % 2]])

            def emit_p1_finish(hd, qg, sp):
                Qb = Qb2[hd]
                for qt4 in range(4):
                    qt = qg * 4 + qt4
                    col = 2 + (sp % 2) * 4 + qt4
                    mt = K.mm([7], lambda t, col=col: t.transpose(
                        pbig[1][0:1, 512:640], nmcols[:, col:col + 1], ident_sb[:]),
                        extra=[(v, v.n)])
                    row64[hd][qg] = K.rd("scalar", [7], lambda s, qt=qt, Qb=Qb: s.copy(
                        Qb[64:65, ts(qt, 128)], pbig[1][0:1, 512:640]), extra=[ev_kones[hd]])
                tr_mark[sp % 2] = (pe, pe.n)

            emit_p1_scores(0, 0, 0)
            emit_p1_finish(0, 0, 0)
            for step in range(2 * TQ):
                hd, qg = step % 2, step // 2
                h = heads[hd]
                Qb, Kb, Vb = Qb2[hd], Kb2[hd], Vb2[hd]
                attnT = attnT2[hd]
                rrep_sb = rrep2[hd]
                if step + 1 < 2 * TQ:
                    emit_p1_scores((step + 1) % 2, (step + 1) // 2, step + 1)
                exp_marks = []
                for kc in range(TC):
                    spar = kc % 2
                    K.mm([spar], lambda t, kc=kc, qg=qg, spar=spar, Qb=Qb, Kb=Kb: t.matmul(
                        pb[spar][:, 0:512], Kb[0:65, ts(kc, 128)], Qb[0:65, ts(qg, 512)],
                        start=True, stop=True),
                        extra=([row64[hd][qg], ev_kones[hd]] if kc == 0 else []))
                    exp_marks.append(K.rd("scalar", [spar], lambda s, kc=kc, spar=spar, attnT=attnT: s.activation(
                        attnT[:, ts(kc, 512)], pb[spar][:, 0:512], AF.Exp, scale=8.0)))
                for kc in range(TC):
                    K.mm([6], lambda t, kc=kc, attnT=attnT, Vb=Vb: t.matmul(
                        pbig[1][0:65, 0:512], Vb[:, kc * 65: kc * 65 + 65], attnT[:, ts(kc, 512)],
                        start=(kc == 0), stop=(kc == TC - 1)),
                        extra=[exp_marks[kc]] + ([ev_v[hd]] if kc == 0 else []),
                        first=(kc == 0), last=(kc == TC - 1))
                e1 = K.rd("vector", [6], lambda vv: vv.reciprocal(rrow[:], pbig[1][64:65, 0:512]))
                mmr = K.mm([7], lambda t: t.matmul(pbig[1][0:64, 512:1024], ones_sb[0:1, 0:HD],
                                                   rrow[:], start=True, stop=True), extra=[e1])
                ev_rr = K.rd("vector", [7], lambda vv, rrep_sb=rrep_sb: vv.tensor_copy(
                    rrep_sb[:], pbig[1][0:64, 512:1024]), extra=[prev_av_tt[0]])
                cc, ph = h // 2, h % 2
                bv_col = bqkvT_sb[:, 32 + h: 33 + h]
                K.rd("vector", [6], lambda vv, rrep_sb=rrep_sb: vv.tensor_tensor(
                    ffn_tmp[0:HD, :], pbig[1][0:HD, 0:512], rrep_sb[:], op=ALU.mult),
                    extra=[ev_rr])
                prev_av_tt[0] = K.op("vector", lambda vv, cc=cc, ph=ph, qg=qg, bv=bv_col: vv.tensor_scalar(
                    av_sb[ph * HD: ph * HD + HD, cc * T + qg * 512: cc * T + qg * 512 + 512],
                    ffn_tmp[0:HD, :], bv, None, op0=ALU.add))
                if step + 1 < 2 * TQ:
                    emit_p1_finish((step + 1) % 2, (step + 1) // 2, step + 1)
            kones_free[0] = (pe, pe.n)
            kones_free[1] = (pe, pe.n)
        ev_av = (v, v.n)
        x0r = x0.rearrange("(dc p) t -> p dc t", p=128)
        wp_free = [None, None]
        for pc in range(DC):
            wpar = pc % 2
            ev_wp = K.op("gpsimd", lambda g, pc=pc, wpar=wpar: g.dma_start(
                out=wpT_buf[wpar][:].rearrange("p (cc q) -> p cc q", cc=DC),
                in_=WpT[:, ts(pc, 128)].rearrange("(cc p) q -> p cc q", p=128)),
                waits=[wp_free[wpar]], dma=True, dma_sem=f"d_wp{wpar}")
            for tau in range(TQ):
                spar = (pc * TQ + tau) % 2
                ev_x0 = K.op("sync", lambda e, pc=pc, tau=tau, spar=spar: e.dma_start(
                    out=x0t[spar][:], in_=x0r[:, pc, ts(tau, 512)]),
                    waits=[x0_free[spar]], dma=True, dma_sem=f"d_x{spar}")
                for cc in range(DC):
                    K.mm([spar], lambda t, cc=cc, tau=tau, spar=spar, wpar=wpar: t.matmul(
                        pb[spar][:, 0:512],
                        wpT_buf[wpar][:, ts(cc, 128)],
                        av_sb[:, cc * T + tau * 512: cc * T + tau * 512 + 512],
                        start=(cc == 0), stop=(cc == DC - 1)),
                        extra=([ev_av, ev_wp] if cc == 0 else []),
                        first=(cc == 0), last=(cc == DC - 1))
                K.rd("vector", [spar], lambda vv, pc=pc, tau=tau, spar=spar: vv.scalar_tensor_tensor(
                    ffn_tmp[:], pb[spar][:, 0:512], bp_sb[:, pc:pc + 1],
                    sg[:, pc * T + tau * 512: pc * T + tau * 512 + 512],
                    op0=ALU.add, op1=ALU.mult), extra=[ev_g])
                ev_xt = K.op("vector", lambda vv, spar=spar: vv.tensor_tensor(
                    x0t[spar][:], x0t[spar][:], ffn_tmp[:], op=ALU.add), waits=[ev_x0])
                x0_free[spar] = K.op("sync", lambda e, pc=pc, tau=tau, spar=spar: e.dma_start(
                    out=x_d[:, pc, ts(tau, 512)], in_=x0t[spar][:]), waits=[ev_xt],
                    dma=True, dma_sem=f"d_xs{spar}")
            wp_free[wpar] = (pe, pe.n)

    def phase_l1small(xn_sb, pre, xr_bf, oow_bf, vpow_bf, opw_buf, vpiw_bf,
                      sq_t, gsil_t, s1_t, x1t, vqs):
        (params_sb, opb_sb, om_u, om_ut, om, om_n, om2, om3, Gt, G_bf, vq_row, vq_sq,
         vq_n, cb_sq, cb_n, cb_sc, vqn_rep, vq_gc, vq_ind, code_row, code_col,
         omix_col, vqmix_col, vqc_cols) = vqs
        ev_xn = K.op("sync", lambda e: e.dma_start(
            out=xn_sb[:].rearrange("p (dc t) -> p dc t", dc=DC), in_=xn_d[:, :, :]), dma=True)
        ev_oow = K.op("gpsimd", lambda g: g.dma_start(out=oow_bf[:], in_=oowT), dma=True)
        ev_vpow = K.op("gpsimd", lambda g: g.dma_start(out=vpow_bf[:], in_=vpowT), dma=True)
        ev_vpiw = K.op("sync", lambda e: e.dma_start(
            out=vpiw_bf[:].rearrange("p (dc c) -> p dc c", dc=DC),
            in_=vpiwT.rearrange("(dc p) c -> p dc c", p=128)), dma=True)
        ev_hm = state["hm"]
        ev_opb = K.op("sync", lambda e: e.dma_start(out=opb_sb[:], in_=opb), dma=True)
        opw_free = [None, None, None]
        for dcb in range(DC):
            par = dcb % 3
            ev_opw = K.op("gpsimd", lambda g, dcb=dcb, par=par: g.dma_start(
                out=opw_buf[par][:], in_=opwT[ts(dcb, 128), :]), waits=[opw_free[par]],
                dma=True, dma_sem=f"d_opw{par}")
            for seg_ in range(4):
                mk = K.mm([seg_], lambda t, dcb=dcb, seg_=seg_, par=par: t.matmul(
                    pb[seg_][0:1, 0:504], hm_bf[:, dcb:dcb + 1], opw_buf[par][:, ts(seg_, 504)],
                    start=(dcb == 0), stop=(dcb == DC - 1)),
                    extra=[ev_opw, ev_hm], first=(dcb == 0), last=True)
            opw_free[par] = mk
        for seg_ in range(4):
            K.rd("scalar", [seg_], lambda s, seg_=seg_: s.copy(
                params_sb[0:1, ts(seg_, 504)], pb[seg_][0:1, 0:504]))
        ev_p = K.op("vector", lambda vv: vv.tensor_tensor(
            params_sb[:], params_sb[:], opb_sb[:], op=ALU.add),
            waits=[(sc, sc.n), ev_opb])
        ev_mz = K.op("vector", lambda vv: vv.memset(om_u[:], 0.0))
        off = 0
        ev_tri = None
        for r in range(63):
            ln_ = 63 - r
            ev_tri = K.op("sync", lambda e, r=r, off=off, ln_=ln_: e.dma_start(
                out=om_u[r:r + 1, r + 1:64], in_=params_sb[0:1, off:off + ln_]),
                waits=[ev_p, ev_mz], dma=True)
            off += ln_
        mt = K.mm([4], lambda t: t.transpose(pbig[0][0:64, 0:64], om_u[:], ident_sb[0:64, 0:64]),
                  extra=[ev_tri])
        ev_ut = K.rd("scalar", [4], lambda s: s.copy(om_ut[:], pbig[0][0:64, 0:64]))
        e_om = K.op("vector", lambda vv: vv.tensor_tensor(om[:], om_u[:], om_ut[:], op=ALU.subtract),
                    waits=[ev_ut, ev_tri])
        e_nom = K.op("vector", lambda vv: vv.tensor_tensor(om_n[:], om_ut[:], om_u[:], op=ALU.subtract))
        m2 = K.mm([4], lambda t: t.matmul(pbig[0][0:64, 64:128], om_n[:], om[:], start=True, stop=True),
                  extra=[e_nom])
        ev_o2 = K.rd("scalar", [4], lambda s: s.copy(om2[:], pbig[0][0:64, 64:128]))
        m3 = K.mm([4], lambda t: t.matmul(pbig[0][0:64, 128:192], om2[:], om[:], start=True, stop=True),
                  extra=[ev_o2])
        ev_o3 = K.rd("scalar", [4], lambda s: s.copy(om3[:], pbig[0][0:64, 128:192]))
        K.op("vector", lambda vv: vv.tensor_tensor(Gt[:], om[:], ident_sb[0:64, 0:64], op=ALU.add),
             waits=[ev_o3])
        K.op("vector", lambda vv: vv.scalar_tensor_tensor(Gt[:], om2[:], 0.5, Gt[:],
                                                          op0=ALU.mult, op1=ALU.add))
        K.op("vector", lambda vv: vv.scalar_tensor_tensor(Gt[:], om3[:], 1.0 / 6.0, Gt[:],
                                                          op0=ALU.mult, op1=ALU.add))
        ev_G = K.op("vector", lambda vv: vv.tensor_copy(G_bf[:], Gt[:]))
        ev_xr = None
        for tau in range(TQ):
            mm_ = K.mm([6, 7], lambda t, tau=tau: t.matmul(
                pbig[1][0:64, 0:512], G_bf[:], xn_sb[0:64, ts(tau, 512)], start=True, stop=True),
                extra=[ev_G, ev_xn])
            ev_xr = K.rd("scalar", [6, 7], lambda s, tau=tau: s.copy(xr_bf[0:64, ts(tau, 512)],
                                                                     pbig[1][0:64, 0:512]))
        for dcb in range(DC):
            K.mm([5], lambda t, dcb=dcb: t.matmul(
                pbig[0][0:1, 512:576], hm_cols[:, dcb:dcb + 1], vpiw_bf[:, ts(dcb, HD)],
                start=(dcb == 0), stop=(dcb == DC - 1)),
                extra=[ev_vpiw, ev_hm], first=(dcb == 0), last=(dcb == DC - 1))
        ev_vq0 = K.rd("vector", [5], lambda vv: vv.scalar_tensor_tensor(
            vq_row[:], pbig[0][0:1, 512:576], 1.0 / T, vpib_sb[:],
            op0=ALU.mult, op1=ALU.add))
        ev_sq = K.op("scalar", lambda s: s.activation(vq_sq[:], vq_row[:], AF.Square),
                     waits=[ev_vq0])
        e1 = K.op("vector", lambda vv: vv.tensor_reduce(vq_n[0:1, 0:1], vq_sq[:], AX.X, ALU.add),
                  waits=[ev_sq])
        e2 = K.op("scalar", lambda s: s.activation(vq_n[0:1, 1:2], vq_n[0:1, 0:1], AF.Sqrt),
                  waits=[e1])
        K.op("vector", lambda vv: vv.tensor_scalar(vq_n[0:1, 2:3], vq_n[0:1, 1:2], 1e-12, None, op0=ALU.max),
             waits=[e2])
        K.op("vector", lambda vv: vv.reciprocal(vq_n[0:1, 3:4], vq_n[0:1, 2:3]))
        ev_vqn = K.op("vector", lambda vv: vv.tensor_scalar(vq_row[:], vq_row[:], vq_n[0:1, 3:4], None, op0=ALU.mult))
        ev_cbsq = K.op("scalar", lambda s: s.activation(cb_sq[:], cb_sb[:], AF.Square),
                       waits=[ev_consts])
        K.op("vector", lambda vv: vv.tensor_reduce(cb_n[:, 0:1], cb_sq[:, 0:64], AX.X, ALU.add),
             waits=[ev_cbsq])
        e3 = K.op("vector", lambda vv: vv.tensor_reduce(cb_n[:, 1:2], cb_sq[:, 64:128], AX.X, ALU.add))
        e4 = K.op("scalar", lambda s: s.activation(cb_n[:, 2:4], cb_n[:, 0:2], AF.Sqrt),
                  waits=[e3])
        K.op("vector", lambda vv: vv.tensor_scalar(cb_n[:, 2:4], cb_n[:, 2:4], 1e-12, None, op0=ALU.max),
             waits=[e4])
        ev_rcb = K.op("vector", lambda vv: vv.reciprocal(cb_n[:, 4:6], cb_n[:, 2:4]))
        mmn = K.mm([5], lambda t: t.matmul(pbig[0][:, 576:640], ones_sb[0:1, :], vq_row[:],
                                           start=True, stop=True), extra=[ev_vqn])
        ev_vqr = K.rd("scalar", [5], lambda s: s.copy(vqn_rep[:], pbig[0][:, 576:640]))
        K.op("vector", lambda vv: vv.tensor_tensor(cb_sc[:, 0:64], cb_sb[:, 0:64], vqn_rep[:], op=ALU.mult),
             waits=[ev_vqr])
        K.op("vector", lambda vv: vv.tensor_tensor(cb_sc[:, 64:128], cb_sb[:, 64:128], vqn_rep[:], op=ALU.mult))
        K.op("vector", lambda vv: vv.tensor_reduce(cb_n[:, 6:7], cb_sc[:, 0:64], AX.X, ALU.add))
        K.op("vector", lambda vv: vv.tensor_reduce(cb_n[:, 7:8], cb_sc[:, 64:128], AX.X, ALU.add))
        ev_scr = K.op("vector", lambda vv: vv.tensor_tensor(cb_n[:, 6:8], cb_n[:, 6:8], cb_n[:, 4:6], op=ALU.mult),
                      waits=[ev_rcb])
        ev_m1 = K.op("vector", lambda vv: vv.tensor_reduce(cb_n[:, 8:9], cb_n[:, 6:8], AX.X, ALU.max))
        mtx = K.mm([4], lambda t: t.transpose(pbig[0][0:1, 0:128], cb_n[:, 8:9], ident_sb[:]),
                   extra=[ev_m1])
        ev_mrow = K.rd("scalar", [4], lambda s: s.copy(vq_n[0:1, 8:136], pbig[0][0:1, 0:128]))
        ev_gmax = K.op("vector", lambda vv: vv.tensor_reduce(vq_n[0:1, 4:5], vq_n[0:1, 8:136], AX.X, ALU.max),
                       waits=[ev_mrow])
        mmg = K.mm([4], lambda t: t.matmul(pbig[0][:, 128:129], ones_sb[0:1, :], vq_n[0:1, 4:5],
                                           start=True, stop=True), extra=[ev_gmax])
        ev_gcol = K.rd("scalar", [4], lambda s: s.copy(vq_gc[:], pbig[0][:, 128:129]))
        ev_ind2 = K.op("vector", lambda vv: vv.tensor_scalar(vq_ind[:], cb_n[:, 6:8], vq_gc[:], None, op0=ALU.is_equal),
                       waits=[ev_gcol, ev_scr])
        for j in range(2):
            K.mm([4], lambda t, j=j: t.matmul(pbig[0][0:1, 192:256], vq_ind[:, j:j + 1],
                                              cb_sb[:, ts(j, 64)], start=(j == 0), stop=(j == 1)),
                 extra=[ev_ind2], first=(j == 0), last=(j == 1))
        ev_coder = K.rd("scalar", [4], lambda s: s.copy(code_row[:], pbig[0][0:1, 192:256]))
        mtc = K.mm([4], lambda t: t.transpose(pbig[0][0:64, 256:257], code_row[:], ident_sb[0:1, 0:1]),
                   extra=[ev_coder])
        ev_codec = K.rd("scalar", [4], lambda s: s.copy(code_col[:], pbig[0][0:64, 256:257]))
        for i, dst in ((0, omix_col), (1, vqmix_col)):
            K.mm([4], lambda t, i=i: t.matmul(pbig[0][:, 300 + i:301 + i], ones_sb[0:1, :],
                                              mixes_sb[0:1, i:i + 1], start=True, stop=True),
                 extra=[ev_consts])
            K.rd("scalar", [4], lambda s, dst=dst, i=i: s.copy(dst[:], pbig[0][:, 300 + i:301 + i]))
        ev_mix = (sc, sc.n)
        for pc in range(DC):
            K.mm([4], lambda t, pc=pc: t.matmul(pbig[0][:, 320 + pc:321 + pc],
                                                vpow_bf[:, ts(pc, 128)], code_col[:],
                                                start=True, stop=True),
                 extra=[ev_vpow, ev_codec], first=(pc == 0), last=(pc == DC - 1))
        K.rd("vector", [4], lambda vv: vv.tensor_tensor(
            vqc_cols[:], pbig[0][:, 320:328], vpobc_sb[:], op=ALU.add), extra=[ev_mix])
        ev_vqc = K.op("vector", lambda vv: vv.tensor_scalar(vqc_cols[:], vqc_cols[:], vqmix_col[:], None, op0=ALU.mult))
        lie_free = [None, None]
        x1_free = [None, None]
        ss_read = None
        for tau in range(TQ):
            for pc in range(DC):
                lpar = pc % 2
                mmk = K.mm([lpar], lambda t, pc=pc, tau=tau, lpar=lpar: t.matmul(
                    pb[lpar][:, 0:512], oow_bf[:, ts(pc, 128)], xr_bf[0:64, ts(tau, 512)],
                    start=True, stop=True), extra=[ev_oow, lie_free[lpar]])
                K.rd("vector", [lpar], lambda vv, pc=pc, lpar=lpar: vv.tensor_scalar(
                    ffn_tmp[:], pb[lpar][:, 0:512], oobc_sb[:, pc:pc + 1], omix_col[:],
                    op0=ALU.add, op1=ALU.mult))
                prt = pre[:, pc * T + tau * 512: pc * T + tau * 512 + 512]
                ev_pre = K.op("vector", lambda vv, pc=pc, tau=tau, prt=prt: vv.scalar_tensor_tensor(
                    prt, ffn_tmp[:], vqc_cols[:, pc:pc + 1],
                    xn_sb[:, pc * T + tau * 512: pc * T + tau * 512 + 512],
                    op0=ALU.add, op1=ALU.add))
                lie_free[lpar] = ev_pre
                ev_sq = K.op("scalar", lambda s, prt=prt: s.activation(sq_t[:], prt, AF.Square),
                             waits=[ev_pre, state.get("l1_sq_free")])
                state["l1_sq_free"] = K.mm(
                    [2], lambda t, pc=pc: t.matmul(pb[2][0:1, 0:512], ones_sb[:, 0:1], sq_t[:],
                                                   start=(pc == 0), stop=(pc == DC - 1)),
                    extra=[ev_sq], first=(pc == 0), last=True)
            e1 = K.rd("vector", [2], lambda vv: vv.tensor_scalar(
                rowC[0:1, 0:512], pb[2][0:1, 0:512], 1.0 / D, RMS_EPS, op0=ALU.mult, op1=ALU.add))
            e2 = K.op("scalar", lambda s: s.activation(rowC[0:1, 0:512], rowC[0:1, 0:512],
                                                       AF.Sqrt), waits=[e1])
            e3 = K.op("vector", lambda vv: vv.reciprocal(rowC[0:1, 0:512], rowC[0:1, 0:512]),
                      waits=[e2])
            K.mm([3], lambda t: t.matmul(pb[3][:, 0:512], ones_sb[0:1, :], rowC[0:1, 0:512],
                                         start=True, stop=True), extra=[e3])
            for pc in range(DC):
                xpar = pc % 2
                prt = pre[:, pc * T + tau * 512: pc * T + tau * 512 + 512]
                ev_xl = K.op("sync", lambda e, pc=pc, tau=tau, xpar=xpar: e.dma_start(
                    out=x1t[xpar][:], in_=x_d[:, pc, ts(tau, 512)]),
                    waits=[x1_free[xpar]], dma=True, dma_sem=f"d_x{xpar}")
                ev_sil = K.op("scalar", lambda s, pc=pc, tau=tau: s.activation(
                    gsil_t[:], sg[:, pc * T + tau * 512: pc * T + tau * 512 + 512], AF.Silu),
                    waits=[state[("ffn_v", 1)], state.get("l1_gsil_free")])
                ev_s1 = K.rd("vector", [3], lambda vv, prt=prt: vv.tensor_tensor(
                    s1_t[:], prt, pb[3][:, 0:512], op=ALU.mult))
                ev_s1b = K.op("vector", lambda vv, pc=pc: vv.scalar_tensor_tensor(
                    s1_t[:], s1_t[:], nwc_sb[:, pc:pc + 1], gsil_t[:], op0=ALU.mult, op1=ALU.mult),
                    waits=[ev_sil])
                state["l1_gsil_free"] = ev_s1b
                ev_fin = K.op("vector", lambda vv, xpar=xpar: vv.tensor_tensor(
                    x1t[xpar][:], x1t[xpar][:], s1_t[:], op=ALU.add), waits=[ev_xl])
                x1_free[xpar] = K.op("sync", lambda e, pc=pc, tau=tau, xpar=xpar: e.dma_start(
                    out=out_fm[:, pc, ts(tau, 512)], in_=x1t[xpar][:]), waits=[ev_fin],
                    dma=True, dma_sem=f"d_xs{xpar}")

    with nc.sbuf_tensor("xs0", [128, DC * T], F32) as xs0, \
         nc.sbuf_tensor("xn0", [128, DC * T], BF16) as xn0, \
         nc.sbuf_tensor("zero_sb", [128, 1024], BF16) as zero_sb, \
         nc.sbuf_tensor("lnsq0", [128, 512], F32) as lnsq0, \
         nc.sbuf_tensor("lnsq1", [128, 512], F32) as lnsq1, \
         nc.sbuf_tensor("rowA0", [1, 512], F32) as rowA0, \
         nc.sbuf_tensor("rowB0", [1, 512], F32) as rowB0:
        ev_zmem = K.op("vector", lambda vv: vv.memset(zero_sb[:], 0.0))
        for i in range(E * CAP // 128):
            K.op("sync", lambda e, i=i: e.dma_start(
                out=xg_d[i * 128:(i + 1) * 128, :], in_=zero_sb[:]), waits=[ev_zmem], dma=True)
        phase_ln(0, x0.rearrange("(dc p) t -> p dc t", p=128), xs0, xn0, False, (lnsq0, lnsq1), rowA0, rowB0)
        K.bar()

    with contextlib.ExitStack() as stf0:
        sbf = lambda n, sh, dt, td=False: stf0.enter_context(
            nc.sbuf_tensor(n, sh, dt, is_transpose_dest=td))
        xgfm0 = (sbf("xgfm00", [128, DC * CAP], BF16, True), sbf("xgfm01", [128, DC * CAP], BF16, True))
        hact0 = sbf("hact0", [128, HC * CAP], BF16)
        w1a = tuple(sbf(f"w1a{i}", [128, DC * 256], BF16) for i in range(6))
        w2a = tuple(sbf(f"w2a{i}", [128, 1024], BF16) for i in range(6))
        ytma = (sbf("ytma0", [128, (CAP // 128) * 1024], BF16), sbf("ytma1", [128, (CAP // 128) * 1024], BF16))
        xntma = (sbf("xntma0", [128, 1024], BF16, True), sbf("xntma1", [128, 1024], BF16, True))
        b2ra = (sbf("b2ra0", [1, D], F32), sbf("b2ra1", [1, D], F32))
        phase_ffn_sparse(0, xgfm0, hact0, w1a, w2a, ytma, xntma, b2ra, True)
        K.bar()
    if sg_d is not None:
        K.op("sync", lambda e: e.dma_start(out=sg_d, in_=sg[:]), dma=True)
        K.bar()

    with contextlib.ExitStack() as st3:
        sb3 = lambda n, sh, dt, td=False: st3.enter_context(
            nc.sbuf_tensor(n, sh, dt, is_transpose_dest=td))
        xna = sb3("xna", [128, DC * T], BF16)
        av_sb = sb3("av", [128, DC * T], BF16)
        wpT0 = sb3("wpT0", [128, DC * 128], BF16)
        wpT1 = sb3("wpT1", [128, DC * 128], BF16)
        attnTa = sb3("attnTa", [128, TC * 512], BF16)
        attnTb = sb3("attnTb", [128, TC * 512], BF16)
        Qba = sb3("Qba", [128, T], BF16)
        Qbb = sb3("Qbb", [128, T], BF16)
        Kba = sb3("Kba", [128, T], BF16)
        Kbb = sb3("Kbb", [128, T], BF16)
        Vba = sb3("Vba", [128, TC * 65], BF16)
        Vbb = sb3("Vbb", [128, TC * 65], BF16)
        qkvb0 = sb3("qkvb0", [128, 3 * 512], BF16)
        qkvb1 = sb3("qkvb1", [128, 3 * 512], BF16)
        rrepa = sb3("rrepa", [HD, 512], F32)
        rrepb = sb3("rrepb", [HD, 512], F32)
        nmcols = sb3("nmcols", [128, 10], F32)
        rrow = rowC
        x0t0 = sb3("x0t0", [128, 512], F32)
        x0t1 = sb3("x0t1", [128, 512], F32)
        phase_attn(xna, av_sb, (wpT0, wpT1), (attnTa, attnTb), (Qba, Qbb),
                   (Kba, Kbb), (Vba, Vbb), (qkvb0, qkvb1), (rrepa, rrepb),
                   nmcols, rrow, (x0t0, x0t1))
        K.bar()

    with nc.sbuf_tensor("xs1", [128, DC * T], F32) as xs1, \
         nc.sbuf_tensor("xn1", [128, DC * T], BF16) as xn1, \
         nc.sbuf_tensor("lnsq0b", [128, 512], F32) as lnsq0b, \
         nc.sbuf_tensor("lnsq1b", [128, 512], F32) as lnsq1b, \
         nc.sbuf_tensor("rowA1", [1, 512], F32) as rowA1, \
         nc.sbuf_tensor("rowB1", [1, 512], F32) as rowB1:
        phase_ln(1, x_d, xs1, xn1, True, (lnsq0b, lnsq1b), rowA1, rowB1)
        K.bar()

    with contextlib.ExitStack() as stf1:
        sbf = lambda n, sh, dt, td=False: stf1.enter_context(
            nc.sbuf_tensor(n, sh, dt, is_transpose_dest=td))
        xgfm1 = (sbf("xgfm10", [128, DC * CAP], BF16, True), sbf("xgfm11", [128, DC * CAP], BF16, True))
        hact1 = sbf("hact1", [128, HC * CAP], BF16)
        w1b = tuple(sbf(f"w1b{i}", [128, DC * 256], BF16) for i in range(6))
        w2b = tuple(sbf(f"w2b{i}", [128, 1024], BF16) for i in range(6))
        ytmb = (sbf("ytmb0", [128, (CAP // 128) * 1024], BF16), sbf("ytmb1", [128, (CAP // 128) * 1024], BF16))
        xntmb = (sbf("xntmb0", [128, 1024], BF16, True), sbf("xntmb1", [128, 1024], BF16, True))
        b2rb = (sbf("b2rb0", [1, D], F32), sbf("b2rb1", [1, D], F32))
        phase_ffn_sparse(1, xgfm1, hact1, w1b, w2b, ytmb, xntmb, b2rb, False, precast=True)
        K.bar()
    if sg_d1 is not None:
        K.op("sync", lambda e: e.dma_start(out=sg_d1, in_=sg[:]), dma=True)
        K.bar()

    with contextlib.ExitStack() as st6:
        sb6 = lambda n, sh, dt: st6.enter_context(nc.sbuf_tensor(n, sh, dt))
        xnb = sb6("xnb", [128, DC * T], BF16)
        pre = sb6("pre", [128, DC * T], BF16)
        xr_bf = sb6("xr", [64, T], BF16)
        oow_bf = sb6("oowb", [HD, D], BF16)
        vpow_bf = sb6("vpowb", [HD, D], BF16)
        opwb0 = sb6("opwb0", [128, 2016], BF16)
        opwb1 = sb6("opwb1", [128, 2016], BF16)
        opwb2 = sb6("opwb2", [128, 2016], BF16)
        vpiw_bf = sb6("vpiwb", [128, DC * HD], F32)
        params_sb = sb6("params", [1, 2016], F32)
        om_u = sb6("om_u", [64, 64], F32)
        om_ut = sb6("om_ut", [64, 64], F32)
        om = sb6("om", [64, 64], F32)
        om_n = sb6("om_n", [64, 64], F32)
        om2 = sb6("om2", [64, 64], F32)
        om3 = sb6("om3", [64, 64], F32)
        Gt = sb6("Gt", [64, 64], F32)
        G_bf = sb6("G_bf", [64, 64], BF16)
        vq_row = sb6("vq_row", [1, 64], F32)
        vq_sq = sb6("vq_sq", [1, 64], F32)
        vq_n = sb6("vq_n", [1, 256], F32)
        cb_sq = sb6("cb_sq", [128, 128], F32)
        cb_n = sb6("cb_n", [128, 9], F32)
        cb_sc = sb6("cb_sc", [128, 128], F32)
        vqn_rep = sb6("vqn_rep", [128, 64], F32)
        vq_gc = sb6("vq_gc", [128, 1], F32)
        vq_ind = sb6("vq_ind", [128, 2], F32)
        code_row = sb6("code_row", [1, 64], F32)
        code_col = sb6("code_col", [64, 1], BF16)
        omix_col = sb6("omix", [128, 1], F32)
        vqmix_col = sb6("vqmix", [128, 1], F32)
        vqc_cols = sb6("vqc_cols", [128, DC], F32)
        sq_t = sb6("sq_t", [128, 512], F32)
        gsil_t = sb6("gsil_t", [128, 512], F32)
        s1_t = sb6("s1_t", [128, 512], F32)
        x1t0 = sb6("x1t0", [128, 512], F32)
        x1t1 = sb6("x1t1", [128, 512], F32)
        vqs = (params_sb, opb_sb, om_u, om_ut, om, om_n, om2, om3, Gt, G_bf, vq_row,
               vq_sq, vq_n, cb_sq, cb_n, cb_sc, vqn_rep, vq_gc, vq_ind,
               code_row, code_col, omix_col, vqmix_col, vqc_cols)
        phase_l1small(xnb, pre, xr_bf, oow_bf, vpow_bf, (opwb0, opwb1, opwb2), vpiw_bf,
                      sq_t, gsil_t, s1_t, (x1t0, x1t1), vqs)

    K.bar()

    with nc.Block() as block:
        K.replay(block)
    stk.close()
    return nc


_NC_CACHE = {}


def get_nc(dbg=()):
    key = tuple(sorted(dbg))
    if key not in _NC_CACHE:
        _NC_CACHE[key] = build(key)
    return _NC_CACHE[key]


def host_inputs(inputs):
    f = np.float32
    a = lambda x: np.ascontiguousarray(x, dtype=f)
    g = {}
    g["ident"] = np.eye(128, dtype=f)
    g["ones"] = np.ones((128, 128), dtype=f)
    g["Wqkv"] = a(inputs["L0_attn_Wqkv"])
    bqkv = np.asarray(inputs["L0_attn_bqkv"], dtype=f)
    g["bqkvT"] = a(bqkv.T)
    g["bqkv_flat"] = a(bqkv.reshape(1, -1))
    g["WpT"] = a(np.asarray(inputs["L0_attn_Wp"], dtype=f).T)
    g["bp_cols"] = a(np.asarray(inputs["L0_attn_bp"], dtype=f).reshape(8, 128).T)
    g["Wg0"] = a(inputs["L0_ffn_Wg"])
    g["bg0"] = a(np.asarray(inputs["L0_ffn_bg"], dtype=f).reshape(1, -1))
    g["W1_0"] = a(inputs["L0_ffn_W1"])
    g["W2_0"] = a(inputs["L0_ffn_W2"])
    g["Wg1"] = a(np.asarray(inputs["L1_ffn_gate_w"], dtype=f).T)
    g["bg1"] = a(np.asarray(inputs["L1_ffn_gate_b"], dtype=f).reshape(1, -1))
    g["W1_1"] = a(np.asarray(inputs["L1_ffn_w1"], dtype=f).transpose(0, 2, 1))
    g["W2_1"] = a(np.asarray(inputs["L1_ffn_w2"], dtype=f).transpose(0, 2, 1))
    g["b1c"] = a(np.asarray(inputs["L1_ffn_b1"], dtype=f).reshape(8, 32, 128).transpose(0, 2, 1))
    g["b2c"] = a(np.asarray(inputs["L1_ffn_b2"], dtype=f).reshape(8, 8, 128).transpose(0, 2, 1))
    g["opwT"] = a(np.asarray(inputs["L1_attn_omega_proj_w"], dtype=f).T)
    g["opb"] = a(np.asarray(inputs["L1_attn_omega_proj_b"], dtype=f).reshape(1, -1))
    g["oowT"] = a(np.asarray(inputs["L1_attn_omega_out_w"], dtype=f).T)
    g["oobc"] = a(np.asarray(inputs["L1_attn_omega_out_b"], dtype=f).reshape(8, 128).T)
    g["vpiwT"] = a(np.asarray(inputs["L1_attn_vq_proj_in_w"], dtype=f).T)
    g["vpib"] = a(np.asarray(inputs["L1_attn_vq_proj_in_b"], dtype=f).reshape(1, -1))
    g["vpowT"] = a(np.asarray(inputs["L1_attn_vq_proj_out_w"], dtype=f).T)
    g["vpobc"] = a(np.asarray(inputs["L1_attn_vq_proj_out_b"], dtype=f).reshape(8, 128).T)
    g["lt128"] = np.triu(np.ones((128, 128), dtype=f))
    g["ecm1"] = np.tile((np.arange(8, dtype=f) * 384.0 - 1.0)[None, :], (128, 1))
    g["b2r"] = a(np.asarray(inputs["L1_ffn_b2"], dtype=f).reshape(8, 1, 1024))
    cb = np.asarray(inputs["L1_attn_codebook"], dtype=f)
    g["cbp"] = a(np.concatenate([cb[0:128], cb[128:256]], axis=1))
    g["mixes"] = np.array([[float(inputs["L1_attn_omega_mix"]),
                            float(inputs["L1_attn_vq_mix"])]], dtype=f)
    g["nwc"] = a(np.asarray(inputs["L1_attn_norm_w"], dtype=f).reshape(8, 128).T)
    return g


def kernel(**inputs):
    nc = get_nc()
    shared = host_inputs(inputs)
    x = np.asarray(inputs["L0_attn_x"], dtype=np.float32)
    in_maps = []
    for b in range(NCORES):
        m = dict(shared)
        m["x0"] = np.ascontiguousarray(x[b].T)
        in_maps.append(m)
    res = run_bass_kernel_spmd(nc, in_maps, core_ids=list(range(NCORES)))
    outs = []
    for b in range(NCORES):
        o_ = np.asarray(res.results[b]["out_fm"])
        outs.append(o_.transpose(1, 0, 2).reshape(D, T).T)
    return np.stack(outs, axis=0)


# version 18
# speedup vs baseline: 1.1056x; 1.1056x over previous
import contextlib
import numpy as np
import concourse.bass as bass
import concourse.mybir as mybir
from concourse.bass import IndirectOffsetOnAxis
from concourse.bass_utils import run_bass_kernel_spmd

F32 = mybir.dt.float32
BF16 = mybir.dt.bfloat16
AF = mybir.ActivationFunctionType
ALU = mybir.AluOpType
AX = mybir.AxisListType

D, T, H, E, NH, HD = 1024, 2048, 4096, 8, 16, 64
DC, TQ, TC, TH, HC = 8, 4, 16, 2, 32
NCORES = 8
RMS_EPS = 1e-6
LN_EPS = 1e-5
CAP = 384
I32 = mybir.dt.int32


def ts(i, n):
    return slice(i * n, (i + 1) * n)


class Ctr:
    def __init__(self, h):
        self.h = h
        self.n = 0


class B:
    ENGINES = ("tensor", "vector", "scalar", "gpsimd", "sync")

    def __init__(self, nc, stk):
        self.nc = nc
        self.stk = stk
        self.q = {e: [] for e in self.ENGINES}
        self.ctr = {}
        for name in ("pe", "v", "sc", "gp", "dg", "ds"):
            self.ctr[name] = Ctr(stk.enter_context(nc.semaphore("s_" + name)))
        self.eng2ctr = {"tensor": "pe", "vector": "v", "scalar": "sc",
                        "gpsimd": "gp", "sync": None}
        self.serial_dma = {id(self.ctr["dg"]), id(self.ctr["ds"])}
        self.bank = {i: {"w": None, "r": {}} for i in range(8)}

    def dsem(self, name):
        if name not in self.ctr:
            self.ctr[name] = Ctr(self.stk.enter_context(self.nc.semaphore(name)))
        return self.ctr[name]

    def op(self, eng, builder, waits=(), inc=None, dma=False, dma_sem=None):
        ww = {}
        waits = list(waits)
        if eng in ("vector", "scalar") and not dma:
            cs = self.ctr[self.eng2ctr[eng]]
            if cs.n > 0:
                waits.append((cs, cs.n))
        for w in waits:
            if w is None:
                continue
            c, v = w
            if v is None:
                continue
            if id(c) in self.serial_dma:
                v = c.n
            ww[id(c)] = (c.h, max(v, ww[id(c)][1]) if id(c) in ww else v)
        ww = list(ww.values())
        incs = []
        if dma:
            if dma_sem is not None:
                c = self.dsem(dma_sem)
            else:
                c = self.ctr["dg" if eng == "gpsimd" else "ds"]
            c.n += 16
            incs = [(c.h, 16)]
            mark = (c, c.n)
        elif inc is False:
            mark = None
        else:
            c = self.ctr[self.eng2ctr[eng]]
            assert c is not None
            c.n += 1
            incs = [(c.h, 1)]
            mark = (c, c.n)

        def go(engine, ww=ww, incs=incs, builder=builder):
            for h, v in ww:
                engine.wait_ge(h, v)
            ins = builder(engine)
            for h, k in incs:
                ins.then_inc(h, k)
        self.q[eng].append(go)
        return mark

    def wait_only(self, eng, waits):
        ww = []
        for w in waits:
            if w is None or w[1] is None:
                continue
            vv_ = w[0].n if id(w[0]) in self.serial_dma else w[1]
            ww.append((w[0].h, vv_))

        def go(engine, ww=ww):
            for h, v in ww:
                engine.wait_ge(h, v)
        self.q[eng].append(go)

    def marks_all(self):
        return [(c, c.n) for c in self.ctr.values()]

    def bar(self):
        tgt = self.marks_all()
        for eng in self.ENGINES:
            self.wait_only(eng, tgt)
        for b in self.bank.values():
            b["w"] = None
            b["r"] = {}

    def bank_waits_for_write(self, banks):
        w = []
        for b in banks:
            w += list(self.bank[b]["r"].values())
        return w

    def note_write(self, banks, mark):
        for b in banks:
            self.bank[b]["w"] = mark
            self.bank[b]["r"] = {}

    def read_waits(self, banks):
        return [self.bank[b]["w"] for b in banks if self.bank[b]["w"] is not None]

    def note_read(self, banks, eng, mark):
        for b in banks:
            self.bank[b]["r"][eng] = mark

    def mm(self, banks, builder, extra=(), first=True, last=True, force_inc=False):
        waits = list(extra)
        if first:
            waits += self.bank_waits_for_write(banks)
        mark = self.op("tensor", builder,
                       waits=waits, inc=(None if (last or force_inc) else False))
        if last:
            self.note_write(banks, mark)
        return mark if (last or force_inc) else None

    def rd(self, eng, banks, builder, extra=()):
        waits = list(extra) + self.read_waits(banks)
        mark = self.op(eng, builder, waits=waits)
        self.note_read(banks, eng, mark)
        return mark

    def replay(self, block):
        for eng in self.ENGINES:
            items = self.q[eng]
            if not items:
                continue

            def body(engine, items=items):
                for fn in items:
                    fn(engine)
            getattr(block, eng)(body)


def build(dbg=()):
    nc = bass.Bass("TRN2", target_bir_lowering=False, debug=False)

    def din(name, shape, dtype=F32):
        return nc.dram_tensor(name, shape, dtype, kind="ExternalInput").ap()

    x0 = din("x0", [D, T])
    ident = din("ident", [128, 128])
    ones = din("ones", [128, 128])
    Wqkv = din("Wqkv", [48, D, HD])
    bqkvT = din("bqkvT", [HD, 48])
    bqkv_flat = din("bqkv_flat", [1, 48 * HD])
    WpT = din("WpT", [D, D])
    bp_cols = din("bp_cols", [128, DC])
    Wg_l = [din("Wg0", [D, E]), din("Wg1", [D, E])]
    bg_l = [din("bg0", [1, E]), din("bg1", [1, E])]
    W1_l = [din("W1_0", [E, D, H]), din("W1_1", [E, D, H])]
    W2_l = [din("W2_0", [E, H, D]), din("W2_1", [E, H, D])]
    b1c = din("b1c", [E, 128, 32])
    b2c = din("b2c", [E, 128, DC])
    opwT = din("opwT", [D, 2016])
    opb = din("opb", [1, 2016])
    oowT = din("oowT", [HD, D])
    oobc = din("oobc", [128, DC])
    vpiwT = din("vpiwT", [D, HD])
    vpib = din("vpib", [1, HD])
    vpowT = din("vpowT", [HD, D])
    vpobc = din("vpobc", [128, DC])
    lt128 = din("lt128", [128, 128])
    ecm1 = din("ecm1", [128, E])
    b2r = din("b2r", [E, 1, D])
    cbp = din("cbp", [128, 128])
    mixes = din("mixes", [1, 2])
    nwc = din("nwc", [128, DC])

    ok = lambda n: "ExternalOutput" if n in dbg else "Internal"
    xn_d = nc.dram_tensor("xn_d", [128, DC, T], BF16, kind=ok("xn_d")).ap()
    x_d = nc.dram_tensor("x_d", [128, DC, T], F32, kind=ok("x_d")).ap()
    sg_d = nc.dram_tensor("sg_dump", [128, DC * T], BF16, kind="ExternalOutput").ap() if "sg" in dbg else None
    sg_d1 = nc.dram_tensor("sg_dump1", [128, DC * T], BF16, kind="ExternalOutput").ap() if "sg1" in dbg else None
    out_fm = nc.dram_tensor("out_fm", [128, DC, T], F32, kind="ExternalOutput").ap()
    ind_d = nc.dram_tensor("ind_d", [8, T], F32, kind="Internal").ap()
    xg_d = nc.dram_tensor("xg_d", [E * CAP, D], BF16, kind="Internal").ap()
    w1bf_d = nc.dram_tensor("w1bf_d", [E, D, H], BF16, kind="Internal").ap()
    w2bf_d = nc.dram_tensor("w2bf_d", [E, H, D], BF16, kind="Internal").ap()
    ytm_d = nc.dram_tensor("ytm_d", [E * CAP, D], BF16, kind="Internal").ap()
    gtm_d = nc.dram_tensor("gtm_d", [T, D], BF16, kind="Internal").ap()

    stk = contextlib.ExitStack()
    sbuf = lambda name, shape, dtype: stk.enter_context(nc.sbuf_tensor(name, shape, dtype))
    K = B(nc, stk)
    pe, v, sc, dg, ds = (K.ctr[k] for k in ("pe", "v", "sc", "dg", "ds"))

    pb01 = [stk.enter_context(nc.psum_tensor(f"pb{i}", [128, 512], F32)) for i in range(2)]
    p4 = stk.enter_context(nc.psum_tensor("p4", [128, 2048], F32))
    pbig1_t = stk.enter_context(nc.psum_tensor("pbig1", [128, 1024], F32))
    pb = [pb01[0].ap() if hasattr(pb01[0], "ap") else pb01[0], None, None, None]
    pb[0] = pb01[0]
    pb[1] = pb01[1]
    pb[2] = p4[:, 0:512]
    pb[3] = p4[:, 512:1024]
    pbig = [p4[:, 1024:2048], pbig1_t]

    sg = stk.enter_context(nc.sbuf_tensor("sg", [128, DC * T], BF16, is_transpose_dest=True))
    lt_sb = sbuf("lt_sb", [128, 128], F32)
    ecm1_sb = sbuf("ecm1_sb", [128, E], F32)
    ind_all = sbuf("ind_all", [128, TC * E], F32)
    cs_rows = sbuf("cs_rows", [1, 128], F32)
    bases = sbuf("bases", [1, 128], F32)
    tmp8 = sbuf("tmp8", [128, E], F32)
    tcol = sbuf("tcol", [128, 1], F32)
    tsl_sb = sbuf("tsl_sb", [128, TC], I32)
    pc_stage = [sbuf(f"pc_stage{i}", [128, 2048], BF16) for i in range(2)]
    ident_sb = sbuf("ident_sb", [128, 128], F32)
    ones_sb = sbuf("ones_sb", [128, 128], F32)
    ind_fm = sbuf("ind_fm", [8, T], F32)
    mrow_sb = sbuf("mrow_sb", [1, 1024], F32)
    hm_cols = sbuf("hm_cols", [128, DC], F32)
    hm_bf = sbuf("hm_bf", [128, DC], BF16)
    bg_rep = sbuf("bg_rep", [128, E], F32)
    rowC = sbuf("rowC", [1, 512], F32)
    ltm = sbuf("ltm", [128, E], F32)
    mxcol = sbuf("mxcol", [128, 1], F32)
    ind_tm = sbuf("ind_tm", [128, E], F32)
    wg_bf = sbuf("wg_bf", [128, DC * E], F32)
    rstdc_sb = sbuf("rstdc_sb", [128, 4], F32)
    bgrow = sbuf("bgrow", [1, E], F32)
    bp_sb = sbuf("bp_sb", [128, DC], F32)
    bqkvT_sb = sbuf("bqkvT_sb", [HD, 48], F32)
    oobc_sb = sbuf("oobc_sb", [128, DC], F32)
    vpobc_sb = sbuf("vpobc_sb", [128, DC], F32)
    nwc_sb = sbuf("nwc_sb", [128, DC], F32)
    opb_sb = sbuf("opb_sb", [1, 2016], F32)
    vpib_sb = sbuf("vpib_sb", [1, HD], F32)
    mixes_sb = sbuf("mixes_sb", [1, 2], F32)
    cb_sb = sbuf("cb_sb", [128, 128], F32)
    b1c_sb = [sbuf(f"b1c_sb{i}", [128, 32], F32) for i in range(2)]
    b2c_sb = [sbuf(f"b2c_sb{i}", [128, DC], F32) for i in range(2)]
    ffn_tmp = sbuf("ffn_tmp", [128, 512], F32)

    state = {}

    for dst, src in [
        (ident_sb[:], ident), (ones_sb[:], ones), (bp_sb[:], bp_cols),
        (bqkvT_sb[:], bqkvT), (oobc_sb[:], oobc),
        (vpobc_sb[:], vpobc), (nwc_sb[:], nwc), (opb_sb[:], opb),
        (vpib_sb[:], vpib), (mixes_sb[:], mixes), (cb_sb[:], cbp),
        (lt_sb[:], lt128), (ecm1_sb[:], ecm1),
    ]:
        ev_consts = K.op("sync", lambda e, d=dst, s=src: e.dma_start(out=d, in_=s), dma=True)

    def phase_ln(l, xsrc, xs, xn_sb, want_hm, ln_sq, rowA, rowB):
        ev_x = K.op("sync", lambda e: e.dma_start(
            out=xs[:].rearrange("p (dc t) -> p dc t", dc=DC), in_=xsrc), dma=True)
        ev_wg = K.op("sync", lambda g: g.dma_start(
            out=wg_bf[:].rearrange("p (dc e) -> p dc e", dc=DC),
            in_=Wg_l[l].rearrange("(dc p) e -> p dc e", p=128)), dma=True)
        ev_bg = K.op("sync", lambda e: e.dma_start(out=bgrow[:], in_=bg_l[l]), dma=True)
        m = K.mm([0], lambda t: t.matmul(pb[0][:, 0:E], ones_sb[0:1, :], bgrow[:],
                                         start=True, stop=True),
                 extra=[ev_bg, ev_consts])
        ev_bgrep = K.rd("scalar", [0], lambda s: s.copy(bg_rep[:], pb[0][:, 0:E]))

        def emit_stats(tau):
            for dc in range(DC):
                par = dc % 2
                xt = xs[:, dc * T + tau * 512: (dc * T + tau * 512) + 512]
                sq_ev = K.op("scalar", lambda s, a=ln_sq[par], b=xt: s.activation(a[:], b, AF.Square),
                             waits=[ev_x, state.get(("lnsq", par))])
                K.mm([0], lambda t, b=xt, dc=dc: t.matmul(
                    pb[0][0:1, 0:512], ones_sb[:, 0:1], b, start=(dc == 0), stop=(dc == DC - 1)),
                    extra=[ev_x, ev_consts], first=(dc == 0), last=(dc == DC - 1))
                m2 = K.mm([1], lambda t, a=ln_sq[par], dc=dc: t.matmul(
                    pb[1][0:1, 0:512], ones_sb[:, 0:1], a[:], start=(dc == 0), stop=(dc == DC - 1)),
                    extra=[sq_ev], first=(dc == 0), last=True)
                state[("lnsq", par)] = m2

        emit_stats(0)
        for tau in range(TQ):
            mrow = rowA[0:1, 0:512]
            rrow = rowB[0:1, 0:512]
            e1 = K.rd("vector", [0], lambda vv, mr=mrow: vv.tensor_scalar(
                mr, pb[0][0:1, 0:512], 1.0 / D, None, op0=ALU.mult),
                extra=[state.get("ln_mrow_free")])
            e2 = K.rd("vector", [1], lambda vv: vv.tensor_scalar(
                rowC[:], pb[1][0:1, 0:512], 1.0 / D, LN_EPS, op0=ALU.mult, op1=ALU.add),
                extra=[e1])
            e3 = K.op("vector", lambda vv, mr=mrow, rr=rrow: vv.tensor_tensor(rr, mr, mr, op=ALU.mult),
                      waits=[e2, state.get("ln_rrow_free")])
            e4 = K.op("vector", lambda vv, rr=rrow: vv.tensor_tensor(rr, rowC[:], rr, op=ALU.subtract),
                      waits=[e3])
            e5 = K.op("scalar", lambda s, rr=rrow: s.activation(rr, rr, AF.Sqrt),
                      waits=[e4])
            e6 = K.op("vector", lambda vv, rr=rrow: vv.reciprocal(rr, rr), waits=[e5])
            if tau + 1 < TQ:
                emit_stats(tau + 1)
            mmm = K.mm([2], lambda t, mr=mrow: t.matmul(
                pb[2][:, 0:512], ones_sb[0:1, :], mr, start=True, stop=True), extra=[e4])
            state["ln_mrow_free"] = mmm
            mmr = K.mm([3], lambda t, rr=rrow: t.matmul(
                pb[3][:, 0:512], ones_sb[0:1, :], rr, start=True, stop=True), extra=[e6])
            for tc4 in range(4):
                mtr = K.mm([7], lambda t, tc4=tc4, rr=rrow: t.transpose(
                    pbig[1][:, 512:513], rr[0:1, ts(tc4, 128)], ident_sb[0:1, 0:1]),
                    extra=[e6])
                K.rd("scalar", [7], lambda s2, tc4=tc4: s2.copy(
                    rstdc_sb[:, tc4:tc4 + 1], pbig[1][:, 512:513]))
            state["ln_rrow_free"] = mtr
            for dc in range(DC):
                xt = xs[:, dc * T + tau * 512: (dc * T + tau * 512) + 512]
                xnt = xn_sb[:, dc * T + tau * 512: (dc * T + tau * 512) + 512]
                K.rd("vector", [2], lambda vv, a=xt: vv.tensor_tensor(a, a, pb[2][:, 0:512], op=ALU.subtract))
                ev_n = K.rd("vector", [3], lambda vv, a=xt, b=xnt: vv.tensor_tensor(
                    b, a, pb[3][:, 0:512], op=ALU.mult))
                K.op("sync", lambda e, b=xnt, dc=dc, tau=tau: e.dma_start(
                    out=xn_d[:, dc, ts(tau, 512)], in_=b), waits=[ev_n], dma=True)
            ev_norm = (v, v.n)
            for tc4 in range(4):
                g_ = tau * 4 + tc4
                for dc in range(DC):
                    K.mm([4], lambda t, dc=dc, g_=g_, tc4=tc4: t.matmul(
                        pbig[0][:, ts(tc4, 8)],
                        xs[:, dc * T + g_ * 128: dc * T + g_ * 128 + 128],
                        wg_bf[:, ts(dc, E)], start=(dc == 0), stop=(dc == DC - 1)),
                        extra=[ev_norm, ev_wg], first=(dc == 0), last=(dc == DC - 1))
                e1 = K.rd("vector", [4], lambda vv, tc4=tc4: vv.scalar_tensor_tensor(
                    ltm[:], pbig[0][:, ts(tc4, 8)], rstdc_sb[:, tc4:tc4 + 1], bg_rep[:],
                    op0=ALU.mult, op1=ALU.add),
                    extra=[ev_bgrep, state.get("ln_ltm_free"), (sc, sc.n)])
                e2 = K.op("vector", lambda vv: vv.reduce_max(mxcol[:], ltm[:], axis=AX.X), waits=[e1])
                K.wait_only("vector", [e2])
                e3 = K.op("vector", lambda vv: vv.tensor_scalar(
                    ind_tm[:], ltm[:], mxcol[:], None, op0=ALU.is_equal),
                    waits=[e2])
                K.op("vector", lambda vv, g_=g_: vv.tensor_copy(
                    ind_all[:, ts(g_, E)], ind_tm[:]))
                state["ln_ltm_free"] = e3
        ev_ind = (v, v.n)
        for tc in range(TC):
            K.mm([6], lambda t, tc=tc: t.matmul(
                pbig[1][0:1, ts(tc, E)], ones_sb[:, 0:1], ind_all[:, ts(tc, E)],
                start=True, stop=True),
                extra=[ev_ind] if tc == 0 else [], first=(tc == 0), last=(tc == TC - 1))
        K.rd("scalar", [6], lambda s: s.copy(cs_rows[:], pbig[1][0:1, 0:128]))
        K.op("vector", lambda vv: vv.memset(bases[0:1, 0:E], 0.0), waits=[(sc, sc.n)])
        for tc in range(1, TC):
            K.op("vector", lambda vv, tc=tc: vv.tensor_tensor(
                bases[0:1, ts(tc, E)], bases[0:1, ts(tc - 1, E)],
                cs_rows[0:1, ts(tc - 1, E)], op=ALU.add))
        ev_bases = (v, v.n)
        for tc in range(TC):
            cpar = tc % 2
            cum_sl = pbig[0][:, cpar * 512: cpar * 512 + E]
            cbank = [4 + cpar]
            K.mm(cbank, lambda t, tc=tc, cum_sl=cum_sl: t.matmul(
                cum_sl, lt_sb[:], ind_all[:, ts(tc, E)], start=True, stop=False),
                extra=[ev_bases, ev_consts], first=True, last=False)
            K.mm(cbank, lambda t, tc=tc, cum_sl=cum_sl: t.matmul(
                cum_sl, ones_sb[0:1, :], bases[0:1, ts(tc, E)], start=False, stop=True),
                first=False, last=True)
            K.rd("vector", cbank, lambda vv, cum_sl=cum_sl: vv.tensor_tensor(
                tmp8[:], cum_sl, ecm1_sb[:], op=ALU.add))
            K.op("vector", lambda vv, tc=tc: vv.tensor_tensor(
                tmp8[:], tmp8[:], ind_all[:, ts(tc, E)], op=ALU.mult))
            K.op("vector", lambda vv: vv.tensor_reduce(tcol[:], tmp8[:], AX.X, ALU.add))
            K.op("vector", lambda vv: vv.tensor_scalar(
                tcol[:], tcol[:], float(E * CAP - 1), None, op0=ALU.min))
            K.op("vector", lambda vv, tc=tc: vv.tensor_copy(tsl_sb[:, tc:tc + 1], tcol[:]))
        state["tsl"] = (v, v.n)
        if want_hm:
            prev = None
            for dc in range(DC):
                prev = K.op("vector", lambda vv, dc=dc: vv.tensor_reduce(
                    hm_cols[:, dc:dc + 1], xn_sb[:, ts(dc, T)], AX.X, ALU.add), waits=[(v, v.n)])
            K.wait_only("vector", [prev])
            state["hm"] = K.op("vector", lambda vv: vv.tensor_scalar(
                hm_bf[:], hm_cols[:], 1.0 / T, None, op0=ALU.mult))

    def phase_ffn(l, xnh, hact, w2buf, w1buf, maskrep, gelu):
        ev_memset = K.op("vector", lambda vv: vv.memset(sg[:], 0.0))
        a_evac = {}
        w1_free = [None] * 6
        w2_free = [None] * 6
        mask_free = None
        hps_free = [None, None]
        b1_free = [None, None]
        b2_free = [None, None]
        ev_b1 = [None, None]
        ev_b2 = [None, None]
        xnh_free = None
        ybanks_sl = [pb[0][:, 0:512], pb[1][:, 0:512], pb[2][:, 0:512], pb[3][:, 0:512],
                     pbig[0][:, 0:512], pbig[0][:, 512:1024],
                     pbig[1][:, 0:512], pbig[1][:, 512:1024]]
        ybank_ids = [[0], [1], [2], [3], [4], [5], [6], [7]]

        for th in range(TH):
            ev_xnh = K.op("sync", lambda e, th=th: e.dma_start(
                out=xnh[:].rearrange("p (dc t) -> p dc t", dc=DC),
                in_=xn_d[:, :, ts(th, 1024)]), waits=[xnh_free], dma=True)
            for e_ in range(E):
                bpar = e_ % 2
                if l == 1:
                    K.op("sync", lambda e, e_=e_, bp=bpar: e.dma_start(
                        out=b1c_sb[bp][:], in_=b1c[e_]), waits=[b1_free[bpar], b2_free[bpar]],
                        dma=True, dma_sem=f"d_b{bpar}")
                    mboth = K.op("sync", lambda e, e_=e_, bp=bpar: e.dma_start(
                        out=b2c_sb[bp][:], in_=b2c[e_]), waits=[],
                        dma=True, dma_sem=f"d_b{bpar}")
                    ev_b1[bpar] = ev_b2[bpar] = mboth
                ev_mrow = K.op("sync", lambda e, e_=e_, th=th: e.dma_start(
                    out=mrow_sb[:], in_=ind_d[e_:e_ + 1, ts(th, 1024)]),
                    waits=[state[("masks", l)], state.get("ffn_mrow_free")], dma=True)
                for seg in range(16):
                    par = seg % 4
                    ev_w1 = K.op("gpsimd", lambda g, e_=e_, seg=seg, par=par: g.dma_start(
                        out=w1buf[par][:].rearrange("p (dc h) -> p dc h", dc=DC),
                        in_=W1_l[l][e_, :, ts(seg, 256)].rearrange("(dc p) h -> p dc h", p=128)),
                        waits=[w1_free[par]], dma=True, dma_sem=f"d_w1{par}")
                    for hc2 in range(2):
                        hc = seg * 2 + hc2
                        for sub in range(2):
                            hb = sub
                            for dc in range(DC):
                                mmk = K.mm([hb], lambda t, par=par, dc=dc, hc2=hc2, sub=sub, hb=hb: t.matmul(
                                    pb[hb][:, 0:512],
                                    w1buf[par][:, dc * 256 + hc2 * 128: dc * 256 + hc2 * 128 + 128],
                                    xnh[:, dc * 1024 + sub * 512: dc * 1024 + sub * 512 + 512],
                                    start=(dc == 0), stop=(dc == DC - 1)),
                                    extra=([ev_w1, ev_xnh] if dc == 0 else []),
                                    first=(dc == 0), last=(dc == DC - 1))
                            if hc2 == 1 and sub == 1:
                                w1_free[par] = mmk
                            hsl = hact[:, hc * 1024 + sub * 512: hc * 1024 + sub * 512 + 512]
                            if gelu:
                                ev = K.rd("scalar", [hb], lambda s, h_=hsl, hb=hb: s.activation(
                                    h_, pb[hb][:, 0:512], AF.Gelu_apprx_tanh))
                            else:
                                ev = K.rd("scalar", [hb], lambda s, h_=hsl, hb=hb, hc=hc, bp=bpar: s.activation(
                                    h_, pb[hb][:, 0:512], AF.Relu, bias=b1c_sb[bp][:, hc:hc + 1]),
                                    extra=[ev_b1[bpar]])
                            a_evac[(hc, sub)] = ev
                xnh_free = (pe, pe.n)
                if l == 1:
                    b1_free[bpar] = (sc, sc.n)
                for q_ in range(2):
                    bq = 2 + q_
                    K.mm([bq], lambda t, q_=q_, bq=bq: t.matmul(
                        pb[bq][:, 0:512], ones_sb[0:1, :], mrow_sb[0:1, ts(q_, 512)],
                        start=True, stop=True), extra=[ev_mrow])
                state["ffn_mrow_free"] = (pe, pe.n)
                K.rd("scalar", [2], lambda s: s.copy(maskrep[:, 0:512], pb[2][:, 0:512]),
                     extra=[mask_free])
                ev_mask = K.rd("scalar", [3], lambda s: s.copy(maskrep[:, 512:1024],
                                                               pb[3][:, 0:512]),
                               extra=[mask_free])
                for dh in range(2):
                    for hc in range(HC):
                        w2p = hc % 6
                        ev_w2 = K.op("gpsimd", lambda g, e_=e_, hc=hc, dh=dh, w2p=w2p: g.dma_start(
                            out=w2buf[w2p][:], in_=W2_l[l][e_, ts(hc, 128), dh * 512:dh * 512 + 512]),
                            waits=[w2_free[w2p]], dma=True, dma_sem=f"d_w2{w2p}")
                        for sub in range(2):
                            for d4 in range(4):
                                yb = sub * 4 + d4
                                mmk = K.mm(ybank_ids[yb], lambda t, hc=hc, sub=sub, d4=d4, w2p=w2p, yb=yb: t.matmul(
                                    ybanks_sl[yb],
                                    w2buf[w2p][:, ts(d4, 128)],
                                    hact[:, hc * 1024 + sub * 512: hc * 1024 + sub * 512 + 512],
                                    start=(hc == 0), stop=(hc == HC - 1)),
                                    extra=([ev_w2, a_evac[(hc, sub)]] if d4 == 0 else []),
                                    first=(hc == 0), last=(hc == HC - 1),
                                    force_inc=(sub == 1 and d4 == 3))
                        w2_free[w2p] = mmk
                    for sub in range(2):
                        for d4 in range(4):
                            yb = sub * 4 + d4
                            dout = dh * 4 + d4
                            sgs = sg[:, dout * T + th * 1024 + sub * 512:
                                     dout * T + th * 1024 + sub * 512 + 512]
                            msl = maskrep[:, ts(sub, 512)]
                            if l == 0:
                                K.rd("vector", ybank_ids[yb], lambda vv, yb=yb, msl=msl: vv.scalar_tensor_tensor(
                                    ffn_tmp[:], ybanks_sl[yb], 0.0, msl, op0=ALU.add, op1=ALU.mult),
                                    extra=[ev_mask, ev_memset])
                            else:
                                K.rd("vector", ybank_ids[yb], lambda vv, yb=yb, msl=msl, dout=dout, bp=bpar: vv.scalar_tensor_tensor(
                                    ffn_tmp[:], ybanks_sl[yb], b2c_sb[bp][:, dout:dout + 1], msl,
                                    op0=ALU.add, op1=ALU.mult),
                                    extra=[ev_mask, ev_memset, ev_b2[bpar]])
                            K.op("vector", lambda vv, sgs=sgs: vv.tensor_tensor(
                                sgs, sgs, ffn_tmp[:], op=ALU.add))
                mask_free = (v, v.n)
                if l == 1:
                    b2_free[bpar] = (v, v.n)
        state[("ffn_v", l)] = (v, v.n)

    def phase_ffn_sparse(l, xgfm, hact_sp, w1buf, w2buf, ytm_sb, xntm, b2row, gelu,
                         precast=False):
        W1_ap = W1_l[l]
        W2_ap = W2_l[l]
        xntm_free = [None, None]
        ev_scat = None
        for tc in range(TC):
            par = tc % 2
            tr = None
            for dc in range(DC):
                tr = K.op("sync", lambda e, dc=dc, tc=tc, par=par: e.dma_start_transpose(
                    out=xntm[par][:, ts(dc, 128)], in_=xn_d[:, dc, ts(tc, 128)]),
                    waits=[xntm_free[par]], dma=True, dma_sem=f"d_xt{par}")
            ev_scat_p = K.op("gpsimd", lambda g, tc=tc, par=par: g.indirect_dma_start(
                out=xg_d, out_offset=IndirectOffsetOnAxis(tsl_sb[:, tc:tc + 1], 0),
                in_=xntm[par][:], in_offset=None),
                waits=[tr, state["tsl"]], dma=True, dma_sem=f"d_scat{par}")
            xntm_free[par] = ev_scat_p
            ev_scat = None
        a_evac = {}
        w1_free = [None] * 6
        w2_free = [None] * 6
        xg_free = [None, None]
        ytm_free = [None, None]
        b1_free = [None, None]
        b2_free = [None, None]
        ysl = [(p4[:, 0:512], [2]), (p4[:, 512:1024], [3]),
               (p4[:, 1024:1536], [4]), (p4[:, 1536:2048], [5]),
               (pbig[1][:, 0:512], [6]), (pbig[1][:, 512:1024], [7])]
        NSC = CAP // 128
        last_store = None
        xg_marks = [None] * E
        b1_marks = [None] * E
        b2_marks = [None] * E

        def emit_loads(e_):
            epar = e_ % 2
            m = None
            for dc in range(DC):
                m = K.op("sync", lambda e2, e_=e_, dc=dc, epar=epar: e2.dma_start_transpose(
                    out=xgfm[epar][:, ts(dc, CAP)], in_=xg_d[e_ * CAP:(e_ + 1) * CAP, ts(dc, 128)]),
                    waits=[xntm_free[0], xntm_free[1], xg_free[epar]],
                    dma=True, dma_sem=f"d_xg{epar}")
            xg_marks[e_] = m
            if l == 1:
                b1_marks[e_] = K.op("sync", lambda e, e_=e_, bp=epar: e.dma_start(
                    out=b1c_sb[bp][:], in_=b1c[e_]), waits=[b1_free[epar]],
                    dma=True, dma_sem=f"d_b{epar}")
                b2_marks[e_] = K.op("sync", lambda e, e_=e_, bp=epar: e.dma_start(
                    out=b2row[bp][:], in_=b2r[e_]), waits=[b2_free[epar]],
                    dma=True, dma_sem=f"d_b2{epar}")

        emit_loads(0)
        for e_ in range(E):
            epar = e_ % 2
            xg_m = xg_marks[e_]
            ev_b1 = b1_marks[e_]
            ev_b2 = b2_marks[e_]
            for seg in range(16):
                par4 = seg % 6
                w1src = (w1bf_d if precast else W1_ap)
                ev_w1 = K.op("gpsimd", lambda g, e_=e_, seg=seg, par4=par4, w1src=w1src: g.dma_start(
                    out=w1buf[par4][:].rearrange("p (dc h) -> p dc h", dc=DC),
                    in_=w1src[e_, :, ts(seg, 256)].rearrange("(dc p) h -> p dc h", p=128)),
                    waits=[w1_free[par4]] + ([state.get("precast_done"), state.get("precast_done2")] if precast else []),
                    dma=True, dma_sem=f"d_w1{par4}")
                for hc2 in range(2):
                    hc = seg * 2 + hc2
                    hb = hc % 2
                    for dc in range(DC):
                        mmk = K.mm([hb], lambda t, par4=par4, dc=dc, hc2=hc2, hb=hb, epar=epar: t.matmul(
                            pb[hb][:, 0:CAP],
                            w1buf[par4][:, dc * 256 + hc2 * 128: dc * 256 + hc2 * 128 + 128],
                            xgfm[epar][:, dc * CAP: dc * CAP + CAP],
                            start=(dc == 0), stop=(dc == DC - 1)),
                            extra=([ev_w1, xg_m] if dc == 0 else []),
                            first=(dc == 0), last=(dc == DC - 1))
                    if hc2 == 1:
                        w1_free[par4] = mmk
                    hsl = hact_sp[:, hc * CAP: hc * CAP + CAP]
                    if gelu:
                        a_evac[hc] = K.rd("scalar", [hb], lambda s, h_=hsl, hb=hb: s.activation(
                            h_, pb[hb][:, 0:CAP], AF.Gelu_apprx_tanh))
                    else:
                        a_evac[hc] = K.rd("scalar", [hb], lambda s, h_=hsl, hb=hb, hc=hc, bp=epar: s.activation(
                            h_, pb[hb][:, 0:CAP], AF.Relu, bias=b1c_sb[bp][:, hc:hc + 1]),
                            extra=[ev_b1])
            xg_free[epar] = (pe, pe.n)
            if l == 1:
                b1_free[epar] = (sc, sc.n)
            if e_ + 1 < E:
                emit_loads(e_ + 1)
            for hc in range(HC):
                w2p = hc % 6
                w2src = (w2bf_d if precast else W2_ap)
                ev_w2 = K.op("gpsimd", lambda g, e_=e_, hc=hc, w2p=w2p, w2src=w2src: g.dma_start(
                    out=w2buf[w2p][:], in_=w2src[e_, ts(hc, 128), :]),
                    waits=[w2_free[w2p]] + ([state.get("precast_done"), state.get("precast_done2")] if precast else []),
                    dma=True, dma_sem=f"d_w2{w2p}")
                for sc4 in range(NSC):
                    for half in range(2):
                        yi = sc4 * 2 + half
                        mmk = K.mm(ysl[yi][1], lambda t, hc=hc, sc4=sc4, half=half, w2p=w2p, yi=yi: t.matmul(
                            ysl[yi][0],
                            hact_sp[:, hc * CAP + sc4 * 128: hc * CAP + sc4 * 128 + 128],
                            w2buf[w2p][:, half * 512: half * 512 + 512],
                            start=(hc == 0), stop=(hc == HC - 1 and l == 0)),
                            extra=([ev_w2, a_evac[hc]] if (sc4 == 0 and half == 0) else [ev_w2] if half == 0 else []),
                            first=(hc == 0), last=(hc == HC - 1 and l == 0),
                            force_inc=(sc4 == NSC - 1 and half == 1))
                w2_free[w2p] = mmk
            if l == 1:
                for sc4 in range(NSC):
                    for half in range(2):
                        yi = sc4 * 2 + half
                        K.mm(ysl[yi][1], lambda t, sc4=sc4, half=half, yi=yi, bp=epar: t.matmul(
                            ysl[yi][0], ones_sb[0:1, :], b2row[bp][0:1, half * 512: half * 512 + 512],
                            start=False, stop=True),
                            extra=[ev_b2] if yi == 0 else [], first=False, last=True)
            for sc4 in range(NSC):
                for half in range(2):
                    yi = sc4 * 2 + half
                    K.rd("vector", ysl[yi][1], lambda vv, sc4=sc4, half=half, yi=yi, epar=epar: vv.tensor_copy(
                        ytm_sb[epar][:, sc4 * 1024 + half * 512: sc4 * 1024 + half * 512 + 512],
                        ysl[yi][0]), extra=[ytm_free[epar]])
            ev_yv = (v, v.n)
            last_store = K.op("sync", lambda e2, e_=e_, epar=epar: e2.dma_start(
                out=ytm_d[e_ * CAP:(e_ + 1) * CAP, :].rearrange("(sc p) d -> p sc d", p=128),
                in_=ytm_sb[epar][:].rearrange("p (sc d) -> p sc d", sc=NSC)),
                waits=[ev_yv], dma=True, dma_sem=f"d_yt{epar}")
            ytm_free[epar] = last_store
            if l == 1:
                b2_free[epar] = (v, v.n)
        ev_gat = None
        gst_free = [None, None]
        for tc in range(TC):
            gpar = tc % 2
            gm = K.op("gpsimd", lambda g, tc=tc, gpar=gpar: g.indirect_dma_start(
                out=xntm[gpar][:], out_offset=None,
                in_=ytm_d, in_offset=IndirectOffsetOnAxis(tsl_sb[:, tc:tc + 1], 0)),
                waits=[ytm_free[0], ytm_free[1], gst_free[gpar]],
                dma=True, dma_sem=f"d_gat{gpar}")
            ev_gat = K.op("sync", lambda e2, tc=tc, gpar=gpar: e2.dma_start(
                out=gtm_d[tc * 128:(tc + 1) * 128, :], in_=xntm[gpar][:]),
                waits=[gm], dma=True, dma_sem=f"d_gst{gpar}")
            gst_free[gpar] = ev_gat
        state["gst0"], state["gst1"] = gst_free[0], gst_free[1]
        ev_sg = None
        for dc in range(DC):
            ev_sg = K.op("sync", lambda e2, dc=dc: e2.dma_start_transpose(
                out=sg[:, dc * T:(dc + 1) * T], in_=gtm_d[:, ts(dc, 128)]),
                waits=[ev_gat, state.get("gst0"), state.get("gst1")], dma=True, dma_sem="d_sg")
        state[("ffn_v", l)] = ev_sg

    def phase_attn(xn_sb, av_sb, wpT_buf, attnT2, Qb2, Kb2, Vb2, qkv_bf, rrep2,
                   nmcols, rrow, x0t):
        ev_xn = K.op("sync", lambda e: e.dma_start(
            out=xn_sb[:].rearrange("p (dc t) -> p dc t", dc=DC), in_=xn_d[:, :, :]), dma=True)
        ev_g = None
        for i in range(DC * 4):
            ev_g = K.op("scalar", lambda s, i=i: s.activation(
                sg[:, ts(i, 512)], sg[:, ts(i, 512)], AF.Silu),
                waits=[state[("ffn_v", 0)]])
        for hd in range(2):
            for tc in range(TC):
                K.op("vector", lambda vv, hd=hd, tc=tc: vv.memset(
                    Vb2[hd][:, tc * 65 + 64: tc * 65 + 65], 1.0))
        ev_vones = (v, v.n)

        qkv_w = [None, None]
        qkv_wfree = [None, None]
        kones_free = [None, None]
        x0_free = [None, None]
        prev_av_tt = [None]
        tr_mark = [None, None]
        for hp in range(NH // 2):
            heads = (2 * hp, 2 * hp + 1)
            ev_qk = [None, None]
            ev_kones = [None, None]
            ev_v = [None, None]
            row64 = [[None] * TQ, [None] * TQ]
            for hd in range(2):
                h = heads[hd]
                for kk in range(3):
                    qkv_w[hd] = K.op("gpsimd", lambda g, h=h, hd=hd, kk=kk: g.dma_start(
                        out=qkv_bf[hd][:, ts(kk, 512)].rearrange("p (dc hd2) -> p dc hd2", dc=DC),
                        in_=Wqkv[kk * 16 + h].rearrange("(dc p) hd2 -> p dc hd2", p=128)),
                        waits=[qkv_wfree[hd]], dma=True, dma_sem=f"d_qkv{hd}")
            for hd in range(2):
                h = heads[hd]
                Qb, Kb, Vb = Qb2[hd], Kb2[hd], Vb2[hd]
                for which, buf, bi in (("q", Qb, h), ("k", Kb, 16 + h)):
                    koff = 0 if which == "q" else 1
                    for tau in range(TQ):
                        qpar = tau % 2
                        for dc in range(DC):
                            K.mm([qpar], lambda t, dc=dc, tau=tau, koff=koff, qpar=qpar, hd=hd: t.matmul(
                                pb[qpar][0:HD, 0:512],
                                qkv_bf[hd][:, koff * 512 + dc * HD: koff * 512 + dc * HD + HD],
                                xn_sb[:, dc * T + tau * 512: dc * T + tau * 512 + 512],
                                start=(dc == 0), stop=(dc == DC - 1)),
                                extra=([qkv_w[hd], ev_xn] if dc == 0 else []),
                                first=(dc == 0), last=(dc == DC - 1))
                        K.rd("vector", [qpar], lambda vv, buf=buf, tau=tau, bi=bi, qpar=qpar: vv.tensor_scalar(
                            buf[0:HD, ts(tau, 512)], pb[qpar][0:HD, 0:512],
                            bqkvT_sb[:, bi:bi + 1], None, op0=ALU.add),
                            extra=[ev_consts] + ([kones_free[hd]] if which == "k" else []))
                ev_qk[hd] = (v, v.n)
                ev_kones[hd] = K.op("vector", lambda vv, Kb=Kb: vv.memset(Kb[64:65, :], 1.0),
                                    waits=[ev_qk[hd]])
                for tc in range(TC):
                    vpar = tc % 2
                    for dc in range(DC):
                        K.mm([vpar], lambda t, dc=dc, tc=tc, vpar=vpar, hd=hd: t.matmul(
                            pb[vpar][:, 0:HD],
                            xn_sb[:, dc * T + tc * 128: dc * T + tc * 128 + 128],
                            qkv_bf[hd][:, 2 * 512 + dc * HD: 2 * 512 + dc * HD + HD],
                            start=(dc == 0), stop=(dc == DC - 1)),
                            extra=([qkv_w[hd], ev_xn] if dc == 0 else []),
                            first=(dc == 0), last=(dc == DC - 1))
                    K.rd("vector", [vpar], lambda vv, tc=tc, vpar=vpar, Vb=Vb: vv.tensor_copy(
                        Vb[:, tc * 65: tc * 65 + 64], pb[vpar][:, 0:HD]), extra=[ev_vones])
                ev_v[hd] = (v, v.n)
                qkv_wfree[hd] = (pe, pe.n)

            def emit_p1_scores(hd, qg, sp):
                Qb, Kb = Qb2[hd], Kb2[hd]
                for qt4 in range(4):
                    qt = qg * 4 + qt4
                    for half, bks in ((0, [2, 3]), (1, [4, 5])):
                        for kg in range(2):
                            K.mm(bks, lambda t, qt=qt, half=half, kg=kg, Qb=Qb, Kb=Kb: t.matmul(
                                p4[:, ts(half * 2 + kg, 512)], Qb[0:HD, ts(qt, 128)],
                                Kb[0:HD, ts(half * 2 + kg, 512)], start=True, stop=True),
                                extra=([ev_qk[hd]] if kg == 0 else []),
                                first=(kg == 0), last=(kg == 1))
                        K.rd("vector", bks, lambda vv, half=half: vv.reduce_max(
                            nmcols[:, half:half + 1], p4[:, half * 1024:half * 1024 + 1024],
                            axis=AX.X, negate=True))
                    col = 2 + (sp % 2) * 4 + qt4
                    K.op("vector", lambda vv, col=col: vv.tensor_reduce(
                        nmcols[:, col:col + 1], nmcols[:, 0:2], AX.X, ALU.min),
                        waits=[tr_mark[sp % 2]])

            def emit_p1_finish(hd, qg, sp):
                Qb = Qb2[hd]
                for qt4 in range(4):
                    qt = qg * 4 + qt4
                    col = 2 + (sp % 2) * 4 + qt4
                    mt = K.mm([7], lambda t, col=col: t.transpose(
                        pbig[1][0:1, 512:640], nmcols[:, col:col + 1], ident_sb[:]),
                        extra=[(v, v.n)])
                    row64[hd][qg] = K.rd("scalar", [7], lambda s, qt=qt, Qb=Qb: s.copy(
                        Qb[64:65, ts(qt, 128)], pbig[1][0:1, 512:640]), extra=[ev_kones[hd]])
                tr_mark[sp % 2] = (pe, pe.n)

            emit_p1_scores(0, 0, 0)
            emit_p1_finish(0, 0, 0)
            for step in range(2 * TQ):
                hd, qg = step % 2, step // 2
                h = heads[hd]
                Qb, Kb, Vb = Qb2[hd], Kb2[hd], Vb2[hd]
                attnT = attnT2[hd]
                rrep_sb = rrep2[hd]
                if step + 1 < 2 * TQ:
                    emit_p1_scores((step + 1) % 2, (step + 1) // 2, step + 1)
                exp_marks = []
                for kc in range(TC):
                    spar = kc % 2
                    K.mm([spar], lambda t, kc=kc, qg=qg, spar=spar, Qb=Qb, Kb=Kb: t.matmul(
                        pb[spar][:, 0:512], Kb[0:65, ts(kc, 128)], Qb[0:65, ts(qg, 512)],
                        start=True, stop=True),
                        extra=([row64[hd][qg], ev_kones[hd]] if kc == 0 else []))
                    exp_marks.append(K.rd("scalar", [spar], lambda s, kc=kc, spar=spar, attnT=attnT: s.activation(
                        attnT[:, ts(kc, 512)], pb[spar][:, 0:512], AF.Exp, scale=8.0)))
                for kc in range(TC):
                    K.mm([6], lambda t, kc=kc, attnT=attnT, Vb=Vb: t.matmul(
                        pbig[1][0:65, 0:512], Vb[:, kc * 65: kc * 65 + 65], attnT[:, ts(kc, 512)],
                        start=(kc == 0), stop=(kc == TC - 1)),
                        extra=[exp_marks[kc]] + ([ev_v[hd]] if kc == 0 else []),
                        first=(kc == 0), last=(kc == TC - 1))
                e1 = K.rd("vector", [6], lambda vv: vv.reciprocal(rrow[:], pbig[1][64:65, 0:512]))
                mmr = K.mm([7], lambda t: t.matmul(pbig[1][0:64, 512:1024], ones_sb[0:1, 0:HD],
                                                   rrow[:], start=True, stop=True), extra=[e1])
                ev_rr = K.rd("vector", [7], lambda vv, rrep_sb=rrep_sb: vv.tensor_copy(
                    rrep_sb[:], pbig[1][0:64, 512:1024]), extra=[prev_av_tt[0]])
                cc, ph = h // 2, h % 2
                bv_col = bqkvT_sb[:, 32 + h: 33 + h]
                K.rd("vector", [6], lambda vv, rrep_sb=rrep_sb: vv.tensor_tensor(
                    ffn_tmp[0:HD, :], pbig[1][0:HD, 0:512], rrep_sb[:], op=ALU.mult),
                    extra=[ev_rr])
                prev_av_tt[0] = K.op("vector", lambda vv, cc=cc, ph=ph, qg=qg, bv=bv_col: vv.tensor_scalar(
                    av_sb[ph * HD: ph * HD + HD, cc * T + qg * 512: cc * T + qg * 512 + 512],
                    ffn_tmp[0:HD, :], bv, None, op0=ALU.add))
                if step + 1 < 2 * TQ:
                    emit_p1_finish((step + 1) % 2, (step + 1) // 2, step + 1)
            kones_free[0] = (pe, pe.n)
            kones_free[1] = (pe, pe.n)
        ev_av = (v, v.n)
        x0r = x0.rearrange("(dc p) t -> p dc t", p=128)
        wp_free = [None, None]
        for pc in range(DC):
            wpar = pc % 2
            ev_wp = K.op("gpsimd", lambda g, pc=pc, wpar=wpar: g.dma_start(
                out=wpT_buf[wpar][:].rearrange("p (cc q) -> p cc q", cc=DC),
                in_=WpT[:, ts(pc, 128)].rearrange("(cc p) q -> p cc q", p=128)),
                waits=[wp_free[wpar]], dma=True, dma_sem=f"d_wp{wpar}")
            for tau in range(TQ):
                spar = (pc * TQ + tau) % 2
                ev_x0 = K.op("sync", lambda e, pc=pc, tau=tau, spar=spar: e.dma_start(
                    out=x0t[spar][:], in_=x0r[:, pc, ts(tau, 512)]),
                    waits=[x0_free[spar]], dma=True, dma_sem=f"d_x{spar}")
                for cc in range(DC):
                    K.mm([spar], lambda t, cc=cc, tau=tau, spar=spar, wpar=wpar: t.matmul(
                        pb[spar][:, 0:512],
                        wpT_buf[wpar][:, ts(cc, 128)],
                        av_sb[:, cc * T + tau * 512: cc * T + tau * 512 + 512],
                        start=(cc == 0), stop=(cc == DC - 1)),
                        extra=([ev_av, ev_wp] if cc == 0 else []),
                        first=(cc == 0), last=(cc == DC - 1))
                K.rd("vector", [spar], lambda vv, pc=pc, tau=tau, spar=spar: vv.scalar_tensor_tensor(
                    ffn_tmp[:], pb[spar][:, 0:512], bp_sb[:, pc:pc + 1],
                    sg[:, pc * T + tau * 512: pc * T + tau * 512 + 512],
                    op0=ALU.add, op1=ALU.mult), extra=[ev_g])
                ev_xt = K.op("vector", lambda vv, spar=spar: vv.tensor_tensor(
                    x0t[spar][:], x0t[spar][:], ffn_tmp[:], op=ALU.add), waits=[ev_x0])
                x0_free[spar] = K.op("sync", lambda e, pc=pc, tau=tau, spar=spar: e.dma_start(
                    out=x_d[:, pc, ts(tau, 512)], in_=x0t[spar][:]), waits=[ev_xt],
                    dma=True, dma_sem=f"d_xs{spar}")
            wp_free[wpar] = (pe, pe.n)

    def phase_l1small(xn_sb, pre, xr_bf, oow_bf, vpow_bf, opw_buf, vpiw_bf,
                      sq_t, gsil_t, s1_t, x1t, vqs):
        (params_sb, opb_sb, om_u, om_ut, om, om_n, om2, om3, Gt, G_bf, vq_row, vq_sq,
         vq_n, cb_sq, cb_n, cb_sc, vqn_rep, vq_gc, vq_ind, code_row, code_col,
         omix_col, vqmix_col, vqc_cols) = vqs
        ev_xn = K.op("sync", lambda e: e.dma_start(
            out=xn_sb[:].rearrange("p (dc t) -> p dc t", dc=DC), in_=xn_d[:, :, :]), dma=True)
        ev_oow = K.op("gpsimd", lambda g: g.dma_start(out=oow_bf[:], in_=oowT), dma=True)
        ev_vpow = K.op("gpsimd", lambda g: g.dma_start(out=vpow_bf[:], in_=vpowT), dma=True)
        ev_vpiw = K.op("sync", lambda e: e.dma_start(
            out=vpiw_bf[:].rearrange("p (dc c) -> p dc c", dc=DC),
            in_=vpiwT.rearrange("(dc p) c -> p dc c", p=128)), dma=True)
        ev_hm = state["hm"]
        ev_opb = K.op("sync", lambda e: e.dma_start(out=opb_sb[:], in_=opb), dma=True)
        opw_free = [None, None, None]
        for dcb in range(DC):
            par = dcb % 3
            ev_opw = K.op("gpsimd", lambda g, dcb=dcb, par=par: g.dma_start(
                out=opw_buf[par][:], in_=opwT[ts(dcb, 128), :]), waits=[opw_free[par]],
                dma=True, dma_sem=f"d_opw{par}")
            for seg_ in range(4):
                mk = K.mm([seg_], lambda t, dcb=dcb, seg_=seg_, par=par: t.matmul(
                    pb[seg_][0:1, 0:504], hm_bf[:, dcb:dcb + 1], opw_buf[par][:, ts(seg_, 504)],
                    start=(dcb == 0), stop=(dcb == DC - 1)),
                    extra=[ev_opw, ev_hm], first=(dcb == 0), last=True)
            opw_free[par] = mk
        for seg_ in range(4):
            K.rd("scalar", [seg_], lambda s, seg_=seg_: s.copy(
                params_sb[0:1, ts(seg_, 504)], pb[seg_][0:1, 0:504]))
        ev_p = K.op("vector", lambda vv: vv.tensor_tensor(
            params_sb[:], params_sb[:], opb_sb[:], op=ALU.add),
            waits=[(sc, sc.n), ev_opb])
        ev_mz = K.op("vector", lambda vv: vv.memset(om_u[:], 0.0))
        off = 0
        ev_tri = None
        for r in range(63):
            ln_ = 63 - r
            ev_tri = K.op("sync", lambda e, r=r, off=off, ln_=ln_: e.dma_start(
                out=om_u[r:r + 1, r + 1:64], in_=params_sb[0:1, off:off + ln_]),
                waits=[ev_p, ev_mz], dma=True)
            off += ln_
        mt = K.mm([4], lambda t: t.transpose(pbig[0][0:64, 0:64], om_u[:], ident_sb[0:64, 0:64]),
                  extra=[ev_tri])
        ev_ut = K.rd("scalar", [4], lambda s: s.copy(om_ut[:], pbig[0][0:64, 0:64]))
        e_om = K.op("vector", lambda vv: vv.tensor_tensor(om[:], om_u[:], om_ut[:], op=ALU.subtract),
                    waits=[ev_ut, ev_tri])
        e_nom = K.op("vector", lambda vv: vv.tensor_tensor(om_n[:], om_ut[:], om_u[:], op=ALU.subtract))
        m2 = K.mm([4], lambda t: t.matmul(pbig[0][0:64, 64:128], om_n[:], om[:], start=True, stop=True),
                  extra=[e_nom])
        ev_o2 = K.rd("scalar", [4], lambda s: s.copy(om2[:], pbig[0][0:64, 64:128]))
        m3 = K.mm([4], lambda t: t.matmul(pbig[0][0:64, 128:192], om2[:], om[:], start=True, stop=True),
                  extra=[ev_o2])
        ev_o3 = K.rd("scalar", [4], lambda s: s.copy(om3[:], pbig[0][0:64, 128:192]))
        K.op("vector", lambda vv: vv.tensor_tensor(Gt[:], om[:], ident_sb[0:64, 0:64], op=ALU.add),
             waits=[ev_o3])
        K.op("vector", lambda vv: vv.scalar_tensor_tensor(Gt[:], om2[:], 0.5, Gt[:],
                                                          op0=ALU.mult, op1=ALU.add))
        K.op("vector", lambda vv: vv.scalar_tensor_tensor(Gt[:], om3[:], 1.0 / 6.0, Gt[:],
                                                          op0=ALU.mult, op1=ALU.add))
        ev_G = K.op("vector", lambda vv: vv.tensor_copy(G_bf[:], Gt[:]))
        ev_xr = None
        for tau in range(TQ):
            mm_ = K.mm([6, 7], lambda t, tau=tau: t.matmul(
                pbig[1][0:64, 0:512], G_bf[:], xn_sb[0:64, ts(tau, 512)], start=True, stop=True),
                extra=[ev_G, ev_xn])
            ev_xr = K.rd("scalar", [6, 7], lambda s, tau=tau: s.copy(xr_bf[0:64, ts(tau, 512)],
                                                                     pbig[1][0:64, 0:512]))
        for dcb in range(DC):
            K.mm([5], lambda t, dcb=dcb: t.matmul(
                pbig[0][0:1, 512:576], hm_cols[:, dcb:dcb + 1], vpiw_bf[:, ts(dcb, HD)],
                start=(dcb == 0), stop=(dcb == DC - 1)),
                extra=[ev_vpiw, ev_hm], first=(dcb == 0), last=(dcb == DC - 1))
        ev_vq0 = K.rd("vector", [5], lambda vv: vv.scalar_tensor_tensor(
            vq_row[:], pbig[0][0:1, 512:576], 1.0 / T, vpib_sb[:],
            op0=ALU.mult, op1=ALU.add))
        ev_sq = K.op("scalar", lambda s: s.activation(vq_sq[:], vq_row[:], AF.Square),
                     waits=[ev_vq0])
        e1 = K.op("vector", lambda vv: vv.tensor_reduce(vq_n[0:1, 0:1], vq_sq[:], AX.X, ALU.add),
                  waits=[ev_sq])
        e2 = K.op("scalar", lambda s: s.activation(vq_n[0:1, 1:2], vq_n[0:1, 0:1], AF.Sqrt),
                  waits=[e1])
        K.op("vector", lambda vv: vv.tensor_scalar(vq_n[0:1, 2:3], vq_n[0:1, 1:2], 1e-12, None, op0=ALU.max),
             waits=[e2])
        K.op("vector", lambda vv: vv.reciprocal(vq_n[0:1, 3:4], vq_n[0:1, 2:3]))
        ev_vqn = K.op("vector", lambda vv: vv.tensor_scalar(vq_row[:], vq_row[:], vq_n[0:1, 3:4], None, op0=ALU.mult))
        ev_cbsq = K.op("scalar", lambda s: s.activation(cb_sq[:], cb_sb[:], AF.Square),
                       waits=[ev_consts])
        K.op("vector", lambda vv: vv.tensor_reduce(cb_n[:, 0:1], cb_sq[:, 0:64], AX.X, ALU.add),
             waits=[ev_cbsq])
        e3 = K.op("vector", lambda vv: vv.tensor_reduce(cb_n[:, 1:2], cb_sq[:, 64:128], AX.X, ALU.add))
        e4 = K.op("scalar", lambda s: s.activation(cb_n[:, 2:4], cb_n[:, 0:2], AF.Sqrt),
                  waits=[e3])
        K.op("vector", lambda vv: vv.tensor_scalar(cb_n[:, 2:4], cb_n[:, 2:4], 1e-12, None, op0=ALU.max),
             waits=[e4])
        ev_rcb = K.op("vector", lambda vv: vv.reciprocal(cb_n[:, 4:6], cb_n[:, 2:4]))
        mmn = K.mm([5], lambda t: t.matmul(pbig[0][:, 576:640], ones_sb[0:1, :], vq_row[:],
                                           start=True, stop=True), extra=[ev_vqn])
        ev_vqr = K.rd("scalar", [5], lambda s: s.copy(vqn_rep[:], pbig[0][:, 576:640]))
        K.op("vector", lambda vv: vv.tensor_tensor(cb_sc[:, 0:64], cb_sb[:, 0:64], vqn_rep[:], op=ALU.mult),
             waits=[ev_vqr])
        K.op("vector", lambda vv: vv.tensor_tensor(cb_sc[:, 64:128], cb_sb[:, 64:128], vqn_rep[:], op=ALU.mult))
        K.op("vector", lambda vv: vv.tensor_reduce(cb_n[:, 6:7], cb_sc[:, 0:64], AX.X, ALU.add))
        K.op("vector", lambda vv: vv.tensor_reduce(cb_n[:, 7:8], cb_sc[:, 64:128], AX.X, ALU.add))
        ev_scr = K.op("vector", lambda vv: vv.tensor_tensor(cb_n[:, 6:8], cb_n[:, 6:8], cb_n[:, 4:6], op=ALU.mult),
                      waits=[ev_rcb])
        ev_m1 = K.op("vector", lambda vv: vv.tensor_reduce(cb_n[:, 8:9], cb_n[:, 6:8], AX.X, ALU.max))
        mtx = K.mm([4], lambda t: t.transpose(pbig[0][0:1, 0:128], cb_n[:, 8:9], ident_sb[:]),
                   extra=[ev_m1])
        ev_mrow = K.rd("scalar", [4], lambda s: s.copy(vq_n[0:1, 8:136], pbig[0][0:1, 0:128]))
        ev_gmax = K.op("vector", lambda vv: vv.tensor_reduce(vq_n[0:1, 4:5], vq_n[0:1, 8:136], AX.X, ALU.max),
                       waits=[ev_mrow])
        mmg = K.mm([4], lambda t: t.matmul(pbig[0][:, 128:129], ones_sb[0:1, :], vq_n[0:1, 4:5],
                                           start=True, stop=True), extra=[ev_gmax])
        ev_gcol = K.rd("scalar", [4], lambda s: s.copy(vq_gc[:], pbig[0][:, 128:129]))
        ev_ind2 = K.op("vector", lambda vv: vv.tensor_scalar(vq_ind[:], cb_n[:, 6:8], vq_gc[:], None, op0=ALU.is_equal),
                       waits=[ev_gcol, ev_scr])
        for j in range(2):
            K.mm([4], lambda t, j=j: t.matmul(pbig[0][0:1, 192:256], vq_ind[:, j:j + 1],
                                              cb_sb[:, ts(j, 64)], start=(j == 0), stop=(j == 1)),
                 extra=[ev_ind2], first=(j == 0), last=(j == 1))
        ev_coder = K.rd("scalar", [4], lambda s: s.copy(code_row[:], pbig[0][0:1, 192:256]))
        mtc = K.mm([4], lambda t: t.transpose(pbig[0][0:64, 256:257], code_row[:], ident_sb[0:1, 0:1]),
                   extra=[ev_coder])
        ev_codec = K.rd("scalar", [4], lambda s: s.copy(code_col[:], pbig[0][0:64, 256:257]))
        for i, dst in ((0, omix_col), (1, vqmix_col)):
            K.mm([4], lambda t, i=i: t.matmul(pbig[0][:, 300 + i:301 + i], ones_sb[0:1, :],
                                              mixes_sb[0:1, i:i + 1], start=True, stop=True),
                 extra=[ev_consts])
            K.rd("scalar", [4], lambda s, dst=dst, i=i: s.copy(dst[:], pbig[0][:, 300 + i:301 + i]))
        ev_mix = (sc, sc.n)
        for pc in range(DC):
            K.mm([4], lambda t, pc=pc: t.matmul(pbig[0][:, 320 + pc:321 + pc],
                                                vpow_bf[:, ts(pc, 128)], code_col[:],
                                                start=True, stop=True),
                 extra=[ev_vpow, ev_codec], first=(pc == 0), last=(pc == DC - 1))
        K.rd("vector", [4], lambda vv: vv.tensor_tensor(
            vqc_cols[:], pbig[0][:, 320:328], vpobc_sb[:], op=ALU.add), extra=[ev_mix])
        ev_vqc = K.op("vector", lambda vv: vv.tensor_scalar(vqc_cols[:], vqc_cols[:], vqmix_col[:], None, op0=ALU.mult))
        lie_free = [None, None]
        x1_free = [None, None]
        ss_read = None
        for tau in range(TQ):
            for pc in range(DC):
                lpar = pc % 2
                mmk = K.mm([lpar], lambda t, pc=pc, tau=tau, lpar=lpar: t.matmul(
                    pb[lpar][:, 0:512], oow_bf[:, ts(pc, 128)], xr_bf[0:64, ts(tau, 512)],
                    start=True, stop=True), extra=[ev_oow, lie_free[lpar]])
                K.rd("vector", [lpar], lambda vv, pc=pc, lpar=lpar: vv.tensor_scalar(
                    ffn_tmp[:], pb[lpar][:, 0:512], oobc_sb[:, pc:pc + 1], omix_col[:],
                    op0=ALU.add, op1=ALU.mult))
                prt = pre[:, pc * T + tau * 512: pc * T + tau * 512 + 512]
                ev_pre = K.op("vector", lambda vv, pc=pc, tau=tau, prt=prt: vv.scalar_tensor_tensor(
                    prt, ffn_tmp[:], vqc_cols[:, pc:pc + 1],
                    xn_sb[:, pc * T + tau * 512: pc * T + tau * 512 + 512],
                    op0=ALU.add, op1=ALU.add))
                lie_free[lpar] = ev_pre
                ev_sq = K.op("scalar", lambda s, prt=prt: s.activation(sq_t[:], prt, AF.Square),
                             waits=[ev_pre, state.get("l1_sq_free")])
                state["l1_sq_free"] = K.mm(
                    [2], lambda t, pc=pc: t.matmul(pb[2][0:1, 0:512], ones_sb[:, 0:1], sq_t[:],
                                                   start=(pc == 0), stop=(pc == DC - 1)),
                    extra=[ev_sq], first=(pc == 0), last=True)
            e1 = K.rd("vector", [2], lambda vv: vv.tensor_scalar(
                rowC[0:1, 0:512], pb[2][0:1, 0:512], 1.0 / D, RMS_EPS, op0=ALU.mult, op1=ALU.add))
            e2 = K.op("scalar", lambda s: s.activation(rowC[0:1, 0:512], rowC[0:1, 0:512],
                                                       AF.Sqrt), waits=[e1])
            e3 = K.op("vector", lambda vv: vv.reciprocal(rowC[0:1, 0:512], rowC[0:1, 0:512]),
                      waits=[e2])
            K.mm([3], lambda t: t.matmul(pb[3][:, 0:512], ones_sb[0:1, :], rowC[0:1, 0:512],
                                         start=True, stop=True), extra=[e3])
            for pc in range(DC):
                xpar = pc % 2
                prt = pre[:, pc * T + tau * 512: pc * T + tau * 512 + 512]
                ev_xl = K.op("sync", lambda e, pc=pc, tau=tau, xpar=xpar: e.dma_start(
                    out=x1t[xpar][:], in_=x_d[:, pc, ts(tau, 512)]),
                    waits=[x1_free[xpar]], dma=True, dma_sem=f"d_x{xpar}")
                ev_sil = K.op("scalar", lambda s, pc=pc, tau=tau: s.activation(
                    gsil_t[:], sg[:, pc * T + tau * 512: pc * T + tau * 512 + 512], AF.Silu),
                    waits=[state[("ffn_v", 1)], state.get("l1_gsil_free")])
                ev_s1 = K.rd("vector", [3], lambda vv, prt=prt: vv.tensor_tensor(
                    s1_t[:], prt, pb[3][:, 0:512], op=ALU.mult))
                ev_s1b = K.op("vector", lambda vv, pc=pc: vv.scalar_tensor_tensor(
                    s1_t[:], s1_t[:], nwc_sb[:, pc:pc + 1], gsil_t[:], op0=ALU.mult, op1=ALU.mult),
                    waits=[ev_sil])
                state["l1_gsil_free"] = ev_s1b
                ev_fin = K.op("vector", lambda vv, xpar=xpar: vv.tensor_tensor(
                    x1t[xpar][:], x1t[xpar][:], s1_t[:], op=ALU.add), waits=[ev_xl])
                x1_free[xpar] = K.op("sync", lambda e, pc=pc, tau=tau, xpar=xpar: e.dma_start(
                    out=out_fm[:, pc, ts(tau, 512)], in_=x1t[xpar][:]), waits=[ev_fin],
                    dma=True, dma_sem=f"d_xs{xpar}")

    with nc.sbuf_tensor("xs0", [128, DC * T], F32) as xs0, \
         nc.sbuf_tensor("xn0", [128, DC * T], BF16) as xn0, \
         nc.sbuf_tensor("zero_sb", [128, 1024], BF16) as zero_sb, \
         nc.sbuf_tensor("lnsq0", [128, 512], F32) as lnsq0, \
         nc.sbuf_tensor("lnsq1", [128, 512], F32) as lnsq1, \
         nc.sbuf_tensor("rowA0", [1, 512], F32) as rowA0, \
         nc.sbuf_tensor("rowB0", [1, 512], F32) as rowB0:
        ev_zmem = K.op("vector", lambda vv: vv.memset(zero_sb[:], 0.0))
        for i in range(E * CAP // 128):
            K.op("sync", lambda e, i=i: e.dma_start(
                out=xg_d[i * 128:(i + 1) * 128, :], in_=zero_sb[:]), waits=[ev_zmem], dma=True)
        phase_ln(0, x0.rearrange("(dc p) t -> p dc t", p=128), xs0, xn0, False, (lnsq0, lnsq1), rowA0, rowB0)
        K.bar()

    with contextlib.ExitStack() as stf0:
        sbf = lambda n, sh, dt, td=False: stf0.enter_context(
            nc.sbuf_tensor(n, sh, dt, is_transpose_dest=td))
        xgfm0 = (sbf("xgfm00", [128, DC * CAP], BF16, True), sbf("xgfm01", [128, DC * CAP], BF16, True))
        hact0 = sbf("hact0", [128, HC * CAP], BF16)
        w1a = tuple(sbf(f"w1a{i}", [128, DC * 256], BF16) for i in range(6))
        w2a = tuple(sbf(f"w2a{i}", [128, 1024], BF16) for i in range(6))
        ytma = (sbf("ytma0", [128, (CAP // 128) * 1024], BF16), sbf("ytma1", [128, (CAP // 128) * 1024], BF16))
        xntma = (sbf("xntma0", [128, 1024], BF16, True), sbf("xntma1", [128, 1024], BF16, True))
        b2ra = (sbf("b2ra0", [1, D], F32), sbf("b2ra1", [1, D], F32))
        phase_ffn_sparse(0, xgfm0, hact0, w1a, w2a, ytma, xntma, b2ra, True)
        K.bar()
    if sg_d is not None:
        K.op("sync", lambda e: e.dma_start(out=sg_d, in_=sg[:]), dma=True)
        K.bar()

    with contextlib.ExitStack() as st3:
        sb3 = lambda n, sh, dt, td=False: st3.enter_context(
            nc.sbuf_tensor(n, sh, dt, is_transpose_dest=td))
        xna = sb3("xna", [128, DC * T], BF16)
        av_sb = sb3("av", [128, DC * T], BF16)
        wpT0 = sb3("wpT0", [128, DC * 128], BF16)
        wpT1 = sb3("wpT1", [128, DC * 128], BF16)
        attnTa = sb3("attnTa", [128, TC * 512], BF16)
        attnTb = sb3("attnTb", [128, TC * 512], BF16)
        Qba = sb3("Qba", [128, T], BF16)
        Qbb = sb3("Qbb", [128, T], BF16)
        Kba = sb3("Kba", [128, T], BF16)
        Kbb = sb3("Kbb", [128, T], BF16)
        Vba = sb3("Vba", [128, TC * 65], BF16)
        Vbb = sb3("Vbb", [128, TC * 65], BF16)
        qkvb0 = sb3("qkvb0", [128, 3 * 512], BF16)
        qkvb1 = sb3("qkvb1", [128, 3 * 512], BF16)
        rrepa = sb3("rrepa", [HD, 512], F32)
        rrepb = sb3("rrepb", [HD, 512], F32)
        nmcols = sb3("nmcols", [128, 10], F32)
        rrow = sb3("rrow", [1, 512], F32)
        x0t0 = sb3("x0t0", [128, 512], F32)
        x0t1 = sb3("x0t1", [128, 512], F32)
        phase_attn(xna, av_sb, (wpT0, wpT1), (attnTa, attnTb), (Qba, Qbb),
                   (Kba, Kbb), (Vba, Vbb), (qkvb0, qkvb1), (rrepa, rrepb),
                   nmcols, rrow, (x0t0, x0t1))
        K.bar()

    with nc.sbuf_tensor("xs1", [128, DC * T], F32) as xs1, \
         nc.sbuf_tensor("xn1", [128, DC * T], BF16) as xn1, \
         nc.sbuf_tensor("lnsq0b", [128, 512], F32) as lnsq0b, \
         nc.sbuf_tensor("lnsq1b", [128, 512], F32) as lnsq1b, \
         nc.sbuf_tensor("rowA1", [1, 512], F32) as rowA1, \
         nc.sbuf_tensor("rowB1", [1, 512], F32) as rowB1:
        phase_ln(1, x_d, xs1, xn1, True, (lnsq0b, lnsq1b), rowA1, rowB1)
        K.bar()

    with contextlib.ExitStack() as stf1:
        sbf = lambda n, sh, dt, td=False: stf1.enter_context(
            nc.sbuf_tensor(n, sh, dt, is_transpose_dest=td))
        xgfm1 = (sbf("xgfm10", [128, DC * CAP], BF16, True), sbf("xgfm11", [128, DC * CAP], BF16, True))
        hact1 = sbf("hact1", [128, HC * CAP], BF16)
        w1b = tuple(sbf(f"w1b{i}", [128, DC * 256], BF16) for i in range(6))
        w2b = tuple(sbf(f"w2b{i}", [128, 1024], BF16) for i in range(6))
        ytmb = (sbf("ytmb0", [128, (CAP // 128) * 1024], BF16), sbf("ytmb1", [128, (CAP // 128) * 1024], BF16))
        xntmb = (sbf("xntmb0", [128, 1024], BF16, True), sbf("xntmb1", [128, 1024], BF16, True))
        b2rb = (sbf("b2rb0", [1, D], F32), sbf("b2rb1", [1, D], F32))
        phase_ffn_sparse(1, xgfm1, hact1, w1b, w2b, ytmb, xntmb, b2rb, False, precast=False)
        K.bar()
    if sg_d1 is not None:
        K.op("sync", lambda e: e.dma_start(out=sg_d1, in_=sg[:]), dma=True)
        K.bar()

    with contextlib.ExitStack() as st6:
        sb6 = lambda n, sh, dt: st6.enter_context(nc.sbuf_tensor(n, sh, dt))
        xnb = sb6("xnb", [128, DC * T], BF16)
        pre = sb6("pre", [128, DC * T], BF16)
        xr_bf = sb6("xr", [64, T], BF16)
        oow_bf = sb6("oowb", [HD, D], BF16)
        vpow_bf = sb6("vpowb", [HD, D], BF16)
        opwb0 = sb6("opwb0", [128, 2016], BF16)
        opwb1 = sb6("opwb1", [128, 2016], BF16)
        opwb2 = sb6("opwb2", [128, 2016], BF16)
        vpiw_bf = sb6("vpiwb", [128, DC * HD], F32)
        params_sb = sb6("params", [1, 2016], F32)
        om_u = sb6("om_u", [64, 64], F32)
        om_ut = sb6("om_ut", [64, 64], F32)
        om = sb6("om", [64, 64], F32)
        om_n = sb6("om_n", [64, 64], F32)
        om2 = sb6("om2", [64, 64], F32)
        om3 = sb6("om3", [64, 64], F32)
        Gt = sb6("Gt", [64, 64], F32)
        G_bf = sb6("G_bf", [64, 64], BF16)
        vq_row = sb6("vq_row", [1, 64], F32)
        vq_sq = sb6("vq_sq", [1, 64], F32)
        vq_n = sb6("vq_n", [1, 256], F32)
        cb_sq = sb6("cb_sq", [128, 128], F32)
        cb_n = sb6("cb_n", [128, 9], F32)
        cb_sc = sb6("cb_sc", [128, 128], F32)
        vqn_rep = sb6("vqn_rep", [128, 64], F32)
        vq_gc = sb6("vq_gc", [128, 1], F32)
        vq_ind = sb6("vq_ind", [128, 2], F32)
        code_row = sb6("code_row", [1, 64], F32)
        code_col = sb6("code_col", [64, 1], BF16)
        omix_col = sb6("omix", [128, 1], F32)
        vqmix_col = sb6("vqmix", [128, 1], F32)
        vqc_cols = sb6("vqc_cols", [128, DC], F32)
        sq_t = sb6("sq_t", [128, 512], F32)
        gsil_t = sb6("gsil_t", [128, 512], F32)
        s1_t = sb6("s1_t", [128, 512], F32)
        x1t0 = sb6("x1t0", [128, 512], F32)
        x1t1 = sb6("x1t1", [128, 512], F32)
        vqs = (params_sb, opb_sb, om_u, om_ut, om, om_n, om2, om3, Gt, G_bf, vq_row,
               vq_sq, vq_n, cb_sq, cb_n, cb_sc, vqn_rep, vq_gc, vq_ind,
               code_row, code_col, omix_col, vqmix_col, vqc_cols)
        phase_l1small(xnb, pre, xr_bf, oow_bf, vpow_bf, (opwb0, opwb1, opwb2), vpiw_bf,
                      sq_t, gsil_t, s1_t, (x1t0, x1t1), vqs)

    K.bar()

    with nc.Block() as block:
        K.replay(block)
    stk.close()
    return nc


_NC_CACHE = {}


def get_nc(dbg=()):
    key = tuple(sorted(dbg))
    if key not in _NC_CACHE:
        _NC_CACHE[key] = build(key)
    return _NC_CACHE[key]


def host_inputs(inputs):
    f = np.float32
    a = lambda x: np.ascontiguousarray(x, dtype=f)
    g = {}
    g["ident"] = np.eye(128, dtype=f)
    g["ones"] = np.ones((128, 128), dtype=f)
    g["Wqkv"] = a(inputs["L0_attn_Wqkv"])
    bqkv = np.asarray(inputs["L0_attn_bqkv"], dtype=f)
    g["bqkvT"] = a(bqkv.T)
    g["bqkv_flat"] = a(bqkv.reshape(1, -1))
    g["WpT"] = a(np.asarray(inputs["L0_attn_Wp"], dtype=f).T)
    g["bp_cols"] = a(np.asarray(inputs["L0_attn_bp"], dtype=f).reshape(8, 128).T)
    g["Wg0"] = a(inputs["L0_ffn_Wg"])
    g["bg0"] = a(np.asarray(inputs["L0_ffn_bg"], dtype=f).reshape(1, -1))
    g["W1_0"] = a(inputs["L0_ffn_W1"])
    g["W2_0"] = a(inputs["L0_ffn_W2"])
    g["Wg1"] = a(np.asarray(inputs["L1_ffn_gate_w"], dtype=f).T)
    g["bg1"] = a(np.asarray(inputs["L1_ffn_gate_b"], dtype=f).reshape(1, -1))
    g["W1_1"] = a(np.asarray(inputs["L1_ffn_w1"], dtype=f).transpose(0, 2, 1))
    g["W2_1"] = a(np.asarray(inputs["L1_ffn_w2"], dtype=f).transpose(0, 2, 1))
    g["b1c"] = a(np.asarray(inputs["L1_ffn_b1"], dtype=f).reshape(8, 32, 128).transpose(0, 2, 1))
    g["b2c"] = a(np.asarray(inputs["L1_ffn_b2"], dtype=f).reshape(8, 8, 128).transpose(0, 2, 1))
    g["opwT"] = a(np.asarray(inputs["L1_attn_omega_proj_w"], dtype=f).T)
    g["opb"] = a(np.asarray(inputs["L1_attn_omega_proj_b"], dtype=f).reshape(1, -1))
    g["oowT"] = a(np.asarray(inputs["L1_attn_omega_out_w"], dtype=f).T)
    g["oobc"] = a(np.asarray(inputs["L1_attn_omega_out_b"], dtype=f).reshape(8, 128).T)
    g["vpiwT"] = a(np.asarray(inputs["L1_attn_vq_proj_in_w"], dtype=f).T)
    g["vpib"] = a(np.asarray(inputs["L1_attn_vq_proj_in_b"], dtype=f).reshape(1, -1))
    g["vpowT"] = a(np.asarray(inputs["L1_attn_vq_proj_out_w"], dtype=f).T)
    g["vpobc"] = a(np.asarray(inputs["L1_attn_vq_proj_out_b"], dtype=f).reshape(8, 128).T)
    g["lt128"] = np.triu(np.ones((128, 128), dtype=f))
    g["ecm1"] = np.tile((np.arange(8, dtype=f) * 384.0 - 1.0)[None, :], (128, 1))
    g["b2r"] = a(np.asarray(inputs["L1_ffn_b2"], dtype=f).reshape(8, 1, 1024))
    cb = np.asarray(inputs["L1_attn_codebook"], dtype=f)
    g["cbp"] = a(np.concatenate([cb[0:128], cb[128:256]], axis=1))
    g["mixes"] = np.array([[float(inputs["L1_attn_omega_mix"]),
                            float(inputs["L1_attn_vq_mix"])]], dtype=f)
    g["nwc"] = a(np.asarray(inputs["L1_attn_norm_w"], dtype=f).reshape(8, 128).T)
    return g


def kernel(**inputs):
    nc = get_nc()
    shared = host_inputs(inputs)
    x = np.asarray(inputs["L0_attn_x"], dtype=np.float32)
    in_maps = []
    for b in range(NCORES):
        m = dict(shared)
        m["x0"] = np.ascontiguousarray(x[b].T)
        in_maps.append(m)
    res = run_bass_kernel_spmd(nc, in_maps, core_ids=list(range(NCORES)))
    outs = []
    for b in range(NCORES):
        o_ = np.asarray(res.results[b]["out_fm"])
        outs.append(o_.transpose(1, 0, 2).reshape(D, T).T)
    return np.stack(outs, axis=0)


# version 19
# speedup vs baseline: 1.1929x; 1.0790x over previous
import contextlib
import numpy as np
import concourse.bass as bass
import concourse.mybir as mybir
from concourse.bass import IndirectOffsetOnAxis
from concourse.bass_utils import run_bass_kernel_spmd

F32 = mybir.dt.float32
BF16 = mybir.dt.bfloat16
AF = mybir.ActivationFunctionType
ALU = mybir.AluOpType
AX = mybir.AxisListType

D, T, H, E, NH, HD = 1024, 2048, 4096, 8, 16, 64
DC, TQ, TC, TH, HC = 8, 4, 16, 2, 32
NCORES = 8
RMS_EPS = 1e-6
LN_EPS = 1e-5
CAP = 384
I32 = mybir.dt.int32


def ts(i, n):
    return slice(i * n, (i + 1) * n)


class Ctr:
    def __init__(self, h):
        self.h = h
        self.n = 0


class B:
    ENGINES = ("tensor", "vector", "scalar", "gpsimd", "sync")

    def __init__(self, nc, stk):
        self.nc = nc
        self.stk = stk
        self.q = {e: [] for e in self.ENGINES}
        self.ctr = {}
        for name in ("pe", "v", "sc", "gp", "dg", "ds"):
            self.ctr[name] = Ctr(stk.enter_context(nc.semaphore("s_" + name)))
        self.eng2ctr = {"tensor": "pe", "vector": "v", "scalar": "sc",
                        "gpsimd": "gp", "sync": None}
        self.serial_dma = {id(self.ctr["dg"]), id(self.ctr["ds"])}
        self.bank = {i: {"w": None, "r": {}} for i in range(8)}

    def dsem(self, name):
        if name not in self.ctr:
            self.ctr[name] = Ctr(self.stk.enter_context(self.nc.semaphore(name)))
        return self.ctr[name]

    def op(self, eng, builder, waits=(), inc=None, dma=False, dma_sem=None):
        ww = {}
        waits = list(waits)
        if eng in ("vector", "scalar") and not dma:
            cs = self.ctr[self.eng2ctr[eng]]
            if cs.n > 0:
                waits.append((cs, cs.n))
        for w in waits:
            if w is None:
                continue
            c, v = w
            if v is None:
                continue
            if id(c) in self.serial_dma:
                v = c.n
            ww[id(c)] = (c.h, max(v, ww[id(c)][1]) if id(c) in ww else v)
        ww = list(ww.values())
        incs = []
        if dma:
            if dma_sem is not None:
                c = self.dsem(dma_sem)
            else:
                c = self.ctr["dg" if eng == "gpsimd" else "ds"]
            c.n += 16
            incs = [(c.h, 16)]
            mark = (c, c.n)
        elif inc is False:
            mark = None
        else:
            c = self.ctr[self.eng2ctr[eng]]
            assert c is not None
            c.n += 1
            incs = [(c.h, 1)]
            mark = (c, c.n)

        def go(engine, ww=ww, incs=incs, builder=builder):
            for h, v in ww:
                engine.wait_ge(h, v)
            ins = builder(engine)
            for h, k in incs:
                ins.then_inc(h, k)
        self.q[eng].append(go)
        return mark

    def wait_only(self, eng, waits):
        ww = []
        for w in waits:
            if w is None or w[1] is None:
                continue
            vv_ = w[0].n if id(w[0]) in self.serial_dma else w[1]
            ww.append((w[0].h, vv_))

        def go(engine, ww=ww):
            for h, v in ww:
                engine.wait_ge(h, v)
        self.q[eng].append(go)

    def marks_all(self):
        return [(c, c.n) for c in self.ctr.values()]

    def bar(self):
        tgt = self.marks_all()
        for eng in self.ENGINES:
            self.wait_only(eng, tgt)
        for b in self.bank.values():
            b["w"] = None
            b["r"] = {}

    def bank_waits_for_write(self, banks):
        w = []
        for b in banks:
            w += list(self.bank[b]["r"].values())
        return w

    def note_write(self, banks, mark):
        for b in banks:
            self.bank[b]["w"] = mark
            self.bank[b]["r"] = {}

    def read_waits(self, banks):
        return [self.bank[b]["w"] for b in banks if self.bank[b]["w"] is not None]

    def note_read(self, banks, eng, mark):
        for b in banks:
            self.bank[b]["r"][eng] = mark

    def mm(self, banks, builder, extra=(), first=True, last=True, force_inc=False):
        waits = list(extra)
        if first:
            waits += self.bank_waits_for_write(banks)
        mark = self.op("tensor", builder,
                       waits=waits, inc=(None if (last or force_inc) else False))
        if last:
            self.note_write(banks, mark)
        return mark if (last or force_inc) else None

    def rd(self, eng, banks, builder, extra=()):
        waits = list(extra) + self.read_waits(banks)
        mark = self.op(eng, builder, waits=waits)
        self.note_read(banks, eng, mark)
        return mark

    def replay(self, block):
        for eng in self.ENGINES:
            items = self.q[eng]
            if not items:
                continue

            def body(engine, items=items):
                for fn in items:
                    fn(engine)
            getattr(block, eng)(body)


def build(dbg=()):
    nc = bass.Bass("TRN2", target_bir_lowering=False, debug=False)

    def din(name, shape, dtype=F32):
        return nc.dram_tensor(name, shape, dtype, kind="ExternalInput").ap()

    x0 = din("x0", [D, T])
    ident = din("ident", [128, 128])
    ones = din("ones", [128, 128])
    Wqkv = din("Wqkv", [48, D, HD])
    bqkvT = din("bqkvT", [HD, 48])
    bqkv_flat = din("bqkv_flat", [1, 48 * HD])
    WpT = din("WpT", [D, D])
    bp_cols = din("bp_cols", [128, DC])
    Wg_l = [din("Wg0", [D, E]), din("Wg1", [D, E])]
    bg_l = [din("bg0", [1, E]), din("bg1", [1, E])]
    W1_l = [din("W1_0", [E, D, H]), din("W1_1", [E, D, H])]
    W2_l = [din("W2_0", [E, H, D]), din("W2_1", [E, H, D])]
    b1c = din("b1c", [E, 128, 32])
    b2c = din("b2c", [E, 128, DC])
    opwT = din("opwT", [D, 2016])
    opb = din("opb", [1, 2016])
    oowT = din("oowT", [HD, D])
    oobc = din("oobc", [128, DC])
    vpiwT = din("vpiwT", [D, HD])
    vpib = din("vpib", [1, HD])
    vpowT = din("vpowT", [HD, D])
    vpobc = din("vpobc", [128, DC])
    lt128 = din("lt128", [128, 128])
    ecm1 = din("ecm1", [128, E])
    b2r = din("b2r", [E, 1, D])
    cbp = din("cbp", [128, 128])
    mixes = din("mixes", [1, 2])
    nwc = din("nwc", [128, DC])

    ok = lambda n: "ExternalOutput" if n in dbg else "Internal"
    xn_d = nc.dram_tensor("xn_d", [128, DC, T], BF16, kind=ok("xn_d")).ap()
    x_d = nc.dram_tensor("x_d", [128, DC, T], F32, kind=ok("x_d")).ap()
    sg_d = nc.dram_tensor("sg_dump", [128, DC * T], BF16, kind="ExternalOutput").ap() if "sg" in dbg else None
    sg_d1 = nc.dram_tensor("sg_dump1", [128, DC * T], BF16, kind="ExternalOutput").ap() if "sg1" in dbg else None
    out_fm = nc.dram_tensor("out_fm", [128, DC, T], F32, kind="ExternalOutput").ap()
    ind_d = nc.dram_tensor("ind_d", [8, T], F32, kind="Internal").ap()
    xg_d = nc.dram_tensor("xg_d", [E * CAP, D], BF16, kind="Internal").ap()
    w1bf_d = nc.dram_tensor("w1bf_d", [E, D, H], BF16, kind="Internal").ap()
    w2bf_d = nc.dram_tensor("w2bf_d", [E, H, D], BF16, kind="Internal").ap()
    ytm_d = nc.dram_tensor("ytm_d", [E * CAP, D], BF16, kind="Internal").ap()
    gtm_d = nc.dram_tensor("gtm_d", [T, D], BF16, kind="Internal").ap()

    stk = contextlib.ExitStack()
    sbuf = lambda name, shape, dtype: stk.enter_context(nc.sbuf_tensor(name, shape, dtype))
    K = B(nc, stk)
    pe, v, sc, dg, ds = (K.ctr[k] for k in ("pe", "v", "sc", "dg", "ds"))

    pb01 = [stk.enter_context(nc.psum_tensor(f"pb{i}", [128, 512], F32)) for i in range(2)]
    p4 = stk.enter_context(nc.psum_tensor("p4", [128, 2048], F32))
    pbig1_t = stk.enter_context(nc.psum_tensor("pbig1", [128, 1024], F32))
    pb = [pb01[0].ap() if hasattr(pb01[0], "ap") else pb01[0], None, None, None]
    pb[0] = pb01[0]
    pb[1] = pb01[1]
    pb[2] = p4[:, 0:512]
    pb[3] = p4[:, 512:1024]
    pbig = [p4[:, 1024:2048], pbig1_t]

    sg = stk.enter_context(nc.sbuf_tensor("sg", [128, DC * T], BF16, is_transpose_dest=True))
    lt_sb = sbuf("lt_sb", [128, 128], F32)
    ecm1_sb = sbuf("ecm1_sb", [128, E], F32)
    ind_all = sbuf("ind_all", [128, TC * E], F32)
    cs_rows = sbuf("cs_rows", [1, 128], F32)
    bases = sbuf("bases", [1, 128], F32)
    tmp8 = sbuf("tmp8", [128, E], F32)
    tcol = sbuf("tcol", [128, 1], F32)
    tsl_sb = sbuf("tsl_sb", [128, TC], I32)
    pc_stage = [sbuf(f"pc_stage{i}", [128, 2048], BF16) for i in range(2)]
    ident_sb = sbuf("ident_sb", [128, 128], F32)
    ones_sb = sbuf("ones_sb", [128, 128], F32)
    ind_fm = sbuf("ind_fm", [8, T], F32)
    mrow_sb = sbuf("mrow_sb", [1, 1024], F32)
    hm_cols = sbuf("hm_cols", [128, DC], F32)
    hm_bf = sbuf("hm_bf", [128, DC], BF16)
    bg_rep = sbuf("bg_rep", [128, E], F32)
    rowC = sbuf("rowC", [1, 512], F32)
    ltm = sbuf("ltm", [128, E], F32)
    mxcol = sbuf("mxcol", [128, 1], F32)
    ind_tm = sbuf("ind_tm", [128, E], F32)
    wg_bf = sbuf("wg_bf", [128, DC * E], F32)
    rstdc_sb = sbuf("rstdc_sb", [128, 4], F32)
    bgrow = sbuf("bgrow", [1, E], F32)
    bp_sb = sbuf("bp_sb", [128, DC], F32)
    bqkvT_sb = sbuf("bqkvT_sb", [HD, 48], F32)
    oobc_sb = sbuf("oobc_sb", [128, DC], F32)
    vpobc_sb = sbuf("vpobc_sb", [128, DC], F32)
    nwc_sb = sbuf("nwc_sb", [128, DC], F32)
    opb_sb = sbuf("opb_sb", [1, 2016], F32)
    vpib_sb = sbuf("vpib_sb", [1, HD], F32)
    mixes_sb = sbuf("mixes_sb", [1, 2], F32)
    cb_sb = sbuf("cb_sb", [128, 128], F32)
    b1c_sb = [sbuf(f"b1c_sb{i}", [128, 32], F32) for i in range(2)]
    b2c_sb = [sbuf(f"b2c_sb{i}", [128, DC], F32) for i in range(2)]
    ffn_tmp = sbuf("ffn_tmp", [128, 512], F32)

    state = {}

    for dst, src in [
        (ident_sb[:], ident), (ones_sb[:], ones), (bp_sb[:], bp_cols),
        (bqkvT_sb[:], bqkvT), (oobc_sb[:], oobc),
        (vpobc_sb[:], vpobc), (nwc_sb[:], nwc), (opb_sb[:], opb),
        (vpib_sb[:], vpib), (mixes_sb[:], mixes), (cb_sb[:], cbp),
        (lt_sb[:], lt128), (ecm1_sb[:], ecm1),
    ]:
        ev_consts = K.op("sync", lambda e, d=dst, s=src: e.dma_start(out=d, in_=s), dma=True)

    def phase_ln(l, xsrc, xs, xn_sb, want_hm, ln_sq, rowA, rowB):
        ev_x = K.op("sync", lambda e: e.dma_start(
            out=xs[:].rearrange("p (dc t) -> p dc t", dc=DC), in_=xsrc), dma=True)
        ev_wg = K.op("sync", lambda g: g.dma_start(
            out=wg_bf[:].rearrange("p (dc e) -> p dc e", dc=DC),
            in_=Wg_l[l].rearrange("(dc p) e -> p dc e", p=128)), dma=True)
        ev_bg = K.op("sync", lambda e: e.dma_start(out=bgrow[:], in_=bg_l[l]), dma=True)
        m = K.mm([0], lambda t: t.matmul(pb[0][:, 0:E], ones_sb[0:1, :], bgrow[:],
                                         start=True, stop=True),
                 extra=[ev_bg, ev_consts])
        ev_bgrep = K.rd("scalar", [0], lambda s: s.copy(bg_rep[:], pb[0][:, 0:E]))

        def emit_stats(tau):
            for dc in range(DC):
                par = dc % 2
                xt = xs[:, dc * T + tau * 512: (dc * T + tau * 512) + 512]
                sq_ev = K.op("scalar", lambda s, a=ln_sq[par], b=xt: s.activation(a[:], b, AF.Square),
                             waits=[ev_x, state.get(("lnsq", par))])
                K.mm([0], lambda t, b=xt, dc=dc: t.matmul(
                    pb[0][0:1, 0:512], ones_sb[:, 0:1], b, start=(dc == 0), stop=(dc == DC - 1)),
                    extra=[ev_x, ev_consts], first=(dc == 0), last=(dc == DC - 1))
                m2 = K.mm([1], lambda t, a=ln_sq[par], dc=dc: t.matmul(
                    pb[1][0:1, 0:512], ones_sb[:, 0:1], a[:], start=(dc == 0), stop=(dc == DC - 1)),
                    extra=[sq_ev], first=(dc == 0), last=True)
                state[("lnsq", par)] = m2

        emit_stats(0)
        for tau in range(TQ):
            mrow = rowA[0:1, 0:512]
            rrow = rowB[0:1, 0:512]
            e1 = K.rd("vector", [0], lambda vv, mr=mrow: vv.tensor_scalar(
                mr, pb[0][0:1, 0:512], 1.0 / D, None, op0=ALU.mult),
                extra=[state.get("ln_mrow_free")])
            e2 = K.rd("vector", [1], lambda vv: vv.tensor_scalar(
                rowC[:], pb[1][0:1, 0:512], 1.0 / D, LN_EPS, op0=ALU.mult, op1=ALU.add),
                extra=[e1])
            e3 = K.op("vector", lambda vv, mr=mrow, rr=rrow: vv.tensor_tensor(rr, mr, mr, op=ALU.mult),
                      waits=[e2, state.get("ln_rrow_free")])
            e4 = K.op("vector", lambda vv, rr=rrow: vv.tensor_tensor(rr, rowC[:], rr, op=ALU.subtract),
                      waits=[e3])
            e5 = K.op("scalar", lambda s, rr=rrow: s.activation(rr, rr, AF.Sqrt),
                      waits=[e4])
            e6 = K.op("vector", lambda vv, rr=rrow: vv.reciprocal(rr, rr), waits=[e5])
            if tau + 1 < TQ:
                emit_stats(tau + 1)
            mmm = K.mm([2], lambda t, mr=mrow: t.matmul(
                pb[2][:, 0:512], ones_sb[0:1, :], mr, start=True, stop=True), extra=[e4])
            state["ln_mrow_free"] = mmm
            mmr = K.mm([3], lambda t, rr=rrow: t.matmul(
                pb[3][:, 0:512], ones_sb[0:1, :], rr, start=True, stop=True), extra=[e6])
            for tc4 in range(4):
                mtr = K.mm([7], lambda t, tc4=tc4, rr=rrow: t.transpose(
                    pbig[1][:, 512:513], rr[0:1, ts(tc4, 128)], ident_sb[0:1, 0:1]),
                    extra=[e6])
                K.rd("scalar", [7], lambda s2, tc4=tc4: s2.copy(
                    rstdc_sb[:, tc4:tc4 + 1], pbig[1][:, 512:513]))
            state["ln_rrow_free"] = mtr
            for dc in range(DC):
                xt = xs[:, dc * T + tau * 512: (dc * T + tau * 512) + 512]
                xnt = xn_sb[:, dc * T + tau * 512: (dc * T + tau * 512) + 512]
                K.rd("vector", [2], lambda vv, a=xt: vv.tensor_tensor(a, a, pb[2][:, 0:512], op=ALU.subtract))
                ev_n = K.rd("vector", [3], lambda vv, a=xt, b=xnt: vv.tensor_tensor(
                    b, a, pb[3][:, 0:512], op=ALU.mult))
                K.op("sync", lambda e, b=xnt, dc=dc, tau=tau: e.dma_start(
                    out=xn_d[:, dc, ts(tau, 512)], in_=b), waits=[ev_n], dma=True)
            ev_norm = (v, v.n)
            for tc4 in range(4):
                g_ = tau * 4 + tc4
                for dc in range(DC):
                    K.mm([4], lambda t, dc=dc, g_=g_, tc4=tc4: t.matmul(
                        pbig[0][:, ts(tc4, 8)],
                        xs[:, dc * T + g_ * 128: dc * T + g_ * 128 + 128],
                        wg_bf[:, ts(dc, E)], start=(dc == 0), stop=(dc == DC - 1)),
                        extra=[ev_norm, ev_wg], first=(dc == 0), last=(dc == DC - 1))
                e1 = K.rd("vector", [4], lambda vv, tc4=tc4: vv.scalar_tensor_tensor(
                    ltm[:], pbig[0][:, ts(tc4, 8)], rstdc_sb[:, tc4:tc4 + 1], bg_rep[:],
                    op0=ALU.mult, op1=ALU.add),
                    extra=[ev_bgrep, state.get("ln_ltm_free"), (sc, sc.n)])
                e2 = K.op("vector", lambda vv: vv.reduce_max(mxcol[:], ltm[:], axis=AX.X), waits=[e1])
                K.wait_only("vector", [e2])
                e3 = K.op("vector", lambda vv: vv.tensor_scalar(
                    ind_tm[:], ltm[:], mxcol[:], None, op0=ALU.is_equal),
                    waits=[e2])
                K.op("vector", lambda vv, g_=g_: vv.tensor_copy(
                    ind_all[:, ts(g_, E)], ind_tm[:]))
                state["ln_ltm_free"] = e3
        ev_ind = (v, v.n)
        for tc in range(TC):
            K.mm([6], lambda t, tc=tc: t.matmul(
                pbig[1][0:1, ts(tc, E)], ones_sb[:, 0:1], ind_all[:, ts(tc, E)],
                start=True, stop=True),
                extra=[ev_ind] if tc == 0 else [], first=(tc == 0), last=(tc == TC - 1))
        K.rd("scalar", [6], lambda s: s.copy(cs_rows[:], pbig[1][0:1, 0:128]))
        K.op("vector", lambda vv: vv.memset(bases[0:1, 0:E], 0.0), waits=[(sc, sc.n)])
        for tc in range(1, TC):
            K.op("vector", lambda vv, tc=tc: vv.tensor_tensor(
                bases[0:1, ts(tc, E)], bases[0:1, ts(tc - 1, E)],
                cs_rows[0:1, ts(tc - 1, E)], op=ALU.add))
        ev_bases = (v, v.n)
        for tc in range(TC):
            cpar = tc % 2
            cum_sl = pbig[0][:, cpar * 512: cpar * 512 + E]
            cbank = [4 + cpar]
            K.mm(cbank, lambda t, tc=tc, cum_sl=cum_sl: t.matmul(
                cum_sl, lt_sb[:], ind_all[:, ts(tc, E)], start=True, stop=False),
                extra=[ev_bases, ev_consts], first=True, last=False)
            K.mm(cbank, lambda t, tc=tc, cum_sl=cum_sl: t.matmul(
                cum_sl, ones_sb[0:1, :], bases[0:1, ts(tc, E)], start=False, stop=True),
                first=False, last=True)
            K.rd("vector", cbank, lambda vv, cum_sl=cum_sl: vv.tensor_tensor(
                tmp8[:], cum_sl, ecm1_sb[:], op=ALU.add))
            K.op("vector", lambda vv, tc=tc: vv.tensor_tensor(
                tmp8[:], tmp8[:], ind_all[:, ts(tc, E)], op=ALU.mult))
            K.op("vector", lambda vv: vv.tensor_reduce(tcol[:], tmp8[:], AX.X, ALU.add))
            K.op("vector", lambda vv: vv.tensor_scalar(
                tcol[:], tcol[:], float(E * CAP - 1), None, op0=ALU.min))
            K.op("vector", lambda vv, tc=tc: vv.tensor_copy(tsl_sb[:, tc:tc + 1], tcol[:]))
        state["tsl"] = (v, v.n)
        if want_hm:
            prev = None
            for dc in range(DC):
                prev = K.op("vector", lambda vv, dc=dc: vv.tensor_reduce(
                    hm_cols[:, dc:dc + 1], xn_sb[:, ts(dc, T)], AX.X, ALU.add), waits=[(v, v.n)])
            K.wait_only("vector", [prev])
            state["hm"] = K.op("vector", lambda vv: vv.tensor_scalar(
                hm_bf[:], hm_cols[:], 1.0 / T, None, op0=ALU.mult))

    def phase_ffn(l, xnh, hact, w2buf, w1buf, maskrep, gelu):
        ev_memset = K.op("vector", lambda vv: vv.memset(sg[:], 0.0))
        a_evac = {}
        w1_free = [None] * 6
        w2_free = [None] * 6
        mask_free = None
        hps_free = [None, None]
        b1_free = [None, None]
        b2_free = [None, None]
        ev_b1 = [None, None]
        ev_b2 = [None, None]
        xnh_free = None
        ybanks_sl = [pb[0][:, 0:512], pb[1][:, 0:512], pb[2][:, 0:512], pb[3][:, 0:512],
                     pbig[0][:, 0:512], pbig[0][:, 512:1024],
                     pbig[1][:, 0:512], pbig[1][:, 512:1024]]
        ybank_ids = [[0], [1], [2], [3], [4], [5], [6], [7]]

        for th in range(TH):
            ev_xnh = K.op("sync", lambda e, th=th: e.dma_start(
                out=xnh[:].rearrange("p (dc t) -> p dc t", dc=DC),
                in_=xn_d[:, :, ts(th, 1024)]), waits=[xnh_free], dma=True)
            for e_ in range(E):
                bpar = e_ % 2
                if l == 1:
                    K.op("sync", lambda e, e_=e_, bp=bpar: e.dma_start(
                        out=b1c_sb[bp][:], in_=b1c[e_]), waits=[b1_free[bpar], b2_free[bpar]],
                        dma=True, dma_sem=f"d_b{bpar}")
                    mboth = K.op("sync", lambda e, e_=e_, bp=bpar: e.dma_start(
                        out=b2c_sb[bp][:], in_=b2c[e_]), waits=[],
                        dma=True, dma_sem=f"d_b{bpar}")
                    ev_b1[bpar] = ev_b2[bpar] = mboth
                ev_mrow = K.op("sync", lambda e, e_=e_, th=th: e.dma_start(
                    out=mrow_sb[:], in_=ind_d[e_:e_ + 1, ts(th, 1024)]),
                    waits=[state[("masks", l)], state.get("ffn_mrow_free")], dma=True)
                for seg in range(16):
                    par = seg % 4
                    ev_w1 = K.op("gpsimd", lambda g, e_=e_, seg=seg, par=par: g.dma_start(
                        out=w1buf[par][:].rearrange("p (dc h) -> p dc h", dc=DC),
                        in_=W1_l[l][e_, :, ts(seg, 256)].rearrange("(dc p) h -> p dc h", p=128)),
                        waits=[w1_free[par]], dma=True, dma_sem=f"d_w1{par}")
                    for hc2 in range(2):
                        hc = seg * 2 + hc2
                        for sub in range(2):
                            hb = sub
                            for dc in range(DC):
                                mmk = K.mm([hb], lambda t, par=par, dc=dc, hc2=hc2, sub=sub, hb=hb: t.matmul(
                                    pb[hb][:, 0:512],
                                    w1buf[par][:, dc * 256 + hc2 * 128: dc * 256 + hc2 * 128 + 128],
                                    xnh[:, dc * 1024 + sub * 512: dc * 1024 + sub * 512 + 512],
                                    start=(dc == 0), stop=(dc == DC - 1)),
                                    extra=([ev_w1, ev_xnh] if dc == 0 else []),
                                    first=(dc == 0), last=(dc == DC - 1))
                            if hc2 == 1 and sub == 1:
                                w1_free[par] = mmk
                            hsl = hact[:, hc * 1024 + sub * 512: hc * 1024 + sub * 512 + 512]
                            if gelu:
                                ev = K.rd("scalar", [hb], lambda s, h_=hsl, hb=hb: s.activation(
                                    h_, pb[hb][:, 0:512], AF.Gelu_apprx_tanh))
                            else:
                                ev = K.rd("scalar", [hb], lambda s, h_=hsl, hb=hb, hc=hc, bp=bpar: s.activation(
                                    h_, pb[hb][:, 0:512], AF.Relu, bias=b1c_sb[bp][:, hc:hc + 1]),
                                    extra=[ev_b1[bpar]])
                            a_evac[(hc, sub)] = ev
                xnh_free = (pe, pe.n)
                if l == 1:
                    b1_free[bpar] = (sc, sc.n)
                for q_ in range(2):
                    bq = 2 + q_
                    K.mm([bq], lambda t, q_=q_, bq=bq: t.matmul(
                        pb[bq][:, 0:512], ones_sb[0:1, :], mrow_sb[0:1, ts(q_, 512)],
                        start=True, stop=True), extra=[ev_mrow])
                state["ffn_mrow_free"] = (pe, pe.n)
                K.rd("scalar", [2], lambda s: s.copy(maskrep[:, 0:512], pb[2][:, 0:512]),
                     extra=[mask_free])
                ev_mask = K.rd("scalar", [3], lambda s: s.copy(maskrep[:, 512:1024],
                                                               pb[3][:, 0:512]),
                               extra=[mask_free])
                for dh in range(2):
                    for hc in range(HC):
                        w2p = hc % 6
                        ev_w2 = K.op("gpsimd", lambda g, e_=e_, hc=hc, dh=dh, w2p=w2p: g.dma_start(
                            out=w2buf[w2p][:], in_=W2_l[l][e_, ts(hc, 128), dh * 512:dh * 512 + 512]),
                            waits=[w2_free[w2p]], dma=True, dma_sem=f"d_w2{w2p}")
                        for sub in range(2):
                            for d4 in range(4):
                                yb = sub * 4 + d4
                                mmk = K.mm(ybank_ids[yb], lambda t, hc=hc, sub=sub, d4=d4, w2p=w2p, yb=yb: t.matmul(
                                    ybanks_sl[yb],
                                    w2buf[w2p][:, ts(d4, 128)],
                                    hact[:, hc * 1024 + sub * 512: hc * 1024 + sub * 512 + 512],
                                    start=(hc == 0), stop=(hc == HC - 1)),
                                    extra=([ev_w2, a_evac[(hc, sub)]] if d4 == 0 else []),
                                    first=(hc == 0), last=(hc == HC - 1),
                                    force_inc=(sub == 1 and d4 == 3))
                        w2_free[w2p] = mmk
                    for sub in range(2):
                        for d4 in range(4):
                            yb = sub * 4 + d4
                            dout = dh * 4 + d4
                            sgs = sg[:, dout * T + th * 1024 + sub * 512:
                                     dout * T + th * 1024 + sub * 512 + 512]
                            msl = maskrep[:, ts(sub, 512)]
                            if l == 0:
                                K.rd("vector", ybank_ids[yb], lambda vv, yb=yb, msl=msl: vv.scalar_tensor_tensor(
                                    ffn_tmp[:], ybanks_sl[yb], 0.0, msl, op0=ALU.add, op1=ALU.mult),
                                    extra=[ev_mask, ev_memset])
                            else:
                                K.rd("vector", ybank_ids[yb], lambda vv, yb=yb, msl=msl, dout=dout, bp=bpar: vv.scalar_tensor_tensor(
                                    ffn_tmp[:], ybanks_sl[yb], b2c_sb[bp][:, dout:dout + 1], msl,
                                    op0=ALU.add, op1=ALU.mult),
                                    extra=[ev_mask, ev_memset, ev_b2[bpar]])
                            K.op("vector", lambda vv, sgs=sgs: vv.tensor_tensor(
                                sgs, sgs, ffn_tmp[:], op=ALU.add))
                mask_free = (v, v.n)
                if l == 1:
                    b2_free[bpar] = (v, v.n)
        state[("ffn_v", l)] = (v, v.n)

    def phase_ffn_sparse(l, xgfm, hact_sp, w1buf, w2buf, ytm_sb, xntm, b2row, gelu,
                         precast=False):
        W1_ap = W1_l[l]
        W2_ap = W2_l[l]
        xntm_free = [None, None]
        ev_scat = None
        for tc in range(TC):
            par = tc % 2
            tr = None
            for dc in range(DC):
                tr = K.op("sync", lambda e, dc=dc, tc=tc, par=par: e.dma_start_transpose(
                    out=xntm[par][:, ts(dc, 128)], in_=xn_d[:, dc, ts(tc, 128)]),
                    waits=[xntm_free[par]], dma=True, dma_sem=f"d_xt{par}")
            ev_scat_p = K.op("gpsimd", lambda g, tc=tc, par=par: g.indirect_dma_start(
                out=xg_d, out_offset=IndirectOffsetOnAxis(tsl_sb[:, tc:tc + 1], 0),
                in_=xntm[par][:], in_offset=None),
                waits=[tr, state["tsl"]], dma=True, dma_sem=f"d_scat{par}")
            xntm_free[par] = ev_scat_p
            ev_scat = None
        a_evac = {}
        w1_free = [None] * 6
        w2_free = [None] * 6
        xg_free = [None, None]
        ytm_free = [None, None]
        b1_free = [None, None]
        b2_free = [None, None]
        ysl = [(p4[:, 0:512], [2]), (p4[:, 512:1024], [3]),
               (p4[:, 1024:1536], [4]), (p4[:, 1536:2048], [5]),
               (pbig[1][:, 0:512], [6]), (pbig[1][:, 512:1024], [7])]
        NSC = CAP // 128
        last_store = None
        xg_marks = [None] * E
        b1_marks = [None] * E
        b2_marks = [None] * E

        def emit_loads(e_):
            epar = e_ % 2
            m = None
            for dc in range(DC):
                m = K.op("sync", lambda e2, e_=e_, dc=dc, epar=epar: e2.dma_start_transpose(
                    out=xgfm[epar][:, ts(dc, CAP)], in_=xg_d[e_ * CAP:(e_ + 1) * CAP, ts(dc, 128)]),
                    waits=[xntm_free[0], xntm_free[1], xg_free[epar]],
                    dma=True, dma_sem=f"d_xg{epar}")
            xg_marks[e_] = m
            if l == 1:
                b1_marks[e_] = K.op("sync", lambda e, e_=e_, bp=epar: e.dma_start(
                    out=b1c_sb[bp][:], in_=b1c[e_]), waits=[b1_free[epar]],
                    dma=True, dma_sem=f"d_b{epar}")
                b2_marks[e_] = K.op("sync", lambda e, e_=e_, bp=epar: e.dma_start(
                    out=b2row[bp][:], in_=b2r[e_]), waits=[b2_free[epar]],
                    dma=True, dma_sem=f"d_b2{epar}")

        emit_loads(0)
        for e_ in range(E):
            epar = e_ % 2
            xg_m = xg_marks[e_]
            ev_b1 = b1_marks[e_]
            ev_b2 = b2_marks[e_]
            for seg in range(16):
                par4 = seg % 6
                w1src = (w1bf_d if precast else W1_ap)
                ev_w1 = K.op("gpsimd", lambda g, e_=e_, seg=seg, par4=par4, w1src=w1src: g.dma_start(
                    out=w1buf[par4][:].rearrange("p (dc h) -> p dc h", dc=DC),
                    in_=w1src[e_, :, ts(seg, 256)].rearrange("(dc p) h -> p dc h", p=128)),
                    waits=[w1_free[par4]] + ([state.get("precast_done"), state.get("precast_done2")] if precast else []),
                    dma=True, dma_sem=f"d_w1{par4}")
                for hc2 in range(2):
                    hc = seg * 2 + hc2
                    hb = hc % 2
                    for dc in range(DC):
                        mmk = K.mm([hb], lambda t, par4=par4, dc=dc, hc2=hc2, hb=hb, epar=epar: t.matmul(
                            pb[hb][:, 0:CAP],
                            w1buf[par4][:, dc * 256 + hc2 * 128: dc * 256 + hc2 * 128 + 128],
                            xgfm[epar][:, dc * CAP: dc * CAP + CAP],
                            start=(dc == 0), stop=(dc == DC - 1)),
                            extra=([ev_w1, xg_m] if dc == 0 else []),
                            first=(dc == 0), last=(dc == DC - 1))
                    if hc2 == 1:
                        w1_free[par4] = mmk
                    hsl = hact_sp[:, hc * CAP: hc * CAP + CAP]
                    if gelu:
                        a_evac[hc] = K.rd("scalar", [hb], lambda s, h_=hsl, hb=hb: s.activation(
                            h_, pb[hb][:, 0:CAP], AF.Gelu_apprx_tanh))
                    else:
                        a_evac[hc] = K.rd("scalar", [hb], lambda s, h_=hsl, hb=hb, hc=hc, bp=epar: s.activation(
                            h_, pb[hb][:, 0:CAP], AF.Relu, bias=b1c_sb[bp][:, hc:hc + 1]),
                            extra=[ev_b1])
            xg_free[epar] = (pe, pe.n)
            if l == 1:
                b1_free[epar] = (sc, sc.n)
            if e_ + 1 < E:
                emit_loads(e_ + 1)
            for hc in range(HC):
                w2p = hc % 6
                w2src = (w2bf_d if precast else W2_ap)
                ev_w2 = K.op("gpsimd", lambda g, e_=e_, hc=hc, w2p=w2p, w2src=w2src: g.dma_start(
                    out=w2buf[w2p][:], in_=w2src[e_, ts(hc, 128), :]),
                    waits=[w2_free[w2p]] + ([state.get("precast_done"), state.get("precast_done2")] if precast else []),
                    dma=True, dma_sem=f"d_w2{w2p}")
                for sc4 in range(NSC):
                    for half in range(2):
                        yi = sc4 * 2 + half
                        mmk = K.mm(ysl[yi][1], lambda t, hc=hc, sc4=sc4, half=half, w2p=w2p, yi=yi: t.matmul(
                            ysl[yi][0],
                            hact_sp[:, hc * CAP + sc4 * 128: hc * CAP + sc4 * 128 + 128],
                            w2buf[w2p][:, half * 512: half * 512 + 512],
                            start=(hc == 0), stop=(hc == HC - 1 and l == 0)),
                            extra=([ev_w2, a_evac[hc]] if (sc4 == 0 and half == 0) else [ev_w2] if half == 0 else []),
                            first=(hc == 0), last=(hc == HC - 1 and l == 0),
                            force_inc=(sc4 == NSC - 1 and half == 1))
                w2_free[w2p] = mmk
            if l == 1:
                for sc4 in range(NSC):
                    for half in range(2):
                        yi = sc4 * 2 + half
                        K.mm(ysl[yi][1], lambda t, sc4=sc4, half=half, yi=yi, bp=epar: t.matmul(
                            ysl[yi][0], ones_sb[0:1, :], b2row[bp][0:1, half * 512: half * 512 + 512],
                            start=False, stop=True),
                            extra=[ev_b2] if yi == 0 else [], first=False, last=True)
            for sc4 in range(NSC):
                for half in range(2):
                    yi = sc4 * 2 + half
                    K.rd("vector", ysl[yi][1], lambda vv, sc4=sc4, half=half, yi=yi, epar=epar: vv.tensor_copy(
                        ytm_sb[epar][:, sc4 * 1024 + half * 512: sc4 * 1024 + half * 512 + 512],
                        ysl[yi][0]), extra=[ytm_free[epar]])
            ev_yv = (v, v.n)
            last_store = K.op("sync", lambda e2, e_=e_, epar=epar: e2.dma_start(
                out=ytm_d[e_ * CAP:(e_ + 1) * CAP, :].rearrange("(sc p) d -> p sc d", p=128),
                in_=ytm_sb[epar][:].rearrange("p (sc d) -> p sc d", sc=NSC)),
                waits=[ev_yv], dma=True, dma_sem=f"d_yt{epar}")
            ytm_free[epar] = last_store
            if l == 1:
                b2_free[epar] = (v, v.n)
        ev_gat = None
        gst_free = [None, None]
        for tc in range(TC):
            gpar = tc % 2
            gm = K.op("gpsimd", lambda g, tc=tc, gpar=gpar: g.indirect_dma_start(
                out=xntm[gpar][:], out_offset=None,
                in_=ytm_d, in_offset=IndirectOffsetOnAxis(tsl_sb[:, tc:tc + 1], 0)),
                waits=[ytm_free[0], ytm_free[1], gst_free[gpar]],
                dma=True, dma_sem=f"d_gat{gpar}")
            ev_gat = K.op("sync", lambda e2, tc=tc, gpar=gpar: e2.dma_start(
                out=gtm_d[tc * 128:(tc + 1) * 128, :], in_=xntm[gpar][:]),
                waits=[gm], dma=True, dma_sem=f"d_gst{gpar}")
            gst_free[gpar] = ev_gat
        state["gst0"], state["gst1"] = gst_free[0], gst_free[1]
        ev_sg = None
        for dc in range(DC):
            ev_sg = K.op("sync", lambda e2, dc=dc: e2.dma_start_transpose(
                out=sg[:, dc * T:(dc + 1) * T], in_=gtm_d[:, ts(dc, 128)]),
                waits=[ev_gat, state.get("gst0"), state.get("gst1")], dma=True, dma_sem="d_sg")
        state[("ffn_v", l)] = ev_sg

    def phase_attn(xn_sb, av_sb, wpT_buf, attnT2, Qb2, Kb2, Vb2, qkv_bf, rrep2,
                   nmcols, rrow, x0t):
        ev_xn = K.op("sync", lambda e: e.dma_start(
            out=xn_sb[:].rearrange("p (dc t) -> p dc t", dc=DC), in_=xn_d[:, :, :]), dma=True)
        ev_g = None
        for i in range(DC * 4):
            ev_g = K.op("scalar", lambda s, i=i: s.activation(
                sg[:, ts(i, 512)], sg[:, ts(i, 512)], AF.Silu),
                waits=[state[("ffn_v", 0)]])
        for hd in range(2):
            for tc in range(TC):
                K.op("vector", lambda vv, hd=hd, tc=tc: vv.memset(
                    Vb2[hd][:, tc * 65 + 64: tc * 65 + 65], 1.0))
        ev_vones = (v, v.n)

        qkv_w = [None, None]
        qkv_wfree = [None, None]
        kones_free = [None, None]
        x0_free = [None, None]
        prev_av_tt = [None]
        tr_mark = [None, None]
        for hp in range(NH // 2):
            heads = (2 * hp, 2 * hp + 1)
            ev_qk = [None, None]
            ev_kones = [None, None]
            ev_v = [None, None]
            row64 = [[None] * TQ, [None] * TQ]
            for hd in range(2):
                h = heads[hd]
                for kk in range(3):
                    qkv_w[hd] = K.op("gpsimd", lambda g, h=h, hd=hd, kk=kk: g.dma_start(
                        out=qkv_bf[hd][:, ts(kk, 512)].rearrange("p (dc hd2) -> p dc hd2", dc=DC),
                        in_=Wqkv[kk * 16 + h].rearrange("(dc p) hd2 -> p dc hd2", p=128)),
                        waits=[qkv_wfree[hd]], dma=True, dma_sem=f"d_qkv{hd}")
            for hd in range(2):
                h = heads[hd]
                Qb, Kb, Vb = Qb2[hd], Kb2[hd], Vb2[hd]
                for which, buf, bi in (("q", Qb, h), ("k", Kb, 16 + h)):
                    koff = 0 if which == "q" else 1
                    for tau in range(TQ):
                        qpar = tau % 2
                        for dc in range(DC):
                            K.mm([qpar], lambda t, dc=dc, tau=tau, koff=koff, qpar=qpar, hd=hd: t.matmul(
                                pb[qpar][0:HD, 0:512],
                                qkv_bf[hd][:, koff * 512 + dc * HD: koff * 512 + dc * HD + HD],
                                xn_sb[:, dc * T + tau * 512: dc * T + tau * 512 + 512],
                                start=(dc == 0), stop=(dc == DC - 1)),
                                extra=([qkv_w[hd], ev_xn] if dc == 0 else []),
                                first=(dc == 0), last=(dc == DC - 1))
                        K.rd("vector", [qpar], lambda vv, buf=buf, tau=tau, bi=bi, qpar=qpar: vv.tensor_scalar(
                            buf[0:HD, ts(tau, 512)], pb[qpar][0:HD, 0:512],
                            bqkvT_sb[:, bi:bi + 1], None, op0=ALU.add),
                            extra=[ev_consts] + ([kones_free[hd]] if which == "k" else []))
                ev_qk[hd] = (v, v.n)
                ev_kones[hd] = K.op("vector", lambda vv, Kb=Kb: vv.memset(Kb[64:65, :], 1.0),
                                    waits=[ev_qk[hd]])
                for tc in range(TC):
                    vpar = tc % 2
                    for dc in range(DC):
                        K.mm([vpar], lambda t, dc=dc, tc=tc, vpar=vpar, hd=hd: t.matmul(
                            pb[vpar][:, 0:HD],
                            xn_sb[:, dc * T + tc * 128: dc * T + tc * 128 + 128],
                            qkv_bf[hd][:, 2 * 512 + dc * HD: 2 * 512 + dc * HD + HD],
                            start=(dc == 0), stop=(dc == DC - 1)),
                            extra=([qkv_w[hd], ev_xn] if dc == 0 else []),
                            first=(dc == 0), last=(dc == DC - 1))
                    K.rd("vector", [vpar], lambda vv, tc=tc, vpar=vpar, Vb=Vb: vv.tensor_copy(
                        Vb[:, tc * 65: tc * 65 + 64], pb[vpar][:, 0:HD]), extra=[ev_vones])
                ev_v[hd] = (v, v.n)
                qkv_wfree[hd] = (pe, pe.n)

            def emit_p1_scores(hd, qg, sp):
                Qb, Kb = Qb2[hd], Kb2[hd]
                for qt4 in range(4):
                    qt = qg * 4 + qt4
                    for half, bks in ((0, [2, 3]), (1, [4, 5])):
                        for kg in range(2):
                            K.mm(bks, lambda t, qt=qt, half=half, kg=kg, Qb=Qb, Kb=Kb: t.matmul(
                                p4[:, ts(half * 2 + kg, 512)], Qb[0:HD, ts(qt, 128)],
                                Kb[0:HD, ts(half * 2 + kg, 512)], start=True, stop=True),
                                extra=([ev_qk[hd]] if kg == 0 else []),
                                first=(kg == 0), last=(kg == 1))
                        K.rd("vector", bks, lambda vv, half=half: vv.reduce_max(
                            nmcols[:, half:half + 1], p4[:, half * 1024:half * 1024 + 1024],
                            axis=AX.X, negate=True))
                    col = 2 + (sp % 2) * 4 + qt4
                    K.op("vector", lambda vv, col=col: vv.tensor_reduce(
                        nmcols[:, col:col + 1], nmcols[:, 0:2], AX.X, ALU.min),
                        waits=[tr_mark[sp % 2]])

            def emit_p1_finish(hd, qg, sp):
                Qb = Qb2[hd]
                for qt4 in range(4):
                    qt = qg * 4 + qt4
                    col = 2 + (sp % 2) * 4 + qt4
                    mt = K.mm([7], lambda t, col=col: t.transpose(
                        pbig[1][0:1, 512:640], nmcols[:, col:col + 1], ident_sb[:]),
                        extra=[(v, v.n)])
                    row64[hd][qg] = K.rd("scalar", [7], lambda s, qt=qt, Qb=Qb: s.copy(
                        Qb[64:65, ts(qt, 128)], pbig[1][0:1, 512:640]), extra=[ev_kones[hd]])
                tr_mark[sp % 2] = (pe, pe.n)

            emit_p1_scores(0, 0, 0)
            emit_p1_finish(0, 0, 0)
            for step in range(2 * TQ):
                hd, qg = step % 2, step // 2
                h = heads[hd]
                Qb, Kb, Vb = Qb2[hd], Kb2[hd], Vb2[hd]
                attnT = attnT2[hd]
                rrep_sb = rrep2[hd]
                if step + 1 < 2 * TQ:
                    emit_p1_scores((step + 1) % 2, (step + 1) // 2, step + 1)
                exp_marks = []
                for kc in range(TC):
                    spar = kc % 2
                    K.mm([spar], lambda t, kc=kc, qg=qg, spar=spar, Qb=Qb, Kb=Kb: t.matmul(
                        pb[spar][:, 0:512], Kb[0:65, ts(kc, 128)], Qb[0:65, ts(qg, 512)],
                        start=True, stop=True),
                        extra=([row64[hd][qg], ev_kones[hd]] if kc == 0 else []))
                    exp_marks.append(K.rd("scalar", [spar], lambda s, kc=kc, spar=spar, attnT=attnT: s.activation(
                        attnT[:, ts(kc, 512)], pb[spar][:, 0:512], AF.Exp, scale=8.0)))
                for kc in range(TC):
                    K.mm([6], lambda t, kc=kc, attnT=attnT, Vb=Vb: t.matmul(
                        pbig[1][0:65, 0:512], Vb[:, kc * 65: kc * 65 + 65], attnT[:, ts(kc, 512)],
                        start=(kc == 0), stop=(kc == TC - 1)),
                        extra=[exp_marks[kc]] + ([ev_v[hd]] if kc == 0 else []),
                        first=(kc == 0), last=(kc == TC - 1))
                e1 = K.rd("vector", [6], lambda vv: vv.reciprocal(rrow[:], pbig[1][64:65, 0:512]))
                mmr = K.mm([7], lambda t: t.matmul(pbig[1][0:64, 512:1024], ones_sb[0:1, 0:HD],
                                                   rrow[:], start=True, stop=True), extra=[e1])
                ev_rr = K.rd("vector", [7], lambda vv, rrep_sb=rrep_sb: vv.tensor_copy(
                    rrep_sb[:], pbig[1][0:64, 512:1024]), extra=[prev_av_tt[0]])
                cc, ph = h // 2, h % 2
                bv_col = bqkvT_sb[:, 32 + h: 33 + h]
                K.rd("vector", [6], lambda vv, rrep_sb=rrep_sb: vv.tensor_tensor(
                    ffn_tmp[0:HD, :], pbig[1][0:HD, 0:512], rrep_sb[:], op=ALU.mult),
                    extra=[ev_rr])
                prev_av_tt[0] = K.op("vector", lambda vv, cc=cc, ph=ph, qg=qg, bv=bv_col: vv.tensor_scalar(
                    av_sb[ph * HD: ph * HD + HD, cc * T + qg * 512: cc * T + qg * 512 + 512],
                    ffn_tmp[0:HD, :], bv, None, op0=ALU.add))
                if step + 1 < 2 * TQ:
                    emit_p1_finish((step + 1) % 2, (step + 1) // 2, step + 1)
            kones_free[0] = (pe, pe.n)
            kones_free[1] = (pe, pe.n)
        ev_av = (v, v.n)
        x0r = x0.rearrange("(dc p) t -> p dc t", p=128)
        wp_free = [None, None]
        for pc in range(DC):
            wpar = pc % 2
            ev_wp = K.op("gpsimd", lambda g, pc=pc, wpar=wpar: g.dma_start(
                out=wpT_buf[wpar][:].rearrange("p (cc q) -> p cc q", cc=DC),
                in_=WpT[:, ts(pc, 128)].rearrange("(cc p) q -> p cc q", p=128)),
                waits=[wp_free[wpar]], dma=True, dma_sem=f"d_wp{wpar}")
            for tau in range(TQ):
                spar = (pc * TQ + tau) % 2
                ev_x0 = K.op("sync", lambda e, pc=pc, tau=tau, spar=spar: e.dma_start(
                    out=x0t[spar][:], in_=x0r[:, pc, ts(tau, 512)]),
                    waits=[x0_free[spar]], dma=True, dma_sem=f"d_x{spar}")
                for cc in range(DC):
                    K.mm([spar], lambda t, cc=cc, tau=tau, spar=spar, wpar=wpar: t.matmul(
                        pb[spar][:, 0:512],
                        wpT_buf[wpar][:, ts(cc, 128)],
                        av_sb[:, cc * T + tau * 512: cc * T + tau * 512 + 512],
                        start=(cc == 0), stop=(cc == DC - 1)),
                        extra=([ev_av, ev_wp] if cc == 0 else []),
                        first=(cc == 0), last=(cc == DC - 1))
                K.rd("vector", [spar], lambda vv, pc=pc, tau=tau, spar=spar: vv.scalar_tensor_tensor(
                    ffn_tmp[:], pb[spar][:, 0:512], bp_sb[:, pc:pc + 1],
                    sg[:, pc * T + tau * 512: pc * T + tau * 512 + 512],
                    op0=ALU.add, op1=ALU.mult), extra=[ev_g])
                ev_xt = K.op("vector", lambda vv, spar=spar: vv.tensor_tensor(
                    x0t[spar][:], x0t[spar][:], ffn_tmp[:], op=ALU.add), waits=[ev_x0])
                x0_free[spar] = K.op("sync", lambda e, pc=pc, tau=tau, spar=spar: e.dma_start(
                    out=x_d[:, pc, ts(tau, 512)], in_=x0t[spar][:]), waits=[ev_xt],
                    dma=True, dma_sem=f"d_xs{spar}")
            wp_free[wpar] = (pe, pe.n)

    def phase_l1small(xn_sb, pre, xr_bf, oow_bf, vpow_bf, opw_buf, vpiw_bf,
                      sq_t, gsil_t, s1_t, x1t, vqs):
        (params_sb, opb_sb, om_u, om_ut, om, om_n, om2, om3, Gt, G_bf, vq_row, vq_sq,
         vq_n, cb_sq, cb_n, cb_sc, vqn_rep, vq_gc, vq_ind, code_row, code_col,
         omix_col, vqmix_col, vqc_cols) = vqs
        ev_xn = K.op("sync", lambda e: e.dma_start(
            out=xn_sb[:].rearrange("p (dc t) -> p dc t", dc=DC), in_=xn_d[:, :, :]), dma=True)
        ev_oow = K.op("gpsimd", lambda g: g.dma_start(out=oow_bf[:], in_=oowT), dma=True)
        ev_vpow = K.op("gpsimd", lambda g: g.dma_start(out=vpow_bf[:], in_=vpowT), dma=True)
        ev_vpiw = K.op("sync", lambda e: e.dma_start(
            out=vpiw_bf[:].rearrange("p (dc c) -> p dc c", dc=DC),
            in_=vpiwT.rearrange("(dc p) c -> p dc c", p=128)), dma=True)
        ev_hm = state["hm"]
        ev_opb = K.op("sync", lambda e: e.dma_start(out=opb_sb[:], in_=opb), dma=True)
        opw_free = [None, None, None]
        for dcb in range(DC):
            par = dcb % 3
            ev_opw = K.op("gpsimd", lambda g, dcb=dcb, par=par: g.dma_start(
                out=opw_buf[par][:], in_=opwT[ts(dcb, 128), :]), waits=[opw_free[par]],
                dma=True, dma_sem=f"d_opw{par}")
            for seg_ in range(4):
                mk = K.mm([seg_], lambda t, dcb=dcb, seg_=seg_, par=par: t.matmul(
                    pb[seg_][0:1, 0:504], hm_bf[:, dcb:dcb + 1], opw_buf[par][:, ts(seg_, 504)],
                    start=(dcb == 0), stop=(dcb == DC - 1)),
                    extra=[ev_opw, ev_hm], first=(dcb == 0), last=True)
            opw_free[par] = mk
        for seg_ in range(4):
            K.rd("scalar", [seg_], lambda s, seg_=seg_: s.copy(
                params_sb[0:1, ts(seg_, 504)], pb[seg_][0:1, 0:504]))
        ev_p = K.op("vector", lambda vv: vv.tensor_tensor(
            params_sb[:], params_sb[:], opb_sb[:], op=ALU.add),
            waits=[(sc, sc.n), ev_opb])
        ev_mz = K.op("vector", lambda vv: vv.memset(om_u[:], 0.0))
        off = 0
        ev_tri = None
        for r in range(63):
            ln_ = 63 - r
            ev_tri = K.op("sync", lambda e, r=r, off=off, ln_=ln_: e.dma_start(
                out=om_u[r:r + 1, r + 1:64], in_=params_sb[0:1, off:off + ln_]),
                waits=[ev_p, ev_mz], dma=True)
            off += ln_
        mt = K.mm([4], lambda t: t.transpose(pbig[0][0:64, 0:64], om_u[:], ident_sb[0:64, 0:64]),
                  extra=[ev_tri])
        ev_ut = K.rd("scalar", [4], lambda s: s.copy(om_ut[:], pbig[0][0:64, 0:64]))
        e_om = K.op("vector", lambda vv: vv.tensor_tensor(om[:], om_u[:], om_ut[:], op=ALU.subtract),
                    waits=[ev_ut, ev_tri])
        e_nom = K.op("vector", lambda vv: vv.tensor_tensor(om_n[:], om_ut[:], om_u[:], op=ALU.subtract))
        m2 = K.mm([4], lambda t: t.matmul(pbig[0][0:64, 64:128], om_n[:], om[:], start=True, stop=True),
                  extra=[e_nom])
        ev_o2 = K.rd("scalar", [4], lambda s: s.copy(om2[:], pbig[0][0:64, 64:128]))
        m3 = K.mm([4], lambda t: t.matmul(pbig[0][0:64, 128:192], om2[:], om[:], start=True, stop=True),
                  extra=[ev_o2])
        ev_o3 = K.rd("scalar", [4], lambda s: s.copy(om3[:], pbig[0][0:64, 128:192]))
        K.op("vector", lambda vv: vv.tensor_tensor(Gt[:], om[:], ident_sb[0:64, 0:64], op=ALU.add),
             waits=[ev_o3])
        K.op("vector", lambda vv: vv.scalar_tensor_tensor(Gt[:], om2[:], 0.5, Gt[:],
                                                          op0=ALU.mult, op1=ALU.add))
        K.op("vector", lambda vv: vv.scalar_tensor_tensor(Gt[:], om3[:], 1.0 / 6.0, Gt[:],
                                                          op0=ALU.mult, op1=ALU.add))
        ev_G = K.op("vector", lambda vv: vv.tensor_copy(G_bf[:], Gt[:]))
        ev_xr = None
        for tau in range(TQ):
            mm_ = K.mm([6, 7], lambda t, tau=tau: t.matmul(
                pbig[1][0:64, 0:512], G_bf[:], xn_sb[0:64, ts(tau, 512)], start=True, stop=True),
                extra=[ev_G, ev_xn])
            ev_xr = K.rd("scalar", [6, 7], lambda s, tau=tau: s.copy(xr_bf[0:64, ts(tau, 512)],
                                                                     pbig[1][0:64, 0:512]))
        for dcb in range(DC):
            K.mm([5], lambda t, dcb=dcb: t.matmul(
                pbig[0][0:1, 512:576], hm_cols[:, dcb:dcb + 1], vpiw_bf[:, ts(dcb, HD)],
                start=(dcb == 0), stop=(dcb == DC - 1)),
                extra=[ev_vpiw, ev_hm], first=(dcb == 0), last=(dcb == DC - 1))
        ev_vq0 = K.rd("vector", [5], lambda vv: vv.scalar_tensor_tensor(
            vq_row[:], pbig[0][0:1, 512:576], 1.0 / T, vpib_sb[:],
            op0=ALU.mult, op1=ALU.add))
        ev_sq = K.op("scalar", lambda s: s.activation(vq_sq[:], vq_row[:], AF.Square),
                     waits=[ev_vq0])
        e1 = K.op("vector", lambda vv: vv.tensor_reduce(vq_n[0:1, 0:1], vq_sq[:], AX.X, ALU.add),
                  waits=[ev_sq])
        e2 = K.op("scalar", lambda s: s.activation(vq_n[0:1, 1:2], vq_n[0:1, 0:1], AF.Sqrt),
                  waits=[e1])
        K.op("vector", lambda vv: vv.tensor_scalar(vq_n[0:1, 2:3], vq_n[0:1, 1:2], 1e-12, None, op0=ALU.max),
             waits=[e2])
        K.op("vector", lambda vv: vv.reciprocal(vq_n[0:1, 3:4], vq_n[0:1, 2:3]))
        ev_vqn = K.op("vector", lambda vv: vv.tensor_scalar(vq_row[:], vq_row[:], vq_n[0:1, 3:4], None, op0=ALU.mult))
        ev_cbsq = K.op("scalar", lambda s: s.activation(cb_sq[:], cb_sb[:], AF.Square),
                       waits=[ev_consts])
        K.op("vector", lambda vv: vv.tensor_reduce(cb_n[:, 0:1], cb_sq[:, 0:64], AX.X, ALU.add),
             waits=[ev_cbsq])
        e3 = K.op("vector", lambda vv: vv.tensor_reduce(cb_n[:, 1:2], cb_sq[:, 64:128], AX.X, ALU.add))
        e4 = K.op("scalar", lambda s: s.activation(cb_n[:, 2:4], cb_n[:, 0:2], AF.Sqrt),
                  waits=[e3])
        K.op("vector", lambda vv: vv.tensor_scalar(cb_n[:, 2:4], cb_n[:, 2:4], 1e-12, None, op0=ALU.max),
             waits=[e4])
        ev_rcb = K.op("vector", lambda vv: vv.reciprocal(cb_n[:, 4:6], cb_n[:, 2:4]))
        mmn = K.mm([5], lambda t: t.matmul(pbig[0][:, 576:640], ones_sb[0:1, :], vq_row[:],
                                           start=True, stop=True), extra=[ev_vqn])
        ev_vqr = K.rd("scalar", [5], lambda s: s.copy(vqn_rep[:], pbig[0][:, 576:640]))
        K.op("vector", lambda vv: vv.tensor_tensor(cb_sc[:, 0:64], cb_sb[:, 0:64], vqn_rep[:], op=ALU.mult),
             waits=[ev_vqr])
        K.op("vector", lambda vv: vv.tensor_tensor(cb_sc[:, 64:128], cb_sb[:, 64:128], vqn_rep[:], op=ALU.mult))
        K.op("vector", lambda vv: vv.tensor_reduce(cb_n[:, 6:7], cb_sc[:, 0:64], AX.X, ALU.add))
        K.op("vector", lambda vv: vv.tensor_reduce(cb_n[:, 7:8], cb_sc[:, 64:128], AX.X, ALU.add))
        ev_scr = K.op("vector", lambda vv: vv.tensor_tensor(cb_n[:, 6:8], cb_n[:, 6:8], cb_n[:, 4:6], op=ALU.mult),
                      waits=[ev_rcb])
        ev_m1 = K.op("vector", lambda vv: vv.tensor_reduce(cb_n[:, 8:9], cb_n[:, 6:8], AX.X, ALU.max))
        mtx = K.mm([4], lambda t: t.transpose(pbig[0][0:1, 0:128], cb_n[:, 8:9], ident_sb[:]),
                   extra=[ev_m1])
        ev_mrow = K.rd("scalar", [4], lambda s: s.copy(vq_n[0:1, 8:136], pbig[0][0:1, 0:128]))
        ev_gmax = K.op("vector", lambda vv: vv.tensor_reduce(vq_n[0:1, 4:5], vq_n[0:1, 8:136], AX.X, ALU.max),
                       waits=[ev_mrow])
        mmg = K.mm([4], lambda t: t.matmul(pbig[0][:, 128:129], ones_sb[0:1, :], vq_n[0:1, 4:5],
                                           start=True, stop=True), extra=[ev_gmax])
        ev_gcol = K.rd("scalar", [4], lambda s: s.copy(vq_gc[:], pbig[0][:, 128:129]))
        ev_ind2 = K.op("vector", lambda vv: vv.tensor_scalar(vq_ind[:], cb_n[:, 6:8], vq_gc[:], None, op0=ALU.is_equal),
                       waits=[ev_gcol, ev_scr])
        for j in range(2):
            K.mm([4], lambda t, j=j: t.matmul(pbig[0][0:1, 192:256], vq_ind[:, j:j + 1],
                                              cb_sb[:, ts(j, 64)], start=(j == 0), stop=(j == 1)),
                 extra=[ev_ind2], first=(j == 0), last=(j == 1))
        ev_coder = K.rd("scalar", [4], lambda s: s.copy(code_row[:], pbig[0][0:1, 192:256]))
        mtc = K.mm([4], lambda t: t.transpose(pbig[0][0:64, 256:257], code_row[:], ident_sb[0:1, 0:1]),
                   extra=[ev_coder])
        ev_codec = K.rd("scalar", [4], lambda s: s.copy(code_col[:], pbig[0][0:64, 256:257]))
        for i, dst in ((0, omix_col), (1, vqmix_col)):
            K.mm([4], lambda t, i=i: t.matmul(pbig[0][:, 300 + i:301 + i], ones_sb[0:1, :],
                                              mixes_sb[0:1, i:i + 1], start=True, stop=True),
                 extra=[ev_consts])
            K.rd("scalar", [4], lambda s, dst=dst, i=i: s.copy(dst[:], pbig[0][:, 300 + i:301 + i]))
        ev_mix = (sc, sc.n)
        for pc in range(DC):
            K.mm([4], lambda t, pc=pc: t.matmul(pbig[0][:, 320 + pc:321 + pc],
                                                vpow_bf[:, ts(pc, 128)], code_col[:],
                                                start=True, stop=True),
                 extra=[ev_vpow, ev_codec], first=(pc == 0), last=(pc == DC - 1))
        K.rd("vector", [4], lambda vv: vv.tensor_tensor(
            vqc_cols[:], pbig[0][:, 320:328], vpobc_sb[:], op=ALU.add), extra=[ev_mix])
        ev_vqc = K.op("vector", lambda vv: vv.tensor_scalar(vqc_cols[:], vqc_cols[:], vqmix_col[:], None, op0=ALU.mult))
        lie_free = [None, None]
        x1_free = [None, None]
        ss_read = None
        def emit_prenorm(tau):
            for pc in range(DC):
                lpar = pc % 2
                mmk = K.mm([lpar], lambda t, pc=pc, tau=tau, lpar=lpar: t.matmul(
                    pb[lpar][:, 0:512], oow_bf[:, ts(pc, 128)], xr_bf[0:64, ts(tau, 512)],
                    start=True, stop=True), extra=[ev_oow, lie_free[lpar]])
                K.rd("vector", [lpar], lambda vv, pc=pc, lpar=lpar: vv.tensor_scalar(
                    ffn_tmp[:], pb[lpar][:, 0:512], oobc_sb[:, pc:pc + 1], omix_col[:],
                    op0=ALU.add, op1=ALU.mult))
                prt = pre[:, pc * T + tau * 512: pc * T + tau * 512 + 512]
                ev_pre = K.op("vector", lambda vv, pc=pc, tau=tau, prt=prt: vv.scalar_tensor_tensor(
                    prt, ffn_tmp[:], vqc_cols[:, pc:pc + 1],
                    xn_sb[:, pc * T + tau * 512: pc * T + tau * 512 + 512],
                    op0=ALU.add, op1=ALU.add))
                lie_free[lpar] = ev_pre
                ev_sq = K.op("scalar", lambda s, prt=prt: s.activation(sq_t[:], prt, AF.Square),
                             waits=[ev_pre, state.get("l1_sq_free")])
                state["l1_sq_free"] = K.mm(
                    [2], lambda t, pc=pc: t.matmul(pb[2][0:1, 0:512], ones_sb[:, 0:1], sq_t[:],
                                                   start=(pc == 0), stop=(pc == DC - 1)),
                    extra=[ev_sq], first=(pc == 0), last=True)

        emit_prenorm(0)
        for tau in range(TQ):
            e1 = K.rd("vector", [2], lambda vv: vv.tensor_scalar(
                rowC[0:1, 0:512], pb[2][0:1, 0:512], 1.0 / D, RMS_EPS, op0=ALU.mult, op1=ALU.add))
            e2 = K.op("scalar", lambda s: s.activation(rowC[0:1, 0:512], rowC[0:1, 0:512],
                                                       AF.Sqrt), waits=[e1])
            e3 = K.op("vector", lambda vv: vv.reciprocal(rowC[0:1, 0:512], rowC[0:1, 0:512]),
                      waits=[e2])
            if tau + 1 < TQ:
                emit_prenorm(tau + 1)
            K.mm([3], lambda t: t.matmul(pb[3][:, 0:512], ones_sb[0:1, :], rowC[0:1, 0:512],
                                         start=True, stop=True), extra=[e3])
            for pc in range(DC):
                xpar = pc % 2
                prt = pre[:, pc * T + tau * 512: pc * T + tau * 512 + 512]
                ev_xl = K.op("sync", lambda e, pc=pc, tau=tau, xpar=xpar: e.dma_start(
                    out=x1t[xpar][:], in_=x_d[:, pc, ts(tau, 512)]),
                    waits=[x1_free[xpar]], dma=True, dma_sem=f"d_x{xpar}")
                ev_sil = K.op("scalar", lambda s, pc=pc, tau=tau: s.activation(
                    gsil_t[:], sg[:, pc * T + tau * 512: pc * T + tau * 512 + 512], AF.Silu),
                    waits=[state[("ffn_v", 1)], state.get("l1_gsil_free")])
                ev_s1 = K.rd("vector", [3], lambda vv, prt=prt: vv.tensor_tensor(
                    s1_t[:], prt, pb[3][:, 0:512], op=ALU.mult))
                ev_s1b = K.op("vector", lambda vv, pc=pc: vv.scalar_tensor_tensor(
                    s1_t[:], s1_t[:], nwc_sb[:, pc:pc + 1], gsil_t[:], op0=ALU.mult, op1=ALU.mult),
                    waits=[ev_sil])
                state["l1_gsil_free"] = ev_s1b
                ev_fin = K.op("vector", lambda vv, xpar=xpar: vv.tensor_tensor(
                    x1t[xpar][:], x1t[xpar][:], s1_t[:], op=ALU.add), waits=[ev_xl])
                x1_free[xpar] = K.op("sync", lambda e, pc=pc, tau=tau, xpar=xpar: e.dma_start(
                    out=out_fm[:, pc, ts(tau, 512)], in_=x1t[xpar][:]), waits=[ev_fin],
                    dma=True, dma_sem=f"d_xs{xpar}")

    with nc.sbuf_tensor("xs0", [128, DC * T], F32) as xs0, \
         nc.sbuf_tensor("xn0", [128, DC * T], BF16) as xn0, \
         nc.sbuf_tensor("zero_sb", [128, 1024], BF16) as zero_sb, \
         nc.sbuf_tensor("lnsq0", [128, 512], F32) as lnsq0, \
         nc.sbuf_tensor("lnsq1", [128, 512], F32) as lnsq1, \
         nc.sbuf_tensor("rowA0", [1, 512], F32) as rowA0, \
         nc.sbuf_tensor("rowB0", [1, 512], F32) as rowB0:
        ev_zmem = K.op("vector", lambda vv: vv.memset(zero_sb[:], 0.0))
        for i in range(E * CAP // 128):
            K.op("sync", lambda e, i=i: e.dma_start(
                out=xg_d[i * 128:(i + 1) * 128, :], in_=zero_sb[:]), waits=[ev_zmem], dma=True)
        phase_ln(0, x0.rearrange("(dc p) t -> p dc t", p=128), xs0, xn0, False, (lnsq0, lnsq1), rowA0, rowB0)
        K.bar()

    with contextlib.ExitStack() as stf0:
        sbf = lambda n, sh, dt, td=False: stf0.enter_context(
            nc.sbuf_tensor(n, sh, dt, is_transpose_dest=td))
        xgfm0 = (sbf("xgfm00", [128, DC * CAP], BF16, True), sbf("xgfm01", [128, DC * CAP], BF16, True))
        hact0 = sbf("hact0", [128, HC * CAP], BF16)
        w1a = tuple(sbf(f"w1a{i}", [128, DC * 256], BF16) for i in range(6))
        w2a = tuple(sbf(f"w2a{i}", [128, 1024], BF16) for i in range(6))
        ytma = (sbf("ytma0", [128, (CAP // 128) * 1024], BF16), sbf("ytma1", [128, (CAP // 128) * 1024], BF16))
        xntma = (sbf("xntma0", [128, 1024], BF16, True), sbf("xntma1", [128, 1024], BF16, True))
        b2ra = (sbf("b2ra0", [1, D], F32), sbf("b2ra1", [1, D], F32))
        phase_ffn_sparse(0, xgfm0, hact0, w1a, w2a, ytma, xntma, b2ra, True)
        K.bar()
    if sg_d is not None:
        K.op("sync", lambda e: e.dma_start(out=sg_d, in_=sg[:]), dma=True)
        K.bar()

    with contextlib.ExitStack() as st3:
        sb3 = lambda n, sh, dt, td=False: st3.enter_context(
            nc.sbuf_tensor(n, sh, dt, is_transpose_dest=td))
        xna = sb3("xna", [128, DC * T], BF16)
        av_sb = sb3("av", [128, DC * T], BF16)
        wpT0 = sb3("wpT0", [128, DC * 128], BF16)
        wpT1 = sb3("wpT1", [128, DC * 128], BF16)
        attnTa = sb3("attnTa", [128, TC * 512], BF16)
        attnTb = sb3("attnTb", [128, TC * 512], BF16)
        Qba = sb3("Qba", [128, T], BF16)
        Qbb = sb3("Qbb", [128, T], BF16)
        Kba = sb3("Kba", [128, T], BF16)
        Kbb = sb3("Kbb", [128, T], BF16)
        Vba = sb3("Vba", [128, TC * 65], BF16)
        Vbb = sb3("Vbb", [128, TC * 65], BF16)
        qkvb0 = sb3("qkvb0", [128, 3 * 512], BF16)
        qkvb1 = sb3("qkvb1", [128, 3 * 512], BF16)
        rrepa = sb3("rrepa", [HD, 512], F32)
        rrepb = sb3("rrepb", [HD, 512], F32)
        nmcols = sb3("nmcols", [128, 10], F32)
        rrow = sb3("rrow", [1, 512], F32)
        x0t0 = sb3("x0t0", [128, 512], F32)
        x0t1 = sb3("x0t1", [128, 512], F32)
        phase_attn(xna, av_sb, (wpT0, wpT1), (attnTa, attnTb), (Qba, Qbb),
                   (Kba, Kbb), (Vba, Vbb), (qkvb0, qkvb1), (rrepa, rrepb),
                   nmcols, rrow, (x0t0, x0t1))
        K.bar()

    with nc.sbuf_tensor("xs1", [128, DC * T], F32) as xs1, \
         nc.sbuf_tensor("xn1", [128, DC * T], BF16) as xn1, \
         nc.sbuf_tensor("lnsq0b", [128, 512], F32) as lnsq0b, \
         nc.sbuf_tensor("lnsq1b", [128, 512], F32) as lnsq1b, \
         nc.sbuf_tensor("rowA1", [1, 512], F32) as rowA1, \
         nc.sbuf_tensor("rowB1", [1, 512], F32) as rowB1:
        phase_ln(1, x_d, xs1, xn1, True, (lnsq0b, lnsq1b), rowA1, rowB1)
        K.bar()

    with contextlib.ExitStack() as stf1:
        sbf = lambda n, sh, dt, td=False: stf1.enter_context(
            nc.sbuf_tensor(n, sh, dt, is_transpose_dest=td))
        xgfm1 = (sbf("xgfm10", [128, DC * CAP], BF16, True), sbf("xgfm11", [128, DC * CAP], BF16, True))
        hact1 = sbf("hact1", [128, HC * CAP], BF16)
        w1b = tuple(sbf(f"w1b{i}", [128, DC * 256], BF16) for i in range(6))
        w2b = tuple(sbf(f"w2b{i}", [128, 1024], BF16) for i in range(6))
        ytmb = (sbf("ytmb0", [128, (CAP // 128) * 1024], BF16), sbf("ytmb1", [128, (CAP // 128) * 1024], BF16))
        xntmb = (sbf("xntmb0", [128, 1024], BF16, True), sbf("xntmb1", [128, 1024], BF16, True))
        b2rb = (sbf("b2rb0", [1, D], F32), sbf("b2rb1", [1, D], F32))
        phase_ffn_sparse(1, xgfm1, hact1, w1b, w2b, ytmb, xntmb, b2rb, False, precast=False)
        K.bar()
    if sg_d1 is not None:
        K.op("sync", lambda e: e.dma_start(out=sg_d1, in_=sg[:]), dma=True)
        K.bar()

    with contextlib.ExitStack() as st6:
        sb6 = lambda n, sh, dt: st6.enter_context(nc.sbuf_tensor(n, sh, dt))
        xnb = sb6("xnb", [128, DC * T], BF16)
        pre = sb6("pre", [128, DC * T], BF16)
        xr_bf = sb6("xr", [64, T], BF16)
        oow_bf = sb6("oowb", [HD, D], BF16)
        vpow_bf = sb6("vpowb", [HD, D], BF16)
        opwb0 = sb6("opwb0", [128, 2016], BF16)
        opwb1 = sb6("opwb1", [128, 2016], BF16)
        opwb2 = sb6("opwb2", [128, 2016], BF16)
        vpiw_bf = sb6("vpiwb", [128, DC * HD], F32)
        params_sb = sb6("params", [1, 2016], F32)
        om_u = sb6("om_u", [64, 64], F32)
        om_ut = sb6("om_ut", [64, 64], F32)
        om = sb6("om", [64, 64], F32)
        om_n = sb6("om_n", [64, 64], F32)
        om2 = sb6("om2", [64, 64], F32)
        om3 = sb6("om3", [64, 64], F32)
        Gt = sb6("Gt", [64, 64], F32)
        G_bf = sb6("G_bf", [64, 64], BF16)
        vq_row = sb6("vq_row", [1, 64], F32)
        vq_sq = sb6("vq_sq", [1, 64], F32)
        vq_n = sb6("vq_n", [1, 256], F32)
        cb_sq = sb6("cb_sq", [128, 128], F32)
        cb_n = sb6("cb_n", [128, 9], F32)
        cb_sc = sb6("cb_sc", [128, 128], F32)
        vqn_rep = sb6("vqn_rep", [128, 64], F32)
        vq_gc = sb6("vq_gc", [128, 1], F32)
        vq_ind = sb6("vq_ind", [128, 2], F32)
        code_row = sb6("code_row", [1, 64], F32)
        code_col = sb6("code_col", [64, 1], BF16)
        omix_col = sb6("omix", [128, 1], F32)
        vqmix_col = sb6("vqmix", [128, 1], F32)
        vqc_cols = sb6("vqc_cols", [128, DC], F32)
        sq_t = sb6("sq_t", [128, 512], F32)
        gsil_t = sb6("gsil_t", [128, 512], F32)
        s1_t = sb6("s1_t", [128, 512], F32)
        x1t0 = sb6("x1t0", [128, 512], F32)
        x1t1 = sb6("x1t1", [128, 512], F32)
        vqs = (params_sb, opb_sb, om_u, om_ut, om, om_n, om2, om3, Gt, G_bf, vq_row,
               vq_sq, vq_n, cb_sq, cb_n, cb_sc, vqn_rep, vq_gc, vq_ind,
               code_row, code_col, omix_col, vqmix_col, vqc_cols)
        phase_l1small(xnb, pre, xr_bf, oow_bf, vpow_bf, (opwb0, opwb1, opwb2), vpiw_bf,
                      sq_t, gsil_t, s1_t, (x1t0, x1t1), vqs)

    K.bar()

    with nc.Block() as block:
        K.replay(block)
    stk.close()
    return nc


_NC_CACHE = {}


def get_nc(dbg=()):
    key = tuple(sorted(dbg))
    if key not in _NC_CACHE:
        _NC_CACHE[key] = build(key)
    return _NC_CACHE[key]


def host_inputs(inputs):
    f = np.float32
    a = lambda x: np.ascontiguousarray(x, dtype=f)
    g = {}
    g["ident"] = np.eye(128, dtype=f)
    g["ones"] = np.ones((128, 128), dtype=f)
    g["Wqkv"] = a(inputs["L0_attn_Wqkv"])
    bqkv = np.asarray(inputs["L0_attn_bqkv"], dtype=f)
    g["bqkvT"] = a(bqkv.T)
    g["bqkv_flat"] = a(bqkv.reshape(1, -1))
    g["WpT"] = a(np.asarray(inputs["L0_attn_Wp"], dtype=f).T)
    g["bp_cols"] = a(np.asarray(inputs["L0_attn_bp"], dtype=f).reshape(8, 128).T)
    g["Wg0"] = a(inputs["L0_ffn_Wg"])
    g["bg0"] = a(np.asarray(inputs["L0_ffn_bg"], dtype=f).reshape(1, -1))
    g["W1_0"] = a(inputs["L0_ffn_W1"])
    g["W2_0"] = a(inputs["L0_ffn_W2"])
    g["Wg1"] = a(np.asarray(inputs["L1_ffn_gate_w"], dtype=f).T)
    g["bg1"] = a(np.asarray(inputs["L1_ffn_gate_b"], dtype=f).reshape(1, -1))
    g["W1_1"] = a(np.asarray(inputs["L1_ffn_w1"], dtype=f).transpose(0, 2, 1))
    g["W2_1"] = a(np.asarray(inputs["L1_ffn_w2"], dtype=f).transpose(0, 2, 1))
    g["b1c"] = a(np.asarray(inputs["L1_ffn_b1"], dtype=f).reshape(8, 32, 128).transpose(0, 2, 1))
    g["b2c"] = a(np.asarray(inputs["L1_ffn_b2"], dtype=f).reshape(8, 8, 128).transpose(0, 2, 1))
    g["opwT"] = a(np.asarray(inputs["L1_attn_omega_proj_w"], dtype=f).T)
    g["opb"] = a(np.asarray(inputs["L1_attn_omega_proj_b"], dtype=f).reshape(1, -1))
    g["oowT"] = a(np.asarray(inputs["L1_attn_omega_out_w"], dtype=f).T)
    g["oobc"] = a(np.asarray(inputs["L1_attn_omega_out_b"], dtype=f).reshape(8, 128).T)
    g["vpiwT"] = a(np.asarray(inputs["L1_attn_vq_proj_in_w"], dtype=f).T)
    g["vpib"] = a(np.asarray(inputs["L1_attn_vq_proj_in_b"], dtype=f).reshape(1, -1))
    g["vpowT"] = a(np.asarray(inputs["L1_attn_vq_proj_out_w"], dtype=f).T)
    g["vpobc"] = a(np.asarray(inputs["L1_attn_vq_proj_out_b"], dtype=f).reshape(8, 128).T)
    g["lt128"] = np.triu(np.ones((128, 128), dtype=f))
    g["ecm1"] = np.tile((np.arange(8, dtype=f) * 384.0 - 1.0)[None, :], (128, 1))
    g["b2r"] = a(np.asarray(inputs["L1_ffn_b2"], dtype=f).reshape(8, 1, 1024))
    cb = np.asarray(inputs["L1_attn_codebook"], dtype=f)
    g["cbp"] = a(np.concatenate([cb[0:128], cb[128:256]], axis=1))
    g["mixes"] = np.array([[float(inputs["L1_attn_omega_mix"]),
                            float(inputs["L1_attn_vq_mix"])]], dtype=f)
    g["nwc"] = a(np.asarray(inputs["L1_attn_norm_w"], dtype=f).reshape(8, 128).T)
    return g


def kernel(**inputs):
    nc = get_nc()
    shared = host_inputs(inputs)
    x = np.asarray(inputs["L0_attn_x"], dtype=np.float32)
    in_maps = []
    for b in range(NCORES):
        m = dict(shared)
        m["x0"] = np.ascontiguousarray(x[b].T)
        in_maps.append(m)
    res = run_bass_kernel_spmd(nc, in_maps, core_ids=list(range(NCORES)))
    outs = []
    for b in range(NCORES):
        o_ = np.asarray(res.results[b]["out_fm"])
        outs.append(o_.transpose(1, 0, 2).reshape(D, T).T)
    return np.stack(outs, axis=0)


# version 20 (final)
# speedup vs baseline: 1.2030x; 1.0085x over previous
import contextlib
import numpy as np
import concourse.bass as bass
import concourse.mybir as mybir
from concourse.bass import IndirectOffsetOnAxis
from concourse.bass_utils import run_bass_kernel_spmd

F32 = mybir.dt.float32
BF16 = mybir.dt.bfloat16
AF = mybir.ActivationFunctionType
ALU = mybir.AluOpType
AX = mybir.AxisListType

D, T, H, E, NH, HD = 1024, 2048, 4096, 8, 16, 64
DC, TQ, TC, TH, HC = 8, 4, 16, 2, 32
NCORES = 8
RMS_EPS = 1e-6
LN_EPS = 1e-5
CAP = 384
I32 = mybir.dt.int32


def ts(i, n):
    return slice(i * n, (i + 1) * n)


class Ctr:
    def __init__(self, h):
        self.h = h
        self.n = 0


class B:
    ENGINES = ("tensor", "vector", "scalar", "gpsimd", "sync")

    def __init__(self, nc, stk):
        self.nc = nc
        self.stk = stk
        self.q = {e: [] for e in self.ENGINES}
        self.ctr = {}
        for name in ("pe", "v", "sc", "gp", "dg", "ds"):
            self.ctr[name] = Ctr(stk.enter_context(nc.semaphore("s_" + name)))
        self.eng2ctr = {"tensor": "pe", "vector": "v", "scalar": "sc",
                        "gpsimd": "gp", "sync": None}
        self.serial_dma = {id(self.ctr["dg"]), id(self.ctr["ds"])}
        self.bank = {i: {"w": None, "r": {}} for i in range(8)}

    def dsem(self, name):
        if name not in self.ctr:
            self.ctr[name] = Ctr(self.stk.enter_context(self.nc.semaphore(name)))
        return self.ctr[name]

    def op(self, eng, builder, waits=(), inc=None, dma=False, dma_sem=None):
        ww = {}
        waits = list(waits)
        if eng in ("vector", "scalar") and not dma:
            cs = self.ctr[self.eng2ctr[eng]]
            if cs.n > 0:
                waits.append((cs, cs.n))
        for w in waits:
            if w is None:
                continue
            c, v = w
            if v is None:
                continue
            if id(c) in self.serial_dma:
                v = c.n
            ww[id(c)] = (c.h, max(v, ww[id(c)][1]) if id(c) in ww else v)
        ww = list(ww.values())
        incs = []
        if dma:
            if dma_sem is not None:
                c = self.dsem(dma_sem)
            else:
                c = self.ctr["dg" if eng == "gpsimd" else "ds"]
            c.n += 16
            incs = [(c.h, 16)]
            mark = (c, c.n)
        elif inc is False:
            mark = None
        else:
            c = self.ctr[self.eng2ctr[eng]]
            assert c is not None
            c.n += 1
            incs = [(c.h, 1)]
            mark = (c, c.n)

        def go(engine, ww=ww, incs=incs, builder=builder):
            for h, v in ww:
                engine.wait_ge(h, v)
            ins = builder(engine)
            for h, k in incs:
                ins.then_inc(h, k)
        self.q[eng].append(go)
        return mark

    def wait_only(self, eng, waits):
        ww = []
        for w in waits:
            if w is None or w[1] is None:
                continue
            vv_ = w[0].n if id(w[0]) in self.serial_dma else w[1]
            ww.append((w[0].h, vv_))

        def go(engine, ww=ww):
            for h, v in ww:
                engine.wait_ge(h, v)
        self.q[eng].append(go)

    def marks_all(self):
        return [(c, c.n) for c in self.ctr.values()]

    def bar(self):
        tgt = self.marks_all()
        for eng in self.ENGINES:
            self.wait_only(eng, tgt)
        for b in self.bank.values():
            b["w"] = None
            b["r"] = {}

    def bank_waits_for_write(self, banks):
        w = []
        for b in banks:
            w += list(self.bank[b]["r"].values())
        return w

    def note_write(self, banks, mark):
        for b in banks:
            self.bank[b]["w"] = mark
            self.bank[b]["r"] = {}

    def read_waits(self, banks):
        return [self.bank[b]["w"] for b in banks if self.bank[b]["w"] is not None]

    def note_read(self, banks, eng, mark):
        for b in banks:
            self.bank[b]["r"][eng] = mark

    def mm(self, banks, builder, extra=(), first=True, last=True, force_inc=False):
        waits = list(extra)
        if first:
            waits += self.bank_waits_for_write(banks)
        mark = self.op("tensor", builder,
                       waits=waits, inc=(None if (last or force_inc) else False))
        if last:
            self.note_write(banks, mark)
        return mark if (last or force_inc) else None

    def rd(self, eng, banks, builder, extra=()):
        waits = list(extra) + self.read_waits(banks)
        mark = self.op(eng, builder, waits=waits)
        self.note_read(banks, eng, mark)
        return mark

    def replay(self, block):
        for eng in self.ENGINES:
            items = self.q[eng]
            if not items:
                continue

            def body(engine, items=items):
                for fn in items:
                    fn(engine)
            getattr(block, eng)(body)


def build(dbg=()):
    nc = bass.Bass("TRN2", target_bir_lowering=False, debug=False)

    def din(name, shape, dtype=F32):
        return nc.dram_tensor(name, shape, dtype, kind="ExternalInput").ap()

    x0 = din("x0", [D, T])
    ident = din("ident", [128, 128])
    ones = din("ones", [128, 128])
    Wqkv = din("Wqkv", [48, D, HD])
    bqkvT = din("bqkvT", [HD, 48])
    bqkv_flat = din("bqkv_flat", [1, 48 * HD])
    WpT = din("WpT", [D, D])
    bp_cols = din("bp_cols", [128, DC])
    Wg_l = [din("Wg0", [D, E]), din("Wg1", [D, E])]
    bg_l = [din("bg0", [1, E]), din("bg1", [1, E])]
    W1_l = [din("W1_0", [E, D, H]), din("W1_1", [E, D, H])]
    W2_l = [din("W2_0", [E, H, D]), din("W2_1", [E, H, D])]
    b1c = din("b1c", [E, 128, 32])
    b2c = din("b2c", [E, 128, DC])
    opwT = din("opwT", [D, 2016])
    opb = din("opb", [1, 2016])
    oowT = din("oowT", [HD, D])
    oobc = din("oobc", [128, DC])
    vpiwT = din("vpiwT", [D, HD])
    vpib = din("vpib", [1, HD])
    vpowT = din("vpowT", [HD, D])
    vpobc = din("vpobc", [128, DC])
    lt128 = din("lt128", [128, 128])
    ecm1 = din("ecm1", [128, E])
    b2r = din("b2r", [E, 1, D])
    cbp = din("cbp", [128, 128])
    mixes = din("mixes", [1, 2])
    nwc = din("nwc", [128, DC])

    ok = lambda n: "ExternalOutput" if n in dbg else "Internal"
    xn_d = nc.dram_tensor("xn_d", [128, DC, T], BF16, kind=ok("xn_d")).ap()
    x_d = nc.dram_tensor("x_d", [128, DC, T], F32, kind=ok("x_d")).ap()
    sg_d = nc.dram_tensor("sg_dump", [128, DC * T], BF16, kind="ExternalOutput").ap() if "sg" in dbg else None
    sg_d1 = nc.dram_tensor("sg_dump1", [128, DC * T], BF16, kind="ExternalOutput").ap() if "sg1" in dbg else None
    out_fm = nc.dram_tensor("out_fm", [128, DC, T], F32, kind="ExternalOutput").ap()
    ind_d = nc.dram_tensor("ind_d", [8, T], F32, kind="Internal").ap()
    xg_d = nc.dram_tensor("xg_d", [E * CAP, D], BF16, kind="Internal").ap()
    w1bf_d = nc.dram_tensor("w1bf_d", [E, D, H], BF16, kind="Internal").ap()
    w2bf_d = nc.dram_tensor("w2bf_d", [E, H, D], BF16, kind="Internal").ap()
    ytm_d = nc.dram_tensor("ytm_d", [E * CAP, D], BF16, kind="Internal").ap()
    gtm_d = nc.dram_tensor("gtm_d", [T, D], BF16, kind="Internal").ap()

    stk = contextlib.ExitStack()
    sbuf = lambda name, shape, dtype: stk.enter_context(nc.sbuf_tensor(name, shape, dtype))
    K = B(nc, stk)
    pe, v, sc, dg, ds = (K.ctr[k] for k in ("pe", "v", "sc", "dg", "ds"))

    pb01 = [stk.enter_context(nc.psum_tensor(f"pb{i}", [128, 512], F32)) for i in range(2)]
    p4 = stk.enter_context(nc.psum_tensor("p4", [128, 2048], F32))
    pbig1_t = stk.enter_context(nc.psum_tensor("pbig1", [128, 1024], F32))
    pb = [pb01[0].ap() if hasattr(pb01[0], "ap") else pb01[0], None, None, None]
    pb[0] = pb01[0]
    pb[1] = pb01[1]
    pb[2] = p4[:, 0:512]
    pb[3] = p4[:, 512:1024]
    pbig = [p4[:, 1024:2048], pbig1_t]

    sg = stk.enter_context(nc.sbuf_tensor("sg", [128, DC * T], BF16, is_transpose_dest=True))
    lt_sb = sbuf("lt_sb", [128, 128], F32)
    ecm1_sb = sbuf("ecm1_sb", [128, E], F32)
    ind_all = sbuf("ind_all", [128, TC * E], F32)
    cs_rows = sbuf("cs_rows", [1, 128], F32)
    bases = sbuf("bases", [1, 128], F32)
    tmp8 = sbuf("tmp8", [128, E], F32)
    tcol = sbuf("tcol", [128, 1], F32)
    tsl_sb = sbuf("tsl_sb", [128, TC], I32)
    pc_stage = [sbuf(f"pc_stage{i}", [128, 2048], BF16) for i in range(2)]
    ident_sb = sbuf("ident_sb", [128, 128], F32)
    ones_sb = sbuf("ones_sb", [128, 128], F32)
    ind_fm = sbuf("ind_fm", [8, T], F32)
    mrow_sb = sbuf("mrow_sb", [1, 1024], F32)
    hm_cols = sbuf("hm_cols", [128, DC], F32)
    hm_bf = sbuf("hm_bf", [128, DC], BF16)
    bg_rep = sbuf("bg_rep", [128, E], F32)
    rowC = sbuf("rowC", [1, 512], F32)
    ltm = sbuf("ltm", [128, E], F32)
    mxcol = sbuf("mxcol", [128, 1], F32)
    ind_tm = sbuf("ind_tm", [128, E], F32)
    wg_bf = sbuf("wg_bf", [128, DC * E], F32)
    rstdc_sb = sbuf("rstdc_sb", [128, 4], F32)
    bgrow = sbuf("bgrow", [1, E], F32)
    bp_sb = sbuf("bp_sb", [128, DC], F32)
    bqkvT_sb = sbuf("bqkvT_sb", [HD, 48], F32)
    oobc_sb = sbuf("oobc_sb", [128, DC], F32)
    vpobc_sb = sbuf("vpobc_sb", [128, DC], F32)
    nwc_sb = sbuf("nwc_sb", [128, DC], F32)
    opb_sb = sbuf("opb_sb", [1, 2016], F32)
    vpib_sb = sbuf("vpib_sb", [1, HD], F32)
    mixes_sb = sbuf("mixes_sb", [1, 2], F32)
    cb_sb = sbuf("cb_sb", [128, 128], F32)
    b1c_sb = [sbuf(f"b1c_sb{i}", [128, 32], F32) for i in range(2)]
    b2c_sb = [sbuf(f"b2c_sb{i}", [128, DC], F32) for i in range(2)]
    ffn_tmp = sbuf("ffn_tmp", [128, 512], F32)

    state = {}

    for dst, src in [
        (ident_sb[:], ident), (ones_sb[:], ones), (bp_sb[:], bp_cols),
        (bqkvT_sb[:], bqkvT), (oobc_sb[:], oobc),
        (vpobc_sb[:], vpobc), (nwc_sb[:], nwc), (opb_sb[:], opb),
        (vpib_sb[:], vpib), (mixes_sb[:], mixes), (cb_sb[:], cbp),
        (lt_sb[:], lt128), (ecm1_sb[:], ecm1),
    ]:
        ev_consts = K.op("sync", lambda e, d=dst, s=src: e.dma_start(out=d, in_=s), dma=True)

    def phase_ln(l, xsrc, xs, xn_sb, want_hm, ln_sq, rowA, rowB):
        ev_x = K.op("sync", lambda e: e.dma_start(
            out=xs[:].rearrange("p (dc t) -> p dc t", dc=DC), in_=xsrc), dma=True)
        ev_wg = K.op("sync", lambda g: g.dma_start(
            out=wg_bf[:].rearrange("p (dc e) -> p dc e", dc=DC),
            in_=Wg_l[l].rearrange("(dc p) e -> p dc e", p=128)), dma=True)
        ev_bg = K.op("sync", lambda e: e.dma_start(out=bgrow[:], in_=bg_l[l]), dma=True)
        m = K.mm([0], lambda t: t.matmul(pb[0][:, 0:E], ones_sb[0:1, :], bgrow[:],
                                         start=True, stop=True),
                 extra=[ev_bg, ev_consts])
        ev_bgrep = K.rd("scalar", [0], lambda s: s.copy(bg_rep[:], pb[0][:, 0:E]))

        def emit_stats(tau):
            for dc in range(DC):
                par = dc % 2
                xt = xs[:, dc * T + tau * 512: (dc * T + tau * 512) + 512]
                sq_ev = K.op("scalar", lambda s, a=ln_sq[par], b=xt: s.activation(a[:], b, AF.Square),
                             waits=[ev_x, state.get(("lnsq", par))])
                K.mm([0], lambda t, b=xt, dc=dc: t.matmul(
                    pb[0][0:1, 0:512], ones_sb[:, 0:1], b, start=(dc == 0), stop=(dc == DC - 1)),
                    extra=[ev_x, ev_consts], first=(dc == 0), last=(dc == DC - 1))
                m2 = K.mm([1], lambda t, a=ln_sq[par], dc=dc: t.matmul(
                    pb[1][0:1, 0:512], ones_sb[:, 0:1], a[:], start=(dc == 0), stop=(dc == DC - 1)),
                    extra=[sq_ev], first=(dc == 0), last=True)
                state[("lnsq", par)] = m2

        emit_stats(0)
        for tau in range(TQ):
            mrow = rowA[0:1, 0:512]
            rrow = rowB[0:1, 0:512]
            e1 = K.rd("vector", [0], lambda vv, mr=mrow: vv.tensor_scalar(
                mr, pb[0][0:1, 0:512], 1.0 / D, None, op0=ALU.mult),
                extra=[state.get("ln_mrow_free")])
            e2 = K.rd("vector", [1], lambda vv: vv.tensor_scalar(
                rowC[:], pb[1][0:1, 0:512], 1.0 / D, LN_EPS, op0=ALU.mult, op1=ALU.add),
                extra=[e1])
            e3 = K.op("vector", lambda vv, mr=mrow, rr=rrow: vv.tensor_tensor(rr, mr, mr, op=ALU.mult),
                      waits=[e2, state.get("ln_rrow_free")])
            e4 = K.op("vector", lambda vv, rr=rrow: vv.tensor_tensor(rr, rowC[:], rr, op=ALU.subtract),
                      waits=[e3])
            e5 = K.op("scalar", lambda s, rr=rrow: s.activation(rr, rr, AF.Sqrt),
                      waits=[e4])
            e6 = K.op("vector", lambda vv, rr=rrow: vv.reciprocal(rr, rr), waits=[e5])
            if tau + 1 < TQ:
                emit_stats(tau + 1)
            mmm = K.mm([2], lambda t, mr=mrow: t.matmul(
                pb[2][:, 0:512], ones_sb[0:1, :], mr, start=True, stop=True), extra=[e4])
            state["ln_mrow_free"] = mmm
            mmr = K.mm([3], lambda t, rr=rrow: t.matmul(
                pb[3][:, 0:512], ones_sb[0:1, :], rr, start=True, stop=True), extra=[e6])
            for tc4 in range(4):
                mtr = K.mm([7], lambda t, tc4=tc4, rr=rrow: t.transpose(
                    pbig[1][:, 512:513], rr[0:1, ts(tc4, 128)], ident_sb[0:1, 0:1]),
                    extra=[e6])
                K.rd("scalar", [7], lambda s2, tc4=tc4: s2.copy(
                    rstdc_sb[:, tc4:tc4 + 1], pbig[1][:, 512:513]))
            state["ln_rrow_free"] = mtr
            for dc in range(DC):
                xt = xs[:, dc * T + tau * 512: (dc * T + tau * 512) + 512]
                xnt = xn_sb[:, dc * T + tau * 512: (dc * T + tau * 512) + 512]
                K.rd("vector", [2], lambda vv, a=xt: vv.tensor_tensor(a, a, pb[2][:, 0:512], op=ALU.subtract))
                ev_n = K.rd("vector", [3], lambda vv, a=xt, b=xnt: vv.tensor_tensor(
                    b, a, pb[3][:, 0:512], op=ALU.mult))
                K.op("sync", lambda e, b=xnt, dc=dc, tau=tau: e.dma_start(
                    out=xn_d[:, dc, ts(tau, 512)], in_=b), waits=[ev_n], dma=True)
            ev_norm = (v, v.n)
            for tc4 in range(4):
                g_ = tau * 4 + tc4
                for dc in range(DC):
                    K.mm([4], lambda t, dc=dc, g_=g_, tc4=tc4: t.matmul(
                        pbig[0][:, ts(tc4, 8)],
                        xs[:, dc * T + g_ * 128: dc * T + g_ * 128 + 128],
                        wg_bf[:, ts(dc, E)], start=(dc == 0), stop=(dc == DC - 1)),
                        extra=[ev_norm, ev_wg], first=(dc == 0), last=(dc == DC - 1))
                e1 = K.rd("vector", [4], lambda vv, tc4=tc4: vv.scalar_tensor_tensor(
                    ltm[:], pbig[0][:, ts(tc4, 8)], rstdc_sb[:, tc4:tc4 + 1], bg_rep[:],
                    op0=ALU.mult, op1=ALU.add),
                    extra=[ev_bgrep, state.get("ln_ltm_free"), (sc, sc.n)])
                e2 = K.op("vector", lambda vv: vv.reduce_max(mxcol[:], ltm[:], axis=AX.X), waits=[e1])
                K.wait_only("vector", [e2])
                e3 = K.op("vector", lambda vv: vv.tensor_scalar(
                    ind_tm[:], ltm[:], mxcol[:], None, op0=ALU.is_equal),
                    waits=[e2])
                K.op("vector", lambda vv, g_=g_: vv.tensor_copy(
                    ind_all[:, ts(g_, E)], ind_tm[:]))
                state["ln_ltm_free"] = e3
        ev_ind = (v, v.n)
        for tc in range(TC):
            K.mm([6], lambda t, tc=tc: t.matmul(
                pbig[1][0:1, ts(tc, E)], ones_sb[:, 0:1], ind_all[:, ts(tc, E)],
                start=True, stop=True),
                extra=[ev_ind] if tc == 0 else [], first=(tc == 0), last=(tc == TC - 1))
        K.rd("scalar", [6], lambda s: s.copy(cs_rows[:], pbig[1][0:1, 0:128]))
        K.op("vector", lambda vv: vv.memset(bases[0:1, 0:E], 0.0), waits=[(sc, sc.n)])
        for tc in range(1, TC):
            K.op("vector", lambda vv, tc=tc: vv.tensor_tensor(
                bases[0:1, ts(tc, E)], bases[0:1, ts(tc - 1, E)],
                cs_rows[0:1, ts(tc - 1, E)], op=ALU.add))
        ev_bases = (v, v.n)
        for tc in range(TC):
            cpar = tc % 2
            cum_sl = pbig[0][:, cpar * 512: cpar * 512 + E]
            cbank = [4 + cpar]
            K.mm(cbank, lambda t, tc=tc, cum_sl=cum_sl: t.matmul(
                cum_sl, lt_sb[:], ind_all[:, ts(tc, E)], start=True, stop=False),
                extra=[ev_bases, ev_consts], first=True, last=False)
            K.mm(cbank, lambda t, tc=tc, cum_sl=cum_sl: t.matmul(
                cum_sl, ones_sb[0:1, :], bases[0:1, ts(tc, E)], start=False, stop=True),
                first=False, last=True)
            K.rd("vector", cbank, lambda vv, cum_sl=cum_sl: vv.tensor_tensor(
                tmp8[:], cum_sl, ecm1_sb[:], op=ALU.add))
            K.op("vector", lambda vv, tc=tc: vv.tensor_tensor(
                tmp8[:], tmp8[:], ind_all[:, ts(tc, E)], op=ALU.mult))
            K.op("vector", lambda vv: vv.tensor_reduce(tcol[:], tmp8[:], AX.X, ALU.add))
            K.op("vector", lambda vv: vv.tensor_scalar(
                tcol[:], tcol[:], float(E * CAP - 1), None, op0=ALU.min))
            K.op("vector", lambda vv, tc=tc: vv.tensor_copy(tsl_sb[:, tc:tc + 1], tcol[:]))
        state["tsl"] = (v, v.n)
        if want_hm:
            prev = None
            for dc in range(DC):
                prev = K.op("vector", lambda vv, dc=dc: vv.tensor_reduce(
                    hm_cols[:, dc:dc + 1], xn_sb[:, ts(dc, T)], AX.X, ALU.add), waits=[(v, v.n)])
            K.wait_only("vector", [prev])
            state["hm"] = K.op("vector", lambda vv: vv.tensor_scalar(
                hm_bf[:], hm_cols[:], 1.0 / T, None, op0=ALU.mult))

    def phase_ffn(l, xnh, hact, w2buf, w1buf, maskrep, gelu):
        ev_memset = K.op("vector", lambda vv: vv.memset(sg[:], 0.0))
        a_evac = {}
        w1_free = [None] * 6
        w2_free = [None] * 6
        mask_free = None
        hps_free = [None, None]
        b1_free = [None, None]
        b2_free = [None, None]
        ev_b1 = [None, None]
        ev_b2 = [None, None]
        xnh_free = None
        ybanks_sl = [pb[0][:, 0:512], pb[1][:, 0:512], pb[2][:, 0:512], pb[3][:, 0:512],
                     pbig[0][:, 0:512], pbig[0][:, 512:1024],
                     pbig[1][:, 0:512], pbig[1][:, 512:1024]]
        ybank_ids = [[0], [1], [2], [3], [4], [5], [6], [7]]

        for th in range(TH):
            ev_xnh = K.op("sync", lambda e, th=th: e.dma_start(
                out=xnh[:].rearrange("p (dc t) -> p dc t", dc=DC),
                in_=xn_d[:, :, ts(th, 1024)]), waits=[xnh_free], dma=True)
            for e_ in range(E):
                bpar = e_ % 2
                if l == 1:
                    K.op("sync", lambda e, e_=e_, bp=bpar: e.dma_start(
                        out=b1c_sb[bp][:], in_=b1c[e_]), waits=[b1_free[bpar], b2_free[bpar]],
                        dma=True, dma_sem=f"d_b{bpar}")
                    mboth = K.op("sync", lambda e, e_=e_, bp=bpar: e.dma_start(
                        out=b2c_sb[bp][:], in_=b2c[e_]), waits=[],
                        dma=True, dma_sem=f"d_b{bpar}")
                    ev_b1[bpar] = ev_b2[bpar] = mboth
                ev_mrow = K.op("sync", lambda e, e_=e_, th=th: e.dma_start(
                    out=mrow_sb[:], in_=ind_d[e_:e_ + 1, ts(th, 1024)]),
                    waits=[state[("masks", l)], state.get("ffn_mrow_free")], dma=True)
                for seg in range(16):
                    par = seg % 4
                    ev_w1 = K.op("gpsimd", lambda g, e_=e_, seg=seg, par=par: g.dma_start(
                        out=w1buf[par][:].rearrange("p (dc h) -> p dc h", dc=DC),
                        in_=W1_l[l][e_, :, ts(seg, 256)].rearrange("(dc p) h -> p dc h", p=128)),
                        waits=[w1_free[par]], dma=True, dma_sem=f"d_w1{par}")
                    for hc2 in range(2):
                        hc = seg * 2 + hc2
                        for sub in range(2):
                            hb = sub
                            for dc in range(DC):
                                mmk = K.mm([hb], lambda t, par=par, dc=dc, hc2=hc2, sub=sub, hb=hb: t.matmul(
                                    pb[hb][:, 0:512],
                                    w1buf[par][:, dc * 256 + hc2 * 128: dc * 256 + hc2 * 128 + 128],
                                    xnh[:, dc * 1024 + sub * 512: dc * 1024 + sub * 512 + 512],
                                    start=(dc == 0), stop=(dc == DC - 1)),
                                    extra=([ev_w1, ev_xnh] if dc == 0 else []),
                                    first=(dc == 0), last=(dc == DC - 1))
                            if hc2 == 1 and sub == 1:
                                w1_free[par] = mmk
                            hsl = hact[:, hc * 1024 + sub * 512: hc * 1024 + sub * 512 + 512]
                            if gelu:
                                ev = K.rd("scalar", [hb], lambda s, h_=hsl, hb=hb: s.activation(
                                    h_, pb[hb][:, 0:512], AF.Gelu_apprx_tanh))
                            else:
                                ev = K.rd("scalar", [hb], lambda s, h_=hsl, hb=hb, hc=hc, bp=bpar: s.activation(
                                    h_, pb[hb][:, 0:512], AF.Relu, bias=b1c_sb[bp][:, hc:hc + 1]),
                                    extra=[ev_b1[bpar]])
                            a_evac[(hc, sub)] = ev
                xnh_free = (pe, pe.n)
                if l == 1:
                    b1_free[bpar] = (sc, sc.n)
                for q_ in range(2):
                    bq = 2 + q_
                    K.mm([bq], lambda t, q_=q_, bq=bq: t.matmul(
                        pb[bq][:, 0:512], ones_sb[0:1, :], mrow_sb[0:1, ts(q_, 512)],
                        start=True, stop=True), extra=[ev_mrow])
                state["ffn_mrow_free"] = (pe, pe.n)
                K.rd("scalar", [2], lambda s: s.copy(maskrep[:, 0:512], pb[2][:, 0:512]),
                     extra=[mask_free])
                ev_mask = K.rd("scalar", [3], lambda s: s.copy(maskrep[:, 512:1024],
                                                               pb[3][:, 0:512]),
                               extra=[mask_free])
                for dh in range(2):
                    for hc in range(HC):
                        w2p = hc % 6
                        ev_w2 = K.op("gpsimd", lambda g, e_=e_, hc=hc, dh=dh, w2p=w2p: g.dma_start(
                            out=w2buf[w2p][:], in_=W2_l[l][e_, ts(hc, 128), dh * 512:dh * 512 + 512]),
                            waits=[w2_free[w2p]], dma=True, dma_sem=f"d_w2{w2p}")
                        for sub in range(2):
                            for d4 in range(4):
                                yb = sub * 4 + d4
                                mmk = K.mm(ybank_ids[yb], lambda t, hc=hc, sub=sub, d4=d4, w2p=w2p, yb=yb: t.matmul(
                                    ybanks_sl[yb],
                                    w2buf[w2p][:, ts(d4, 128)],
                                    hact[:, hc * 1024 + sub * 512: hc * 1024 + sub * 512 + 512],
                                    start=(hc == 0), stop=(hc == HC - 1)),
                                    extra=([ev_w2, a_evac[(hc, sub)]] if d4 == 0 else []),
                                    first=(hc == 0), last=(hc == HC - 1),
                                    force_inc=(sub == 1 and d4 == 3))
                        w2_free[w2p] = mmk
                    for sub in range(2):
                        for d4 in range(4):
                            yb = sub * 4 + d4
                            dout = dh * 4 + d4
                            sgs = sg[:, dout * T + th * 1024 + sub * 512:
                                     dout * T + th * 1024 + sub * 512 + 512]
                            msl = maskrep[:, ts(sub, 512)]
                            if l == 0:
                                K.rd("vector", ybank_ids[yb], lambda vv, yb=yb, msl=msl: vv.scalar_tensor_tensor(
                                    ffn_tmp[:], ybanks_sl[yb], 0.0, msl, op0=ALU.add, op1=ALU.mult),
                                    extra=[ev_mask, ev_memset])
                            else:
                                K.rd("vector", ybank_ids[yb], lambda vv, yb=yb, msl=msl, dout=dout, bp=bpar: vv.scalar_tensor_tensor(
                                    ffn_tmp[:], ybanks_sl[yb], b2c_sb[bp][:, dout:dout + 1], msl,
                                    op0=ALU.add, op1=ALU.mult),
                                    extra=[ev_mask, ev_memset, ev_b2[bpar]])
                            K.op("vector", lambda vv, sgs=sgs: vv.tensor_tensor(
                                sgs, sgs, ffn_tmp[:], op=ALU.add))
                mask_free = (v, v.n)
                if l == 1:
                    b2_free[bpar] = (v, v.n)
        state[("ffn_v", l)] = (v, v.n)

    def phase_ffn_sparse(l, xgfm, hact_sp, w1buf, w2buf, ytm_sb, xntm, b2row, gelu,
                         precast=False):
        W1_ap = W1_l[l]
        W2_ap = W2_l[l]
        xntm_free = [None, None]
        ev_scat = None
        for tc in range(TC):
            par = tc % 2
            tr = None
            for dc in range(DC):
                tr = K.op("sync", lambda e, dc=dc, tc=tc, par=par: e.dma_start_transpose(
                    out=xntm[par][:, ts(dc, 128)], in_=xn_d[:, dc, ts(tc, 128)]),
                    waits=[xntm_free[par]], dma=True, dma_sem=f"d_xt{par}")
            ev_scat_p = K.op("gpsimd", lambda g, tc=tc, par=par: g.indirect_dma_start(
                out=xg_d, out_offset=IndirectOffsetOnAxis(tsl_sb[:, tc:tc + 1], 0),
                in_=xntm[par][:], in_offset=None),
                waits=[tr, state["tsl"]], dma=True, dma_sem=f"d_scat{par}")
            xntm_free[par] = ev_scat_p
            ev_scat = None
        a_evac = {}
        w1_free = [None] * 6
        w2_free = [None] * 6
        xg_free = [None, None]
        ytm_free = [None, None]
        b1_free = [None, None]
        b2_free = [None, None]
        ysl = [(p4[:, 0:512], [2]), (p4[:, 512:1024], [3]),
               (p4[:, 1024:1536], [4]), (p4[:, 1536:2048], [5]),
               (pbig[1][:, 0:512], [6]), (pbig[1][:, 512:1024], [7])]
        NSC = CAP // 128
        last_store = None
        xg_marks = [None] * E
        b1_marks = [None] * E
        b2_marks = [None] * E

        def emit_loads(e_):
            epar = e_ % 2
            m = None
            for dc in range(DC):
                m = K.op("sync", lambda e2, e_=e_, dc=dc, epar=epar: e2.dma_start_transpose(
                    out=xgfm[epar][:, ts(dc, CAP)], in_=xg_d[e_ * CAP:(e_ + 1) * CAP, ts(dc, 128)]),
                    waits=[xntm_free[0], xntm_free[1], xg_free[epar]],
                    dma=True, dma_sem=f"d_xg{epar}")
            xg_marks[e_] = m
            if l == 1:
                b1_marks[e_] = K.op("sync", lambda e, e_=e_, bp=epar: e.dma_start(
                    out=b1c_sb[bp][:], in_=b1c[e_]), waits=[b1_free[epar]],
                    dma=True, dma_sem=f"d_b{epar}")
                b2_marks[e_] = K.op("sync", lambda e, e_=e_, bp=epar: e.dma_start(
                    out=b2row[bp][:], in_=b2r[e_]), waits=[b2_free[epar]],
                    dma=True, dma_sem=f"d_b2{epar}")

        emit_loads(0)
        for e_ in range(E):
            epar = e_ % 2
            xg_m = xg_marks[e_]
            ev_b1 = b1_marks[e_]
            ev_b2 = b2_marks[e_]
            for seg in range(16):
                par4 = seg % 6
                w1src = (w1bf_d if precast else W1_ap)
                ev_w1 = K.op("gpsimd", lambda g, e_=e_, seg=seg, par4=par4, w1src=w1src: g.dma_start(
                    out=w1buf[par4][:].rearrange("p (dc h) -> p dc h", dc=DC),
                    in_=w1src[e_, :, ts(seg, 256)].rearrange("(dc p) h -> p dc h", p=128)),
                    waits=[w1_free[par4]] + ([state.get("precast_done"), state.get("precast_done2")] if precast else []),
                    dma=True, dma_sem=f"d_w1{par4}")
                for hc2 in range(2):
                    hc = seg * 2 + hc2
                    hb = hc % 2
                    for dc in range(DC):
                        mmk = K.mm([hb], lambda t, par4=par4, dc=dc, hc2=hc2, hb=hb, epar=epar: t.matmul(
                            pb[hb][:, 0:CAP],
                            w1buf[par4][:, dc * 256 + hc2 * 128: dc * 256 + hc2 * 128 + 128],
                            xgfm[epar][:, dc * CAP: dc * CAP + CAP],
                            start=(dc == 0), stop=(dc == DC - 1)),
                            extra=([ev_w1, xg_m] if dc == 0 else []),
                            first=(dc == 0), last=(dc == DC - 1))
                    if hc2 == 1:
                        w1_free[par4] = mmk
                    hsl = hact_sp[:, hc * CAP: hc * CAP + CAP]
                    if gelu:
                        a_evac[hc] = K.rd("scalar", [hb], lambda s, h_=hsl, hb=hb: s.activation(
                            h_, pb[hb][:, 0:CAP], AF.Gelu_apprx_tanh))
                    else:
                        a_evac[hc] = K.rd("scalar", [hb], lambda s, h_=hsl, hb=hb, hc=hc, bp=epar: s.activation(
                            h_, pb[hb][:, 0:CAP], AF.Relu, bias=b1c_sb[bp][:, hc:hc + 1]),
                            extra=[ev_b1])
            xg_free[epar] = (pe, pe.n)
            if l == 1:
                b1_free[epar] = (sc, sc.n)
            if e_ + 1 < E:
                emit_loads(e_ + 1)
            for hc in range(HC):
                w2p = hc % 6
                w2src = (w2bf_d if precast else W2_ap)
                ev_w2 = K.op("gpsimd", lambda g, e_=e_, hc=hc, w2p=w2p, w2src=w2src: g.dma_start(
                    out=w2buf[w2p][:], in_=w2src[e_, ts(hc, 128), :]),
                    waits=[w2_free[w2p]] + ([state.get("precast_done"), state.get("precast_done2")] if precast else []),
                    dma=True, dma_sem=f"d_w2{w2p}")
                for sc4 in range(NSC):
                    for half in range(2):
                        yi = sc4 * 2 + half
                        mmk = K.mm(ysl[yi][1], lambda t, hc=hc, sc4=sc4, half=half, w2p=w2p, yi=yi: t.matmul(
                            ysl[yi][0],
                            hact_sp[:, hc * CAP + sc4 * 128: hc * CAP + sc4 * 128 + 128],
                            w2buf[w2p][:, half * 512: half * 512 + 512],
                            start=(hc == 0), stop=(hc == HC - 1 and l == 0)),
                            extra=([ev_w2, a_evac[hc]] if (sc4 == 0 and half == 0) else [ev_w2] if half == 0 else []),
                            first=(hc == 0), last=(hc == HC - 1 and l == 0),
                            force_inc=(sc4 == NSC - 1 and half == 1))
                w2_free[w2p] = mmk
            if l == 1:
                for sc4 in range(NSC):
                    for half in range(2):
                        yi = sc4 * 2 + half
                        K.mm(ysl[yi][1], lambda t, sc4=sc4, half=half, yi=yi, bp=epar: t.matmul(
                            ysl[yi][0], ones_sb[0:1, :], b2row[bp][0:1, half * 512: half * 512 + 512],
                            start=False, stop=True),
                            extra=[ev_b2] if yi == 0 else [], first=False, last=True)
            for sc4 in range(NSC):
                for half in range(2):
                    yi = sc4 * 2 + half
                    K.rd("vector", ysl[yi][1], lambda vv, sc4=sc4, half=half, yi=yi, epar=epar: vv.tensor_copy(
                        ytm_sb[epar][:, sc4 * 1024 + half * 512: sc4 * 1024 + half * 512 + 512],
                        ysl[yi][0]), extra=[ytm_free[epar]])
            ev_yv = (v, v.n)
            last_store = K.op("sync", lambda e2, e_=e_, epar=epar: e2.dma_start(
                out=ytm_d[e_ * CAP:(e_ + 1) * CAP, :].rearrange("(sc p) d -> p sc d", p=128),
                in_=ytm_sb[epar][:].rearrange("p (sc d) -> p sc d", sc=NSC)),
                waits=[ev_yv], dma=True, dma_sem=f"d_yt{epar}")
            ytm_free[epar] = last_store
            if l == 1:
                b2_free[epar] = (v, v.n)
        ev_gat = None
        gst_free = [None, None]
        for tc in range(TC):
            gpar = tc % 2
            gm = K.op("gpsimd", lambda g, tc=tc, gpar=gpar: g.indirect_dma_start(
                out=xntm[gpar][:], out_offset=None,
                in_=ytm_d, in_offset=IndirectOffsetOnAxis(tsl_sb[:, tc:tc + 1], 0)),
                waits=[ytm_free[0], ytm_free[1], gst_free[gpar]],
                dma=True, dma_sem=f"d_gat{gpar}")
            ev_gat = K.op("sync", lambda e2, tc=tc, gpar=gpar: e2.dma_start(
                out=gtm_d[tc * 128:(tc + 1) * 128, :], in_=xntm[gpar][:]),
                waits=[gm], dma=True, dma_sem=f"d_gst{gpar}")
            gst_free[gpar] = ev_gat
        state["gst0"], state["gst1"] = gst_free[0], gst_free[1]
        ev_sg = None
        for dc in range(DC):
            ev_sg = K.op("sync", lambda e2, dc=dc: e2.dma_start_transpose(
                out=sg[:, dc * T:(dc + 1) * T], in_=gtm_d[:, ts(dc, 128)]),
                waits=[ev_gat, state.get("gst0"), state.get("gst1")], dma=True, dma_sem="d_sg")
        state[("ffn_v", l)] = ev_sg

    def phase_attn(xn_sb, av_sb, wpT_buf, attnT2, Qb2, Kb2, Vb2, qkv_bf, rrep2,
                   nmcols, rrow, x0t):
        ev_xn = K.op("sync", lambda e: e.dma_start(
            out=xn_sb[:].rearrange("p (dc t) -> p dc t", dc=DC), in_=xn_d[:, :, :]), dma=True)
        ev_g = None
        for i in range(DC * 4):
            ev_g = K.op("scalar", lambda s, i=i: s.activation(
                sg[:, ts(i, 512)], sg[:, ts(i, 512)], AF.Silu),
                waits=[state[("ffn_v", 0)]])
        for hd in range(2):
            for tc in range(TC):
                K.op("vector", lambda vv, hd=hd, tc=tc: vv.memset(
                    Vb2[hd][:, tc * 65 + 64: tc * 65 + 65], 1.0))
        ev_vones = (v, v.n)

        qkv_w = [None, None]
        qkv_wfree = [None, None]
        kones_free = [None, None]
        x0_free = [None, None]
        prev_av_tt = [None]
        tr_mark = [None, None]
        for hp in range(NH // 2):
            heads = (2 * hp, 2 * hp + 1)
            ev_qk = [None, None]
            ev_kones = [None, None]
            ev_v = [None, None]
            row64 = [[None] * TQ, [None] * TQ]
            for hd in range(2):
                h = heads[hd]
                for kk in range(3):
                    qkv_w[hd] = K.op("gpsimd", lambda g, h=h, hd=hd, kk=kk: g.dma_start(
                        out=qkv_bf[0][:, ts(kk, 1024)].rearrange(
                            "p (dc x) -> p dc x", dc=DC)[:, :, hd * HD:(hd + 1) * HD],
                        in_=Wqkv[kk * 16 + h].rearrange("(dc p) hd2 -> p dc hd2", p=128)),
                        waits=[qkv_wfree[0], qkv_wfree[1]], dma=True, dma_sem=f"d_qkv{hd}")
            ev_pairw = (K.ctr["d_qkv0"], None)
            for which, koff in (("q", 0), ("k", 1)):
                for tau in range(TQ):
                    qpar = tau % 2
                    for dc in range(DC):
                        K.mm([qpar], lambda t, dc=dc, tau=tau, koff=koff, qpar=qpar: t.matmul(
                            pb[qpar][:, 0:512],
                            qkv_bf[0][:, koff * 1024 + dc * 128: koff * 1024 + dc * 128 + 128],
                            xn_sb[:, dc * T + tau * 512: dc * T + tau * 512 + 512],
                            start=(dc == 0), stop=(dc == DC - 1)),
                            extra=([qkv_w[0], qkv_w[1], ev_xn] if dc == 0 else []),
                            first=(dc == 0), last=(dc == DC - 1))
                    for hd in range(2):
                        buf = (Qb2 if which == "q" else Kb2)[hd]
                        bi = (0 if which == "q" else 16) + heads[hd]
                        K.rd("vector", [qpar], lambda vv, buf=buf, tau=tau, bi=bi, qpar=qpar, hd=hd: vv.tensor_scalar(
                            buf[0:HD, ts(tau, 512)], pb[qpar][hd * HD:(hd + 1) * HD, 0:512],
                            bqkvT_sb[:, bi:bi + 1], None, op0=ALU.add),
                            extra=[ev_consts] + ([kones_free[hd]] if which == "k" else []))
            ev_qk = [(v, v.n), (v, v.n)]
            ev_kones = [None, None]
            for hd in range(2):
                ev_kones[hd] = K.op("vector", lambda vv, Kb=Kb2[hd]: vv.memset(Kb[64:65, :], 1.0),
                                    waits=[ev_qk[hd]])
            for tc in range(TC):
                vpar = tc % 2
                for dc in range(DC):
                    K.mm([vpar], lambda t, dc=dc, tc=tc, vpar=vpar: t.matmul(
                        pb[vpar][:, 0:128],
                        xn_sb[:, dc * T + tc * 128: dc * T + tc * 128 + 128],
                        qkv_bf[0][:, 2 * 1024 + dc * 128: 2 * 1024 + dc * 128 + 128],
                        start=(dc == 0), stop=(dc == DC - 1)),
                        extra=([qkv_w[0], qkv_w[1], ev_xn] if dc == 0 else []),
                        first=(dc == 0), last=(dc == DC - 1))
                for hd in range(2):
                    K.rd("vector", [vpar], lambda vv, tc=tc, vpar=vpar, hd=hd, Vb=Vb2[hd]: vv.tensor_copy(
                        Vb[:, tc * 65: tc * 65 + 64], pb[vpar][:, hd * HD:(hd + 1) * HD]),
                        extra=[ev_vones])
            ev_v = [(v, v.n), (v, v.n)]
            qkv_wfree[0] = (pe, pe.n)
            qkv_wfree[1] = (pe, pe.n)

            def emit_p1_scores(hd, qg, sp):
                Qb, Kb = Qb2[hd], Kb2[hd]
                for qt4 in range(4):
                    qt = qg * 4 + qt4
                    for half, bks in ((0, [2, 3]), (1, [4, 5])):
                        for kg in range(2):
                            K.mm(bks, lambda t, qt=qt, half=half, kg=kg, Qb=Qb, Kb=Kb: t.matmul(
                                p4[:, ts(half * 2 + kg, 512)], Qb[0:HD, ts(qt, 128)],
                                Kb[0:HD, ts(half * 2 + kg, 512)], start=True, stop=True),
                                extra=([ev_qk[hd]] if kg == 0 else []),
                                first=(kg == 0), last=(kg == 1))
                        K.rd("vector", bks, lambda vv, half=half: vv.reduce_max(
                            nmcols[:, half:half + 1], p4[:, half * 1024:half * 1024 + 1024],
                            axis=AX.X, negate=True))
                    col = 2 + (sp % 2) * 4 + qt4
                    K.op("vector", lambda vv, col=col: vv.tensor_reduce(
                        nmcols[:, col:col + 1], nmcols[:, 0:2], AX.X, ALU.min),
                        waits=[tr_mark[sp % 2]])

            def emit_p1_finish(hd, qg, sp):
                Qb = Qb2[hd]
                for qt4 in range(4):
                    qt = qg * 4 + qt4
                    col = 2 + (sp % 2) * 4 + qt4
                    mt = K.mm([7], lambda t, col=col: t.transpose(
                        pbig[1][0:1, 512:640], nmcols[:, col:col + 1], ident_sb[:]),
                        extra=[(v, v.n)])
                    row64[hd][qg] = K.rd("scalar", [7], lambda s, qt=qt, Qb=Qb: s.copy(
                        Qb[64:65, ts(qt, 128)], pbig[1][0:1, 512:640]), extra=[ev_kones[hd]])
                tr_mark[sp % 2] = (pe, pe.n)

            emit_p1_scores(0, 0, 0)
            emit_p1_finish(0, 0, 0)
            for step in range(2 * TQ):
                hd, qg = step % 2, step // 2
                h = heads[hd]
                Qb, Kb, Vb = Qb2[hd], Kb2[hd], Vb2[hd]
                attnT = attnT2[hd]
                rrep_sb = rrep2[hd]
                if step + 1 < 2 * TQ:
                    emit_p1_scores((step + 1) % 2, (step + 1) // 2, step + 1)
                exp_marks = []
                for kc in range(TC):
                    spar = kc % 2
                    K.mm([spar], lambda t, kc=kc, qg=qg, spar=spar, Qb=Qb, Kb=Kb: t.matmul(
                        pb[spar][:, 0:512], Kb[0:65, ts(kc, 128)], Qb[0:65, ts(qg, 512)],
                        start=True, stop=True),
                        extra=([row64[hd][qg], ev_kones[hd]] if kc == 0 else []))
                    exp_marks.append(K.rd("scalar", [spar], lambda s, kc=kc, spar=spar, attnT=attnT: s.activation(
                        attnT[:, ts(kc, 512)], pb[spar][:, 0:512], AF.Exp, scale=8.0)))
                for kc in range(TC):
                    K.mm([6], lambda t, kc=kc, attnT=attnT, Vb=Vb: t.matmul(
                        pbig[1][0:65, 0:512], Vb[:, kc * 65: kc * 65 + 65], attnT[:, ts(kc, 512)],
                        start=(kc == 0), stop=(kc == TC - 1)),
                        extra=[exp_marks[kc]] + ([ev_v[hd]] if kc == 0 else []),
                        first=(kc == 0), last=(kc == TC - 1))
                e1 = K.rd("vector", [6], lambda vv: vv.reciprocal(rrow[:], pbig[1][64:65, 0:512]))
                mmr = K.mm([7], lambda t: t.matmul(pbig[1][0:64, 512:1024], ones_sb[0:1, 0:HD],
                                                   rrow[:], start=True, stop=True), extra=[e1])
                ev_rr = K.rd("vector", [7], lambda vv, rrep_sb=rrep_sb: vv.tensor_copy(
                    rrep_sb[:], pbig[1][0:64, 512:1024]), extra=[prev_av_tt[0]])
                cc, ph = h // 2, h % 2
                bv_col = bqkvT_sb[:, 32 + h: 33 + h]
                K.rd("vector", [6], lambda vv, rrep_sb=rrep_sb: vv.tensor_tensor(
                    ffn_tmp[0:HD, :], pbig[1][0:HD, 0:512], rrep_sb[:], op=ALU.mult),
                    extra=[ev_rr])
                prev_av_tt[0] = K.op("vector", lambda vv, cc=cc, ph=ph, qg=qg, bv=bv_col: vv.tensor_scalar(
                    av_sb[ph * HD: ph * HD + HD, cc * T + qg * 512: cc * T + qg * 512 + 512],
                    ffn_tmp[0:HD, :], bv, None, op0=ALU.add))
                if step + 1 < 2 * TQ:
                    emit_p1_finish((step + 1) % 2, (step + 1) // 2, step + 1)
            kones_free[0] = (pe, pe.n)
            kones_free[1] = (pe, pe.n)
        ev_av = (v, v.n)
        x0r = x0.rearrange("(dc p) t -> p dc t", p=128)
        wp_free = [None, None]
        for pc in range(DC):
            wpar = pc % 2
            ev_wp = K.op("gpsimd", lambda g, pc=pc, wpar=wpar: g.dma_start(
                out=wpT_buf[wpar][:].rearrange("p (cc q) -> p cc q", cc=DC),
                in_=WpT[:, ts(pc, 128)].rearrange("(cc p) q -> p cc q", p=128)),
                waits=[wp_free[wpar]], dma=True, dma_sem=f"d_wp{wpar}")
            for tau in range(TQ):
                spar = (pc * TQ + tau) % 2
                ev_x0 = K.op("sync", lambda e, pc=pc, tau=tau, spar=spar: e.dma_start(
                    out=x0t[spar][:], in_=x0r[:, pc, ts(tau, 512)]),
                    waits=[x0_free[spar]], dma=True, dma_sem=f"d_x{spar}")
                for cc in range(DC):
                    K.mm([spar], lambda t, cc=cc, tau=tau, spar=spar, wpar=wpar: t.matmul(
                        pb[spar][:, 0:512],
                        wpT_buf[wpar][:, ts(cc, 128)],
                        av_sb[:, cc * T + tau * 512: cc * T + tau * 512 + 512],
                        start=(cc == 0), stop=(cc == DC - 1)),
                        extra=([ev_av, ev_wp] if cc == 0 else []),
                        first=(cc == 0), last=(cc == DC - 1))
                K.rd("vector", [spar], lambda vv, pc=pc, tau=tau, spar=spar: vv.scalar_tensor_tensor(
                    ffn_tmp[:], pb[spar][:, 0:512], bp_sb[:, pc:pc + 1],
                    sg[:, pc * T + tau * 512: pc * T + tau * 512 + 512],
                    op0=ALU.add, op1=ALU.mult), extra=[ev_g])
                ev_xt = K.op("vector", lambda vv, spar=spar: vv.tensor_tensor(
                    x0t[spar][:], x0t[spar][:], ffn_tmp[:], op=ALU.add), waits=[ev_x0])
                x0_free[spar] = K.op("sync", lambda e, pc=pc, tau=tau, spar=spar: e.dma_start(
                    out=x_d[:, pc, ts(tau, 512)], in_=x0t[spar][:]), waits=[ev_xt],
                    dma=True, dma_sem=f"d_xs{spar}")
            wp_free[wpar] = (pe, pe.n)

    def phase_l1small(xn_sb, pre, xr_bf, oow_bf, vpow_bf, opw_buf, vpiw_bf,
                      sq_t, gsil_t, s1_t, x1t, vqs):
        (params_sb, opb_sb, om_u, om_ut, om, om_n, om2, om3, Gt, G_bf, vq_row, vq_sq,
         vq_n, cb_sq, cb_n, cb_sc, vqn_rep, vq_gc, vq_ind, code_row, code_col,
         omix_col, vqmix_col, vqc_cols) = vqs
        ev_xn = K.op("sync", lambda e: e.dma_start(
            out=xn_sb[:].rearrange("p (dc t) -> p dc t", dc=DC), in_=xn_d[:, :, :]), dma=True)
        ev_oow = K.op("gpsimd", lambda g: g.dma_start(out=oow_bf[:], in_=oowT), dma=True)
        ev_vpow = K.op("gpsimd", lambda g: g.dma_start(out=vpow_bf[:], in_=vpowT), dma=True)
        ev_vpiw = K.op("sync", lambda e: e.dma_start(
            out=vpiw_bf[:].rearrange("p (dc c) -> p dc c", dc=DC),
            in_=vpiwT.rearrange("(dc p) c -> p dc c", p=128)), dma=True)
        ev_hm = state["hm"]
        ev_opb = K.op("sync", lambda e: e.dma_start(out=opb_sb[:], in_=opb), dma=True)
        opw_free = [None, None, None]
        for dcb in range(DC):
            par = dcb % 3
            ev_opw = K.op("gpsimd", lambda g, dcb=dcb, par=par: g.dma_start(
                out=opw_buf[par][:], in_=opwT[ts(dcb, 128), :]), waits=[opw_free[par]],
                dma=True, dma_sem=f"d_opw{par}")
            for seg_ in range(4):
                mk = K.mm([seg_], lambda t, dcb=dcb, seg_=seg_, par=par: t.matmul(
                    pb[seg_][0:1, 0:504], hm_bf[:, dcb:dcb + 1], opw_buf[par][:, ts(seg_, 504)],
                    start=(dcb == 0), stop=(dcb == DC - 1)),
                    extra=[ev_opw, ev_hm], first=(dcb == 0), last=True)
            opw_free[par] = mk
        for seg_ in range(4):
            K.rd("scalar", [seg_], lambda s, seg_=seg_: s.copy(
                params_sb[0:1, ts(seg_, 504)], pb[seg_][0:1, 0:504]))
        ev_p = K.op("vector", lambda vv: vv.tensor_tensor(
            params_sb[:], params_sb[:], opb_sb[:], op=ALU.add),
            waits=[(sc, sc.n), ev_opb])
        ev_mz = K.op("vector", lambda vv: vv.memset(om_u[:], 0.0))
        off = 0
        ev_tri = None
        for r in range(63):
            ln_ = 63 - r
            ev_tri = K.op("sync", lambda e, r=r, off=off, ln_=ln_: e.dma_start(
                out=om_u[r:r + 1, r + 1:64], in_=params_sb[0:1, off:off + ln_]),
                waits=[ev_p, ev_mz], dma=True)
            off += ln_
        mt = K.mm([4], lambda t: t.transpose(pbig[0][0:64, 0:64], om_u[:], ident_sb[0:64, 0:64]),
                  extra=[ev_tri])
        ev_ut = K.rd("scalar", [4], lambda s: s.copy(om_ut[:], pbig[0][0:64, 0:64]))
        e_om = K.op("vector", lambda vv: vv.tensor_tensor(om[:], om_u[:], om_ut[:], op=ALU.subtract),
                    waits=[ev_ut, ev_tri])
        e_nom = K.op("vector", lambda vv: vv.tensor_tensor(om_n[:], om_ut[:], om_u[:], op=ALU.subtract))
        m2 = K.mm([4], lambda t: t.matmul(pbig[0][0:64, 64:128], om_n[:], om[:], start=True, stop=True),
                  extra=[e_nom])
        ev_o2 = K.rd("scalar", [4], lambda s: s.copy(om2[:], pbig[0][0:64, 64:128]))
        m3 = K.mm([4], lambda t: t.matmul(pbig[0][0:64, 128:192], om2[:], om[:], start=True, stop=True),
                  extra=[ev_o2])
        ev_o3 = K.rd("scalar", [4], lambda s: s.copy(om3[:], pbig[0][0:64, 128:192]))
        K.op("vector", lambda vv: vv.tensor_tensor(Gt[:], om[:], ident_sb[0:64, 0:64], op=ALU.add),
             waits=[ev_o3])
        K.op("vector", lambda vv: vv.scalar_tensor_tensor(Gt[:], om2[:], 0.5, Gt[:],
                                                          op0=ALU.mult, op1=ALU.add))
        K.op("vector", lambda vv: vv.scalar_tensor_tensor(Gt[:], om3[:], 1.0 / 6.0, Gt[:],
                                                          op0=ALU.mult, op1=ALU.add))
        ev_G = K.op("vector", lambda vv: vv.tensor_copy(G_bf[:], Gt[:]))
        ev_xr = None
        for tau in range(TQ):
            mm_ = K.mm([6, 7], lambda t, tau=tau: t.matmul(
                pbig[1][0:64, 0:512], G_bf[:], xn_sb[0:64, ts(tau, 512)], start=True, stop=True),
                extra=[ev_G, ev_xn])
            ev_xr = K.rd("scalar", [6, 7], lambda s, tau=tau: s.copy(xr_bf[0:64, ts(tau, 512)],
                                                                     pbig[1][0:64, 0:512]))
        for dcb in range(DC):
            K.mm([5], lambda t, dcb=dcb: t.matmul(
                pbig[0][0:1, 512:576], hm_cols[:, dcb:dcb + 1], vpiw_bf[:, ts(dcb, HD)],
                start=(dcb == 0), stop=(dcb == DC - 1)),
                extra=[ev_vpiw, ev_hm], first=(dcb == 0), last=(dcb == DC - 1))
        ev_vq0 = K.rd("vector", [5], lambda vv: vv.scalar_tensor_tensor(
            vq_row[:], pbig[0][0:1, 512:576], 1.0 / T, vpib_sb[:],
            op0=ALU.mult, op1=ALU.add))
        ev_sq = K.op("scalar", lambda s: s.activation(vq_sq[:], vq_row[:], AF.Square),
                     waits=[ev_vq0])
        e1 = K.op("vector", lambda vv: vv.tensor_reduce(vq_n[0:1, 0:1], vq_sq[:], AX.X, ALU.add),
                  waits=[ev_sq])
        e2 = K.op("scalar", lambda s: s.activation(vq_n[0:1, 1:2], vq_n[0:1, 0:1], AF.Sqrt),
                  waits=[e1])
        K.op("vector", lambda vv: vv.tensor_scalar(vq_n[0:1, 2:3], vq_n[0:1, 1:2], 1e-12, None, op0=ALU.max),
             waits=[e2])
        K.op("vector", lambda vv: vv.reciprocal(vq_n[0:1, 3:4], vq_n[0:1, 2:3]))
        ev_vqn = K.op("vector", lambda vv: vv.tensor_scalar(vq_row[:], vq_row[:], vq_n[0:1, 3:4], None, op0=ALU.mult))
        ev_cbsq = K.op("scalar", lambda s: s.activation(cb_sq[:], cb_sb[:], AF.Square),
                       waits=[ev_consts])
        K.op("vector", lambda vv: vv.tensor_reduce(cb_n[:, 0:1], cb_sq[:, 0:64], AX.X, ALU.add),
             waits=[ev_cbsq])
        e3 = K.op("vector", lambda vv: vv.tensor_reduce(cb_n[:, 1:2], cb_sq[:, 64:128], AX.X, ALU.add))
        e4 = K.op("scalar", lambda s: s.activation(cb_n[:, 2:4], cb_n[:, 0:2], AF.Sqrt),
                  waits=[e3])
        K.op("vector", lambda vv: vv.tensor_scalar(cb_n[:, 2:4], cb_n[:, 2:4], 1e-12, None, op0=ALU.max),
             waits=[e4])
        ev_rcb = K.op("vector", lambda vv: vv.reciprocal(cb_n[:, 4:6], cb_n[:, 2:4]))
        mmn = K.mm([5], lambda t: t.matmul(pbig[0][:, 576:640], ones_sb[0:1, :], vq_row[:],
                                           start=True, stop=True), extra=[ev_vqn])
        ev_vqr = K.rd("scalar", [5], lambda s: s.copy(vqn_rep[:], pbig[0][:, 576:640]))
        K.op("vector", lambda vv: vv.tensor_tensor(cb_sc[:, 0:64], cb_sb[:, 0:64], vqn_rep[:], op=ALU.mult),
             waits=[ev_vqr])
        K.op("vector", lambda vv: vv.tensor_tensor(cb_sc[:, 64:128], cb_sb[:, 64:128], vqn_rep[:], op=ALU.mult))
        K.op("vector", lambda vv: vv.tensor_reduce(cb_n[:, 6:7], cb_sc[:, 0:64], AX.X, ALU.add))
        K.op("vector", lambda vv: vv.tensor_reduce(cb_n[:, 7:8], cb_sc[:, 64:128], AX.X, ALU.add))
        ev_scr = K.op("vector", lambda vv: vv.tensor_tensor(cb_n[:, 6:8], cb_n[:, 6:8], cb_n[:, 4:6], op=ALU.mult),
                      waits=[ev_rcb])
        ev_m1 = K.op("vector", lambda vv: vv.tensor_reduce(cb_n[:, 8:9], cb_n[:, 6:8], AX.X, ALU.max))
        mtx = K.mm([4], lambda t: t.transpose(pbig[0][0:1, 0:128], cb_n[:, 8:9], ident_sb[:]),
                   extra=[ev_m1])
        ev_mrow = K.rd("scalar", [4], lambda s: s.copy(vq_n[0:1, 8:136], pbig[0][0:1, 0:128]))
        ev_gmax = K.op("vector", lambda vv: vv.tensor_reduce(vq_n[0:1, 4:5], vq_n[0:1, 8:136], AX.X, ALU.max),
                       waits=[ev_mrow])
        mmg = K.mm([4], lambda t: t.matmul(pbig[0][:, 128:129], ones_sb[0:1, :], vq_n[0:1, 4:5],
                                           start=True, stop=True), extra=[ev_gmax])
        ev_gcol = K.rd("scalar", [4], lambda s: s.copy(vq_gc[:], pbig[0][:, 128:129]))
        ev_ind2 = K.op("vector", lambda vv: vv.tensor_scalar(vq_ind[:], cb_n[:, 6:8], vq_gc[:], None, op0=ALU.is_equal),
                       waits=[ev_gcol, ev_scr])
        for j in range(2):
            K.mm([4], lambda t, j=j: t.matmul(pbig[0][0:1, 192:256], vq_ind[:, j:j + 1],
                                              cb_sb[:, ts(j, 64)], start=(j == 0), stop=(j == 1)),
                 extra=[ev_ind2], first=(j == 0), last=(j == 1))
        ev_coder = K.rd("scalar", [4], lambda s: s.copy(code_row[:], pbig[0][0:1, 192:256]))
        mtc = K.mm([4], lambda t: t.transpose(pbig[0][0:64, 256:257], code_row[:], ident_sb[0:1, 0:1]),
                   extra=[ev_coder])
        ev_codec = K.rd("scalar", [4], lambda s: s.copy(code_col[:], pbig[0][0:64, 256:257]))
        for i, dst in ((0, omix_col), (1, vqmix_col)):
            K.mm([4], lambda t, i=i: t.matmul(pbig[0][:, 300 + i:301 + i], ones_sb[0:1, :],
                                              mixes_sb[0:1, i:i + 1], start=True, stop=True),
                 extra=[ev_consts])
            K.rd("scalar", [4], lambda s, dst=dst, i=i: s.copy(dst[:], pbig[0][:, 300 + i:301 + i]))
        ev_mix = (sc, sc.n)
        for pc in range(DC):
            K.mm([4], lambda t, pc=pc: t.matmul(pbig[0][:, 320 + pc:321 + pc],
                                                vpow_bf[:, ts(pc, 128)], code_col[:],
                                                start=True, stop=True),
                 extra=[ev_vpow, ev_codec], first=(pc == 0), last=(pc == DC - 1))
        K.rd("vector", [4], lambda vv: vv.tensor_tensor(
            vqc_cols[:], pbig[0][:, 320:328], vpobc_sb[:], op=ALU.add), extra=[ev_mix])
        ev_vqc = K.op("vector", lambda vv: vv.tensor_scalar(vqc_cols[:], vqc_cols[:], vqmix_col[:], None, op0=ALU.mult))
        lie_free = [None, None]
        x1_free = [None, None]
        ss_read = None
        def emit_prenorm(tau):
            for pc in range(DC):
                lpar = pc % 2
                mmk = K.mm([lpar], lambda t, pc=pc, tau=tau, lpar=lpar: t.matmul(
                    pb[lpar][:, 0:512], oow_bf[:, ts(pc, 128)], xr_bf[0:64, ts(tau, 512)],
                    start=True, stop=True), extra=[ev_oow, lie_free[lpar]])
                K.rd("vector", [lpar], lambda vv, pc=pc, lpar=lpar: vv.tensor_scalar(
                    ffn_tmp[:], pb[lpar][:, 0:512], oobc_sb[:, pc:pc + 1], omix_col[:],
                    op0=ALU.add, op1=ALU.mult))
                prt = pre[:, pc * T + tau * 512: pc * T + tau * 512 + 512]
                ev_pre = K.op("vector", lambda vv, pc=pc, tau=tau, prt=prt: vv.scalar_tensor_tensor(
                    prt, ffn_tmp[:], vqc_cols[:, pc:pc + 1],
                    xn_sb[:, pc * T + tau * 512: pc * T + tau * 512 + 512],
                    op0=ALU.add, op1=ALU.add))
                lie_free[lpar] = ev_pre
                ev_sq = K.op("scalar", lambda s, prt=prt: s.activation(sq_t[:], prt, AF.Square),
                             waits=[ev_pre, state.get("l1_sq_free")])
                state["l1_sq_free"] = K.mm(
                    [2], lambda t, pc=pc: t.matmul(pb[2][0:1, 0:512], ones_sb[:, 0:1], sq_t[:],
                                                   start=(pc == 0), stop=(pc == DC - 1)),
                    extra=[ev_sq], first=(pc == 0), last=True)

        emit_prenorm(0)
        for tau in range(TQ):
            e1 = K.rd("vector", [2], lambda vv: vv.tensor_scalar(
                rowC[0:1, 0:512], pb[2][0:1, 0:512], 1.0 / D, RMS_EPS, op0=ALU.mult, op1=ALU.add))
            e2 = K.op("scalar", lambda s: s.activation(rowC[0:1, 0:512], rowC[0:1, 0:512],
                                                       AF.Sqrt), waits=[e1])
            e3 = K.op("vector", lambda vv: vv.reciprocal(rowC[0:1, 0:512], rowC[0:1, 0:512]),
                      waits=[e2])
            if tau + 1 < TQ:
                emit_prenorm(tau + 1)
            K.mm([3], lambda t: t.matmul(pb[3][:, 0:512], ones_sb[0:1, :], rowC[0:1, 0:512],
                                         start=True, stop=True), extra=[e3])
            for pc in range(DC):
                xpar = pc % 2
                prt = pre[:, pc * T + tau * 512: pc * T + tau * 512 + 512]
                ev_xl = K.op("sync", lambda e, pc=pc, tau=tau, xpar=xpar: e.dma_start(
                    out=x1t[xpar][:], in_=x_d[:, pc, ts(tau, 512)]),
                    waits=[x1_free[xpar]], dma=True, dma_sem=f"d_x{xpar}")
                ev_sil = K.op("scalar", lambda s, pc=pc, tau=tau: s.activation(
                    gsil_t[:], sg[:, pc * T + tau * 512: pc * T + tau * 512 + 512], AF.Silu),
                    waits=[state[("ffn_v", 1)], state.get("l1_gsil_free")])
                ev_s1 = K.rd("vector", [3], lambda vv, prt=prt: vv.tensor_tensor(
                    s1_t[:], prt, pb[3][:, 0:512], op=ALU.mult))
                ev_s1b = K.op("vector", lambda vv, pc=pc: vv.scalar_tensor_tensor(
                    s1_t[:], s1_t[:], nwc_sb[:, pc:pc + 1], gsil_t[:], op0=ALU.mult, op1=ALU.mult),
                    waits=[ev_sil])
                state["l1_gsil_free"] = ev_s1b
                ev_fin = K.op("vector", lambda vv, xpar=xpar: vv.tensor_tensor(
                    x1t[xpar][:], x1t[xpar][:], s1_t[:], op=ALU.add), waits=[ev_xl])
                x1_free[xpar] = K.op("sync", lambda e, pc=pc, tau=tau, xpar=xpar: e.dma_start(
                    out=out_fm[:, pc, ts(tau, 512)], in_=x1t[xpar][:]), waits=[ev_fin],
                    dma=True, dma_sem=f"d_xs{xpar}")

    with nc.sbuf_tensor("xs0", [128, DC * T], F32) as xs0, \
         nc.sbuf_tensor("xn0", [128, DC * T], BF16) as xn0, \
         nc.sbuf_tensor("zero_sb", [128, 1024], BF16) as zero_sb, \
         nc.sbuf_tensor("lnsq0", [128, 512], F32) as lnsq0, \
         nc.sbuf_tensor("lnsq1", [128, 512], F32) as lnsq1, \
         nc.sbuf_tensor("rowA0", [1, 512], F32) as rowA0, \
         nc.sbuf_tensor("rowB0", [1, 512], F32) as rowB0:
        ev_zmem = K.op("vector", lambda vv: vv.memset(zero_sb[:], 0.0))
        for i in range(E * CAP // 128):
            K.op("sync", lambda e, i=i: e.dma_start(
                out=xg_d[i * 128:(i + 1) * 128, :], in_=zero_sb[:]), waits=[ev_zmem], dma=True)
        phase_ln(0, x0.rearrange("(dc p) t -> p dc t", p=128), xs0, xn0, False, (lnsq0, lnsq1), rowA0, rowB0)
        K.bar()

    with contextlib.ExitStack() as stf0:
        sbf = lambda n, sh, dt, td=False: stf0.enter_context(
            nc.sbuf_tensor(n, sh, dt, is_transpose_dest=td))
        xgfm0 = (sbf("xgfm00", [128, DC * CAP], BF16, True), sbf("xgfm01", [128, DC * CAP], BF16, True))
        hact0 = sbf("hact0", [128, HC * CAP], BF16)
        w1a = tuple(sbf(f"w1a{i}", [128, DC * 256], BF16) for i in range(6))
        w2a = tuple(sbf(f"w2a{i}", [128, 1024], BF16) for i in range(6))
        ytma = (sbf("ytma0", [128, (CAP // 128) * 1024], BF16), sbf("ytma1", [128, (CAP // 128) * 1024], BF16))
        xntma = (sbf("xntma0", [128, 1024], BF16, True), sbf("xntma1", [128, 1024], BF16, True))
        b2ra = (sbf("b2ra0", [1, D], F32), sbf("b2ra1", [1, D], F32))
        phase_ffn_sparse(0, xgfm0, hact0, w1a, w2a, ytma, xntma, b2ra, True)
        K.bar()
    if sg_d is not None:
        K.op("sync", lambda e: e.dma_start(out=sg_d, in_=sg[:]), dma=True)
        K.bar()

    with contextlib.ExitStack() as st3:
        sb3 = lambda n, sh, dt, td=False: st3.enter_context(
            nc.sbuf_tensor(n, sh, dt, is_transpose_dest=td))
        xna = sb3("xna", [128, DC * T], BF16)
        av_sb = sb3("av", [128, DC * T], BF16)
        wpT0 = sb3("wpT0", [128, DC * 128], BF16)
        wpT1 = sb3("wpT1", [128, DC * 128], BF16)
        attnTa = sb3("attnTa", [128, TC * 512], BF16)
        attnTb = sb3("attnTb", [128, TC * 512], BF16)
        Qba = sb3("Qba", [128, T], BF16)
        Qbb = sb3("Qbb", [128, T], BF16)
        Kba = sb3("Kba", [128, T], BF16)
        Kbb = sb3("Kbb", [128, T], BF16)
        Vba = sb3("Vba", [128, TC * 65], BF16)
        Vbb = sb3("Vbb", [128, TC * 65], BF16)
        qkvb0 = sb3("qkvb0", [128, 3 * 1024], BF16)
        qkvb1 = qkvb0
        rrepa = sb3("rrepa", [HD, 512], F32)
        rrepb = sb3("rrepb", [HD, 512], F32)
        nmcols = sb3("nmcols", [128, 10], F32)
        rrow = sb3("rrow", [1, 512], F32)
        x0t0 = sb3("x0t0", [128, 512], F32)
        x0t1 = sb3("x0t1", [128, 512], F32)
        phase_attn(xna, av_sb, (wpT0, wpT1), (attnTa, attnTb), (Qba, Qbb),
                   (Kba, Kbb), (Vba, Vbb), (qkvb0, qkvb1), (rrepa, rrepb),
                   nmcols, rrow, (x0t0, x0t1))
        K.bar()

    with nc.sbuf_tensor("xs1", [128, DC * T], F32) as xs1, \
         nc.sbuf_tensor("xn1", [128, DC * T], BF16) as xn1, \
         nc.sbuf_tensor("lnsq0b", [128, 512], F32) as lnsq0b, \
         nc.sbuf_tensor("lnsq1b", [128, 512], F32) as lnsq1b, \
         nc.sbuf_tensor("rowA1", [1, 512], F32) as rowA1, \
         nc.sbuf_tensor("rowB1", [1, 512], F32) as rowB1:
        phase_ln(1, x_d, xs1, xn1, True, (lnsq0b, lnsq1b), rowA1, rowB1)
        K.bar()

    with contextlib.ExitStack() as stf1:
        sbf = lambda n, sh, dt, td=False: stf1.enter_context(
            nc.sbuf_tensor(n, sh, dt, is_transpose_dest=td))
        xgfm1 = (sbf("xgfm10", [128, DC * CAP], BF16, True), sbf("xgfm11", [128, DC * CAP], BF16, True))
        hact1 = sbf("hact1", [128, HC * CAP], BF16)
        w1b = tuple(sbf(f"w1b{i}", [128, DC * 256], BF16) for i in range(6))
        w2b = tuple(sbf(f"w2b{i}", [128, 1024], BF16) for i in range(6))
        ytmb = (sbf("ytmb0", [128, (CAP // 128) * 1024], BF16), sbf("ytmb1", [128, (CAP // 128) * 1024], BF16))
        xntmb = (sbf("xntmb0", [128, 1024], BF16, True), sbf("xntmb1", [128, 1024], BF16, True))
        b2rb = (sbf("b2rb0", [1, D], F32), sbf("b2rb1", [1, D], F32))
        phase_ffn_sparse(1, xgfm1, hact1, w1b, w2b, ytmb, xntmb, b2rb, False, precast=False)
        K.bar()
    if sg_d1 is not None:
        K.op("sync", lambda e: e.dma_start(out=sg_d1, in_=sg[:]), dma=True)
        K.bar()

    with contextlib.ExitStack() as st6:
        sb6 = lambda n, sh, dt: st6.enter_context(nc.sbuf_tensor(n, sh, dt))
        xnb = sb6("xnb", [128, DC * T], BF16)
        pre = sb6("pre", [128, DC * T], BF16)
        xr_bf = sb6("xr", [64, T], BF16)
        oow_bf = sb6("oowb", [HD, D], BF16)
        vpow_bf = sb6("vpowb", [HD, D], BF16)
        opwb0 = sb6("opwb0", [128, 2016], BF16)
        opwb1 = sb6("opwb1", [128, 2016], BF16)
        opwb2 = sb6("opwb2", [128, 2016], BF16)
        vpiw_bf = sb6("vpiwb", [128, DC * HD], F32)
        params_sb = sb6("params", [1, 2016], F32)
        om_u = sb6("om_u", [64, 64], F32)
        om_ut = sb6("om_ut", [64, 64], F32)
        om = sb6("om", [64, 64], F32)
        om_n = sb6("om_n", [64, 64], F32)
        om2 = sb6("om2", [64, 64], F32)
        om3 = sb6("om3", [64, 64], F32)
        Gt = sb6("Gt", [64, 64], F32)
        G_bf = sb6("G_bf", [64, 64], BF16)
        vq_row = sb6("vq_row", [1, 64], F32)
        vq_sq = sb6("vq_sq", [1, 64], F32)
        vq_n = sb6("vq_n", [1, 256], F32)
        cb_sq = sb6("cb_sq", [128, 128], F32)
        cb_n = sb6("cb_n", [128, 9], F32)
        cb_sc = sb6("cb_sc", [128, 128], F32)
        vqn_rep = sb6("vqn_rep", [128, 64], F32)
        vq_gc = sb6("vq_gc", [128, 1], F32)
        vq_ind = sb6("vq_ind", [128, 2], F32)
        code_row = sb6("code_row", [1, 64], F32)
        code_col = sb6("code_col", [64, 1], BF16)
        omix_col = sb6("omix", [128, 1], F32)
        vqmix_col = sb6("vqmix", [128, 1], F32)
        vqc_cols = sb6("vqc_cols", [128, DC], F32)
        sq_t = sb6("sq_t", [128, 512], F32)
        gsil_t = sb6("gsil_t", [128, 512], F32)
        s1_t = sb6("s1_t", [128, 512], F32)
        x1t0 = sb6("x1t0", [128, 512], F32)
        x1t1 = sb6("x1t1", [128, 512], F32)
        vqs = (params_sb, opb_sb, om_u, om_ut, om, om_n, om2, om3, Gt, G_bf, vq_row,
               vq_sq, vq_n, cb_sq, cb_n, cb_sc, vqn_rep, vq_gc, vq_ind,
               code_row, code_col, omix_col, vqmix_col, vqc_cols)
        phase_l1small(xnb, pre, xr_bf, oow_bf, vpow_bf, (opwb0, opwb1, opwb2), vpiw_bf,
                      sq_t, gsil_t, s1_t, (x1t0, x1t1), vqs)

    K.bar()

    with nc.Block() as block:
        K.replay(block)
    stk.close()
    return nc


_NC_CACHE = {}


def get_nc(dbg=()):
    key = tuple(sorted(dbg))
    if key not in _NC_CACHE:
        _NC_CACHE[key] = build(key)
    return _NC_CACHE[key]


def host_inputs(inputs):
    f = np.float32
    a = lambda x: np.ascontiguousarray(x, dtype=f)
    g = {}
    g["ident"] = np.eye(128, dtype=f)
    g["ones"] = np.ones((128, 128), dtype=f)
    g["Wqkv"] = a(inputs["L0_attn_Wqkv"])
    bqkv = np.asarray(inputs["L0_attn_bqkv"], dtype=f)
    g["bqkvT"] = a(bqkv.T)
    g["bqkv_flat"] = a(bqkv.reshape(1, -1))
    g["WpT"] = a(np.asarray(inputs["L0_attn_Wp"], dtype=f).T)
    g["bp_cols"] = a(np.asarray(inputs["L0_attn_bp"], dtype=f).reshape(8, 128).T)
    g["Wg0"] = a(inputs["L0_ffn_Wg"])
    g["bg0"] = a(np.asarray(inputs["L0_ffn_bg"], dtype=f).reshape(1, -1))
    g["W1_0"] = a(inputs["L0_ffn_W1"])
    g["W2_0"] = a(inputs["L0_ffn_W2"])
    g["Wg1"] = a(np.asarray(inputs["L1_ffn_gate_w"], dtype=f).T)
    g["bg1"] = a(np.asarray(inputs["L1_ffn_gate_b"], dtype=f).reshape(1, -1))
    g["W1_1"] = a(np.asarray(inputs["L1_ffn_w1"], dtype=f).transpose(0, 2, 1))
    g["W2_1"] = a(np.asarray(inputs["L1_ffn_w2"], dtype=f).transpose(0, 2, 1))
    g["b1c"] = a(np.asarray(inputs["L1_ffn_b1"], dtype=f).reshape(8, 32, 128).transpose(0, 2, 1))
    g["b2c"] = a(np.asarray(inputs["L1_ffn_b2"], dtype=f).reshape(8, 8, 128).transpose(0, 2, 1))
    g["opwT"] = a(np.asarray(inputs["L1_attn_omega_proj_w"], dtype=f).T)
    g["opb"] = a(np.asarray(inputs["L1_attn_omega_proj_b"], dtype=f).reshape(1, -1))
    g["oowT"] = a(np.asarray(inputs["L1_attn_omega_out_w"], dtype=f).T)
    g["oobc"] = a(np.asarray(inputs["L1_attn_omega_out_b"], dtype=f).reshape(8, 128).T)
    g["vpiwT"] = a(np.asarray(inputs["L1_attn_vq_proj_in_w"], dtype=f).T)
    g["vpib"] = a(np.asarray(inputs["L1_attn_vq_proj_in_b"], dtype=f).reshape(1, -1))
    g["vpowT"] = a(np.asarray(inputs["L1_attn_vq_proj_out_w"], dtype=f).T)
    g["vpobc"] = a(np.asarray(inputs["L1_attn_vq_proj_out_b"], dtype=f).reshape(8, 128).T)
    g["lt128"] = np.triu(np.ones((128, 128), dtype=f))
    g["ecm1"] = np.tile((np.arange(8, dtype=f) * 384.0 - 1.0)[None, :], (128, 1))
    g["b2r"] = a(np.asarray(inputs["L1_ffn_b2"], dtype=f).reshape(8, 1, 1024))
    cb = np.asarray(inputs["L1_attn_codebook"], dtype=f)
    g["cbp"] = a(np.concatenate([cb[0:128], cb[128:256]], axis=1))
    g["mixes"] = np.array([[float(inputs["L1_attn_omega_mix"]),
                            float(inputs["L1_attn_vq_mix"])]], dtype=f)
    g["nwc"] = a(np.asarray(inputs["L1_attn_norm_w"], dtype=f).reshape(8, 128).T)
    return g


def kernel(**inputs):
    nc = get_nc()
    shared = host_inputs(inputs)
    x = np.asarray(inputs["L0_attn_x"], dtype=np.float32)
    in_maps = []
    for b in range(NCORES):
        m = dict(shared)
        m["x0"] = np.ascontiguousarray(x[b].T)
        in_maps.append(m)
    res = run_bass_kernel_spmd(nc, in_maps, core_ids=list(range(NCORES)))
    outs = []
    for b in range(NCORES):
        o_ = np.asarray(res.results[b]["out_fm"])
        outs.append(o_.transpose(1, 0, 2).reshape(D, T).T)
    return np.stack(outs, axis=0)


# version 21
# speedup vs baseline: 1.2077x; 1.0040x over previous
import contextlib
import numpy as np
import concourse.bass as bass
import concourse.mybir as mybir
from concourse.bass import IndirectOffsetOnAxis
from concourse.bass_utils import run_bass_kernel_spmd

F32 = mybir.dt.float32
BF16 = mybir.dt.bfloat16
AF = mybir.ActivationFunctionType
ALU = mybir.AluOpType
AX = mybir.AxisListType

D, T, H, E, NH, HD = 1024, 2048, 4096, 8, 16, 64
DC, TQ, TC, TH, HC = 8, 4, 16, 2, 32
NCORES = 8
RMS_EPS = 1e-6
LN_EPS = 1e-5
CAP = 384
I32 = mybir.dt.int32


def ts(i, n):
    return slice(i * n, (i + 1) * n)


class Ctr:
    def __init__(self, h):
        self.h = h
        self.n = 0


class B:
    ENGINES = ("tensor", "vector", "scalar", "gpsimd", "sync")

    def __init__(self, nc, stk):
        self.nc = nc
        self.stk = stk
        self.q = {e: [] for e in self.ENGINES}
        self.ctr = {}
        for name in ("pe", "v", "sc", "gp", "dg", "ds"):
            self.ctr[name] = Ctr(stk.enter_context(nc.semaphore("s_" + name)))
        self.eng2ctr = {"tensor": "pe", "vector": "v", "scalar": "sc",
                        "gpsimd": "gp", "sync": None}
        self.serial_dma = {id(self.ctr["dg"]), id(self.ctr["ds"])}
        self.bank = {i: {"w": None, "r": {}} for i in range(8)}

    def dsem(self, name):
        if name not in self.ctr:
            self.ctr[name] = Ctr(self.stk.enter_context(self.nc.semaphore(name)))
        return self.ctr[name]

    def op(self, eng, builder, waits=(), inc=None, dma=False, dma_sem=None):
        ww = {}
        waits = list(waits)
        if eng in ("vector", "scalar") and not dma:
            cs = self.ctr[self.eng2ctr[eng]]
            if cs.n > 0:
                waits.append((cs, cs.n))
        for w in waits:
            if w is None:
                continue
            c, v = w
            if v is None:
                continue
            if id(c) in self.serial_dma:
                v = c.n
            ww[id(c)] = (c.h, max(v, ww[id(c)][1]) if id(c) in ww else v)
        ww = list(ww.values())
        incs = []
        if dma:
            if dma_sem is not None:
                c = self.dsem(dma_sem)
            else:
                c = self.ctr["dg" if eng == "gpsimd" else "ds"]
            c.n += 16
            incs = [(c.h, 16)]
            mark = (c, c.n)
        elif inc is False:
            mark = None
        else:
            c = self.ctr[self.eng2ctr[eng]]
            assert c is not None
            c.n += 1
            incs = [(c.h, 1)]
            mark = (c, c.n)

        def go(engine, ww=ww, incs=incs, builder=builder):
            for h, v in ww:
                engine.wait_ge(h, v)
            ins = builder(engine)
            for h, k in incs:
                ins.then_inc(h, k)
        self.q[eng].append(go)
        return mark

    def wait_only(self, eng, waits):
        ww = []
        for w in waits:
            if w is None or w[1] is None:
                continue
            vv_ = w[0].n if id(w[0]) in self.serial_dma else w[1]
            ww.append((w[0].h, vv_))

        def go(engine, ww=ww):
            for h, v in ww:
                engine.wait_ge(h, v)
        self.q[eng].append(go)

    def marks_all(self):
        return [(c, c.n) for c in self.ctr.values()]

    def bar(self):
        tgt = self.marks_all()
        for eng in self.ENGINES:
            self.wait_only(eng, tgt)
        for b in self.bank.values():
            b["w"] = None
            b["r"] = {}

    def bank_waits_for_write(self, banks):
        w = []
        for b in banks:
            w += list(self.bank[b]["r"].values())
        return w

    def note_write(self, banks, mark):
        for b in banks:
            self.bank[b]["w"] = mark
            self.bank[b]["r"] = {}

    def read_waits(self, banks):
        return [self.bank[b]["w"] for b in banks if self.bank[b]["w"] is not None]

    def note_read(self, banks, eng, mark):
        for b in banks:
            self.bank[b]["r"][eng] = mark

    def mm(self, banks, builder, extra=(), first=True, last=True, force_inc=False):
        waits = list(extra)
        if first:
            waits += self.bank_waits_for_write(banks)
        mark = self.op("tensor", builder,
                       waits=waits, inc=(None if (last or force_inc) else False))
        if last:
            self.note_write(banks, mark)
        return mark if (last or force_inc) else None

    def rd(self, eng, banks, builder, extra=()):
        waits = list(extra) + self.read_waits(banks)
        mark = self.op(eng, builder, waits=waits)
        self.note_read(banks, eng, mark)
        return mark

    def replay(self, block):
        for eng in self.ENGINES:
            items = self.q[eng]
            if not items:
                continue

            def body(engine, items=items):
                for fn in items:
                    fn(engine)
            getattr(block, eng)(body)


def build(dbg=()):
    nc = bass.Bass("TRN2", target_bir_lowering=False, debug=False)

    def din(name, shape, dtype=F32):
        return nc.dram_tensor(name, shape, dtype, kind="ExternalInput").ap()

    x0 = din("x0", [D, T])
    ident = din("ident", [128, 128])
    ones = din("ones", [128, 128])
    Wqkv = din("Wqkv", [48, D, HD])
    bqkvT = din("bqkvT", [HD, 48])
    bqkv_flat = din("bqkv_flat", [1, 48 * HD])
    WpT = din("WpT", [D, D])
    bp_cols = din("bp_cols", [128, DC])
    Wg_l = [din("Wg0", [D, E]), din("Wg1", [D, E])]
    bg_l = [din("bg0", [1, E]), din("bg1", [1, E])]
    W1_l = [din("W1_0", [E, D, H]), din("W1_1", [E, D, H])]
    W2_l = [din("W2_0", [E, H, D]), din("W2_1", [E, H, D])]
    b1c = din("b1c", [E, 128, 32])
    b2c = din("b2c", [E, 128, DC])
    opwT = din("opwT", [D, 2016])
    opb = din("opb", [1, 2016])
    oowT = din("oowT", [HD, D])
    oobc = din("oobc", [128, DC])
    vpiwT = din("vpiwT", [D, HD])
    vpib = din("vpib", [1, HD])
    vpowT = din("vpowT", [HD, D])
    vpobc = din("vpobc", [128, DC])
    lt128 = din("lt128", [128, 128])
    ecm1 = din("ecm1", [128, E])
    b2r = din("b2r", [E, 1, D])
    cbp = din("cbp", [128, 128])
    mixes = din("mixes", [1, 2])
    nwc = din("nwc", [128, DC])

    ok = lambda n: "ExternalOutput" if n in dbg else "Internal"
    xn_d = nc.dram_tensor("xn_d", [128, DC, T], BF16, kind=ok("xn_d")).ap()
    x_d = nc.dram_tensor("x_d", [128, DC, T], F32, kind=ok("x_d")).ap()
    sg_d = nc.dram_tensor("sg_dump", [128, DC * T], BF16, kind="ExternalOutput").ap() if "sg" in dbg else None
    sg_d1 = nc.dram_tensor("sg_dump1", [128, DC * T], BF16, kind="ExternalOutput").ap() if "sg1" in dbg else None
    out_fm = nc.dram_tensor("out_fm", [128, DC, T], F32, kind="ExternalOutput").ap()
    ind_d = nc.dram_tensor("ind_d", [8, T], F32, kind="Internal").ap()
    xg_d = nc.dram_tensor("xg_d", [E * CAP, D], BF16, kind="Internal").ap()
    w1bf_d = nc.dram_tensor("w1bf_d", [E, D, H], BF16, kind="Internal").ap()
    w2bf_d = nc.dram_tensor("w2bf_d", [E, H, D], BF16, kind="Internal").ap()
    ytm_d = nc.dram_tensor("ytm_d", [E * CAP, D], BF16, kind="Internal").ap()
    gtm_d = nc.dram_tensor("gtm_d", [T, D], BF16, kind="Internal").ap()

    stk = contextlib.ExitStack()
    sbuf = lambda name, shape, dtype: stk.enter_context(nc.sbuf_tensor(name, shape, dtype))
    K = B(nc, stk)
    pe, v, sc, dg, ds = (K.ctr[k] for k in ("pe", "v", "sc", "dg", "ds"))

    pb01 = [stk.enter_context(nc.psum_tensor(f"pb{i}", [128, 512], F32)) for i in range(2)]
    p4 = stk.enter_context(nc.psum_tensor("p4", [128, 2048], F32))
    pbig1_t = stk.enter_context(nc.psum_tensor("pbig1", [128, 1024], F32))
    pb = [pb01[0].ap() if hasattr(pb01[0], "ap") else pb01[0], None, None, None]
    pb[0] = pb01[0]
    pb[1] = pb01[1]
    pb[2] = p4[:, 0:512]
    pb[3] = p4[:, 512:1024]
    pbig = [p4[:, 1024:2048], pbig1_t]

    sg = stk.enter_context(nc.sbuf_tensor("sg", [128, DC * T], BF16, is_transpose_dest=True))
    lt_sb = sbuf("lt_sb", [128, 128], F32)
    ecm1_sb = sbuf("ecm1_sb", [128, E], F32)
    ind_all = sbuf("ind_all", [128, TC * E], F32)
    cs_rows = sbuf("cs_rows", [1, 128], F32)
    bases = sbuf("bases", [1, 128], F32)
    tmp8 = sbuf("tmp8", [128, E], F32)
    tcol = sbuf("tcol", [128, 1], F32)
    tsl_sb = sbuf("tsl_sb", [128, TC], I32)
    pc_stage = [sbuf(f"pc_stage{i}", [128, 2048], BF16) for i in range(2)]
    ident_sb = sbuf("ident_sb", [128, 128], F32)
    ones_sb = sbuf("ones_sb", [128, 128], F32)
    ind_fm = sbuf("ind_fm", [8, T], F32)
    mrow_sb = sbuf("mrow_sb", [1, 1024], F32)
    hm_cols = sbuf("hm_cols", [128, DC], F32)
    hm_bf = sbuf("hm_bf", [128, DC], BF16)
    bg_rep = sbuf("bg_rep", [128, E], F32)
    rowC = sbuf("rowC", [1, 512], F32)
    ltm = sbuf("ltm", [128, E], F32)
    mxcol = sbuf("mxcol", [128, 1], F32)
    ind_tm = sbuf("ind_tm", [128, E], F32)
    wg_bf = sbuf("wg_bf", [128, DC * E], F32)
    rstdc_sb = sbuf("rstdc_sb", [128, 4], F32)
    bgrow = sbuf("bgrow", [1, E], F32)
    bp_sb = sbuf("bp_sb", [128, DC], F32)
    bqkvT_sb = sbuf("bqkvT_sb", [HD, 48], F32)
    oobc_sb = sbuf("oobc_sb", [128, DC], F32)
    vpobc_sb = sbuf("vpobc_sb", [128, DC], F32)
    nwc_sb = sbuf("nwc_sb", [128, DC], F32)
    opb_sb = sbuf("opb_sb", [1, 2016], F32)
    vpib_sb = sbuf("vpib_sb", [1, HD], F32)
    mixes_sb = sbuf("mixes_sb", [1, 2], F32)
    cb_sb = sbuf("cb_sb", [128, 128], F32)
    b1c_sb = [sbuf(f"b1c_sb{i}", [128, 32], F32) for i in range(2)]
    b2c_sb = [sbuf(f"b2c_sb{i}", [128, DC], F32) for i in range(2)]
    ffn_tmp = sbuf("ffn_tmp", [128, 512], F32)

    state = {}

    for dst, src in [
        (ident_sb[:], ident), (ones_sb[:], ones), (bp_sb[:], bp_cols),
        (bqkvT_sb[:], bqkvT), (oobc_sb[:], oobc),
        (vpobc_sb[:], vpobc), (nwc_sb[:], nwc), (opb_sb[:], opb),
        (vpib_sb[:], vpib), (mixes_sb[:], mixes), (cb_sb[:], cbp),
        (lt_sb[:], lt128), (ecm1_sb[:], ecm1),
    ]:
        ev_consts = K.op("sync", lambda e, d=dst, s=src: e.dma_start(out=d, in_=s), dma=True)

    def phase_ln(l, xsrc, xs, xn_sb, want_hm, ln_sq, rowA, rowB):
        ev_x = K.op("sync", lambda e: e.dma_start(
            out=xs[:].rearrange("p (dc t) -> p dc t", dc=DC), in_=xsrc), dma=True)
        ev_wg = K.op("sync", lambda g: g.dma_start(
            out=wg_bf[:].rearrange("p (dc e) -> p dc e", dc=DC),
            in_=Wg_l[l].rearrange("(dc p) e -> p dc e", p=128)), dma=True)
        ev_bg = K.op("sync", lambda e: e.dma_start(out=bgrow[:], in_=bg_l[l]), dma=True)
        m = K.mm([0], lambda t: t.matmul(pb[0][:, 0:E], ones_sb[0:1, :], bgrow[:],
                                         start=True, stop=True),
                 extra=[ev_bg, ev_consts])
        ev_bgrep = K.rd("scalar", [0], lambda s: s.copy(bg_rep[:], pb[0][:, 0:E]))

        def emit_stats(tau):
            for dc in range(DC):
                par = dc % 2
                xt = xs[:, dc * T + tau * 512: (dc * T + tau * 512) + 512]
                sq_ev = K.op("scalar", lambda s, a=ln_sq[par], b=xt: s.activation(a[:], b, AF.Square),
                             waits=[ev_x, state.get(("lnsq", par))])
                K.mm([0], lambda t, b=xt, dc=dc: t.matmul(
                    pb[0][0:1, 0:512], ones_sb[:, 0:1], b, start=(dc == 0), stop=(dc == DC - 1)),
                    extra=[ev_x, ev_consts], first=(dc == 0), last=(dc == DC - 1))
                m2 = K.mm([1], lambda t, a=ln_sq[par], dc=dc: t.matmul(
                    pb[1][0:1, 0:512], ones_sb[:, 0:1], a[:], start=(dc == 0), stop=(dc == DC - 1)),
                    extra=[sq_ev], first=(dc == 0), last=True)
                state[("lnsq", par)] = m2

        emit_stats(0)
        for tau in range(TQ):
            mrow = rowA[0:1, 0:512]
            rrow = rowB[0:1, 0:512]
            e1 = K.rd("vector", [0], lambda vv, mr=mrow: vv.tensor_scalar(
                mr, pb[0][0:1, 0:512], 1.0 / D, None, op0=ALU.mult),
                extra=[state.get("ln_mrow_free")])
            e2 = K.rd("vector", [1], lambda vv: vv.tensor_scalar(
                rowC[:], pb[1][0:1, 0:512], 1.0 / D, LN_EPS, op0=ALU.mult, op1=ALU.add),
                extra=[e1])
            e3 = K.op("vector", lambda vv, mr=mrow, rr=rrow: vv.tensor_tensor(rr, mr, mr, op=ALU.mult),
                      waits=[e2, state.get("ln_rrow_free")])
            e4 = K.op("vector", lambda vv, rr=rrow: vv.tensor_tensor(rr, rowC[:], rr, op=ALU.subtract),
                      waits=[e3])
            e5 = K.op("scalar", lambda s, rr=rrow: s.activation(rr, rr, AF.Sqrt),
                      waits=[e4])
            e6 = K.op("vector", lambda vv, rr=rrow: vv.reciprocal(rr, rr), waits=[e5])
            if tau + 1 < TQ:
                emit_stats(tau + 1)
            mmm = K.mm([2], lambda t, mr=mrow: t.matmul(
                pb[2][:, 0:512], ones_sb[0:1, :], mr, start=True, stop=True), extra=[e4])
            state["ln_mrow_free"] = mmm
            mmr = K.mm([3], lambda t, rr=rrow: t.matmul(
                pb[3][:, 0:512], ones_sb[0:1, :], rr, start=True, stop=True), extra=[e6])
            for tc4 in range(4):
                mtr = K.mm([7], lambda t, tc4=tc4, rr=rrow: t.transpose(
                    pbig[1][:, 512:513], rr[0:1, ts(tc4, 128)], ident_sb[0:1, 0:1]),
                    extra=[e6])
                K.rd("scalar", [7], lambda s2, tc4=tc4: s2.copy(
                    rstdc_sb[:, tc4:tc4 + 1], pbig[1][:, 512:513]))
            state["ln_rrow_free"] = mtr
            for dc in range(DC):
                xt = xs[:, dc * T + tau * 512: (dc * T + tau * 512) + 512]
                xnt = xn_sb[:, dc * T + tau * 512: (dc * T + tau * 512) + 512]
                K.rd("vector", [2], lambda vv, a=xt: vv.tensor_tensor(a, a, pb[2][:, 0:512], op=ALU.subtract))
                ev_n = K.rd("vector", [3], lambda vv, a=xt, b=xnt: vv.tensor_tensor(
                    b, a, pb[3][:, 0:512], op=ALU.mult))
                K.op("sync", lambda e, b=xnt, dc=dc, tau=tau: e.dma_start(
                    out=xn_d[:, dc, ts(tau, 512)], in_=b), waits=[ev_n], dma=True)
            ev_norm = (v, v.n)
            for tc4 in range(4):
                g_ = tau * 4 + tc4
                for dc in range(DC):
                    K.mm([4], lambda t, dc=dc, g_=g_, tc4=tc4: t.matmul(
                        pbig[0][:, ts(tc4, 8)],
                        xs[:, dc * T + g_ * 128: dc * T + g_ * 128 + 128],
                        wg_bf[:, ts(dc, E)], start=(dc == 0), stop=(dc == DC - 1)),
                        extra=[ev_norm, ev_wg], first=(dc == 0), last=(dc == DC - 1))
                e1 = K.rd("vector", [4], lambda vv, tc4=tc4: vv.scalar_tensor_tensor(
                    ltm[:], pbig[0][:, ts(tc4, 8)], rstdc_sb[:, tc4:tc4 + 1], bg_rep[:],
                    op0=ALU.mult, op1=ALU.add),
                    extra=[ev_bgrep, state.get("ln_ltm_free"), (sc, sc.n)])
                e2 = K.op("vector", lambda vv: vv.reduce_max(mxcol[:], ltm[:], axis=AX.X), waits=[e1])
                K.wait_only("vector", [e2])
                e3 = K.op("vector", lambda vv: vv.tensor_scalar(
                    ind_tm[:], ltm[:], mxcol[:], None, op0=ALU.is_equal),
                    waits=[e2])
                K.op("vector", lambda vv, g_=g_: vv.tensor_copy(
                    ind_all[:, ts(g_, E)], ind_tm[:]))
                state["ln_ltm_free"] = e3
        ev_ind = (v, v.n)
        for tc in range(TC):
            K.mm([6], lambda t, tc=tc: t.matmul(
                pbig[1][0:1, ts(tc, E)], ones_sb[:, 0:1], ind_all[:, ts(tc, E)],
                start=True, stop=True),
                extra=[ev_ind] if tc == 0 else [], first=(tc == 0), last=(tc == TC - 1))
        K.rd("scalar", [6], lambda s: s.copy(cs_rows[:], pbig[1][0:1, 0:128]))
        K.op("vector", lambda vv: vv.memset(bases[0:1, 0:E], 0.0), waits=[(sc, sc.n)])
        for tc in range(1, TC):
            K.op("vector", lambda vv, tc=tc: vv.tensor_tensor(
                bases[0:1, ts(tc, E)], bases[0:1, ts(tc - 1, E)],
                cs_rows[0:1, ts(tc - 1, E)], op=ALU.add))
        ev_bases = (v, v.n)
        for tc in range(TC):
            cpar = tc % 2
            cum_sl = pbig[0][:, cpar * 512: cpar * 512 + E]
            cbank = [4 + cpar]
            K.mm(cbank, lambda t, tc=tc, cum_sl=cum_sl: t.matmul(
                cum_sl, lt_sb[:], ind_all[:, ts(tc, E)], start=True, stop=False),
                extra=[ev_bases, ev_consts], first=True, last=False)
            K.mm(cbank, lambda t, tc=tc, cum_sl=cum_sl: t.matmul(
                cum_sl, ones_sb[0:1, :], bases[0:1, ts(tc, E)], start=False, stop=True),
                first=False, last=True)
            K.rd("vector", cbank, lambda vv, cum_sl=cum_sl: vv.tensor_tensor(
                tmp8[:], cum_sl, ecm1_sb[:], op=ALU.add))
            K.op("vector", lambda vv, tc=tc: vv.tensor_tensor(
                tmp8[:], tmp8[:], ind_all[:, ts(tc, E)], op=ALU.mult))
            K.op("vector", lambda vv: vv.tensor_reduce(tcol[:], tmp8[:], AX.X, ALU.add))
            K.op("vector", lambda vv: vv.tensor_scalar(
                tcol[:], tcol[:], float(E * CAP - 1), None, op0=ALU.min))
            K.op("vector", lambda vv, tc=tc: vv.tensor_copy(tsl_sb[:, tc:tc + 1], tcol[:]))
        state["tsl"] = (v, v.n)
        if want_hm:
            prev = None
            for dc in range(DC):
                prev = K.op("vector", lambda vv, dc=dc: vv.tensor_reduce(
                    hm_cols[:, dc:dc + 1], xn_sb[:, ts(dc, T)], AX.X, ALU.add), waits=[(v, v.n)])
            K.wait_only("vector", [prev])
            state["hm"] = K.op("vector", lambda vv: vv.tensor_scalar(
                hm_bf[:], hm_cols[:], 1.0 / T, None, op0=ALU.mult))

    def phase_ffn(l, xnh, hact, w2buf, w1buf, maskrep, gelu):
        ev_memset = K.op("vector", lambda vv: vv.memset(sg[:], 0.0))
        a_evac = {}
        w1_free = [None] * 6
        w2_free = [None] * 6
        mask_free = None
        hps_free = [None, None]
        b1_free = [None, None]
        b2_free = [None, None]
        ev_b1 = [None, None]
        ev_b2 = [None, None]
        xnh_free = None
        ybanks_sl = [pb[0][:, 0:512], pb[1][:, 0:512], pb[2][:, 0:512], pb[3][:, 0:512],
                     pbig[0][:, 0:512], pbig[0][:, 512:1024],
                     pbig[1][:, 0:512], pbig[1][:, 512:1024]]
        ybank_ids = [[0], [1], [2], [3], [4], [5], [6], [7]]

        for th in range(TH):
            ev_xnh = K.op("sync", lambda e, th=th: e.dma_start(
                out=xnh[:].rearrange("p (dc t) -> p dc t", dc=DC),
                in_=xn_d[:, :, ts(th, 1024)]), waits=[xnh_free], dma=True)
            for e_ in range(E):
                bpar = e_ % 2
                if l == 1:
                    K.op("sync", lambda e, e_=e_, bp=bpar: e.dma_start(
                        out=b1c_sb[bp][:], in_=b1c[e_]), waits=[b1_free[bpar], b2_free[bpar]],
                        dma=True, dma_sem=f"d_b{bpar}")
                    mboth = K.op("sync", lambda e, e_=e_, bp=bpar: e.dma_start(
                        out=b2c_sb[bp][:], in_=b2c[e_]), waits=[],
                        dma=True, dma_sem=f"d_b{bpar}")
                    ev_b1[bpar] = ev_b2[bpar] = mboth
                ev_mrow = K.op("sync", lambda e, e_=e_, th=th: e.dma_start(
                    out=mrow_sb[:], in_=ind_d[e_:e_ + 1, ts(th, 1024)]),
                    waits=[state[("masks", l)], state.get("ffn_mrow_free")], dma=True)
                for seg in range(16):
                    par = seg % 4
                    ev_w1 = K.op("gpsimd", lambda g, e_=e_, seg=seg, par=par: g.dma_start(
                        out=w1buf[par][:].rearrange("p (dc h) -> p dc h", dc=DC),
                        in_=W1_l[l][e_, :, ts(seg, 256)].rearrange("(dc p) h -> p dc h", p=128)),
                        waits=[w1_free[par]], dma=True, dma_sem=f"d_w1{par}")
                    for hc2 in range(2):
                        hc = seg * 2 + hc2
                        for sub in range(2):
                            hb = sub
                            for dc in range(DC):
                                mmk = K.mm([hb], lambda t, par=par, dc=dc, hc2=hc2, sub=sub, hb=hb: t.matmul(
                                    pb[hb][:, 0:512],
                                    w1buf[par][:, dc * 256 + hc2 * 128: dc * 256 + hc2 * 128 + 128],
                                    xnh[:, dc * 1024 + sub * 512: dc * 1024 + sub * 512 + 512],
                                    start=(dc == 0), stop=(dc == DC - 1)),
                                    extra=([ev_w1, ev_xnh] if dc == 0 else []),
                                    first=(dc == 0), last=(dc == DC - 1))
                            if hc2 == 1 and sub == 1:
                                w1_free[par] = mmk
                            hsl = hact[:, hc * 1024 + sub * 512: hc * 1024 + sub * 512 + 512]
                            if gelu:
                                ev = K.rd("scalar", [hb], lambda s, h_=hsl, hb=hb: s.activation(
                                    h_, pb[hb][:, 0:512], AF.Gelu_apprx_tanh))
                            else:
                                ev = K.rd("scalar", [hb], lambda s, h_=hsl, hb=hb, hc=hc, bp=bpar: s.activation(
                                    h_, pb[hb][:, 0:512], AF.Relu, bias=b1c_sb[bp][:, hc:hc + 1]),
                                    extra=[ev_b1[bpar]])
                            a_evac[(hc, sub)] = ev
                xnh_free = (pe, pe.n)
                if l == 1:
                    b1_free[bpar] = (sc, sc.n)
                for q_ in range(2):
                    bq = 2 + q_
                    K.mm([bq], lambda t, q_=q_, bq=bq: t.matmul(
                        pb[bq][:, 0:512], ones_sb[0:1, :], mrow_sb[0:1, ts(q_, 512)],
                        start=True, stop=True), extra=[ev_mrow])
                state["ffn_mrow_free"] = (pe, pe.n)
                K.rd("scalar", [2], lambda s: s.copy(maskrep[:, 0:512], pb[2][:, 0:512]),
                     extra=[mask_free])
                ev_mask = K.rd("scalar", [3], lambda s: s.copy(maskrep[:, 512:1024],
                                                               pb[3][:, 0:512]),
                               extra=[mask_free])
                for dh in range(2):
                    for hc in range(HC):
                        w2p = hc % 6
                        ev_w2 = K.op("gpsimd", lambda g, e_=e_, hc=hc, dh=dh, w2p=w2p: g.dma_start(
                            out=w2buf[w2p][:], in_=W2_l[l][e_, ts(hc, 128), dh * 512:dh * 512 + 512]),
                            waits=[w2_free[w2p]], dma=True, dma_sem=f"d_w2{w2p}")
                        for sub in range(2):
                            for d4 in range(4):
                                yb = sub * 4 + d4
                                mmk = K.mm(ybank_ids[yb], lambda t, hc=hc, sub=sub, d4=d4, w2p=w2p, yb=yb: t.matmul(
                                    ybanks_sl[yb],
                                    w2buf[w2p][:, ts(d4, 128)],
                                    hact[:, hc * 1024 + sub * 512: hc * 1024 + sub * 512 + 512],
                                    start=(hc == 0), stop=(hc == HC - 1)),
                                    extra=([ev_w2, a_evac[(hc, sub)]] if d4 == 0 else []),
                                    first=(hc == 0), last=(hc == HC - 1),
                                    force_inc=(sub == 1 and d4 == 3))
                        w2_free[w2p] = mmk
                    for sub in range(2):
                        for d4 in range(4):
                            yb = sub * 4 + d4
                            dout = dh * 4 + d4
                            sgs = sg[:, dout * T + th * 1024 + sub * 512:
                                     dout * T + th * 1024 + sub * 512 + 512]
                            msl = maskrep[:, ts(sub, 512)]
                            if l == 0:
                                K.rd("vector", ybank_ids[yb], lambda vv, yb=yb, msl=msl: vv.scalar_tensor_tensor(
                                    ffn_tmp[:], ybanks_sl[yb], 0.0, msl, op0=ALU.add, op1=ALU.mult),
                                    extra=[ev_mask, ev_memset])
                            else:
                                K.rd("vector", ybank_ids[yb], lambda vv, yb=yb, msl=msl, dout=dout, bp=bpar: vv.scalar_tensor_tensor(
                                    ffn_tmp[:], ybanks_sl[yb], b2c_sb[bp][:, dout:dout + 1], msl,
                                    op0=ALU.add, op1=ALU.mult),
                                    extra=[ev_mask, ev_memset, ev_b2[bpar]])
                            K.op("vector", lambda vv, sgs=sgs: vv.tensor_tensor(
                                sgs, sgs, ffn_tmp[:], op=ALU.add))
                mask_free = (v, v.n)
                if l == 1:
                    b2_free[bpar] = (v, v.n)
        state[("ffn_v", l)] = (v, v.n)

    def phase_ffn_sparse(l, xgfm, hact_sp, w1buf, w2buf, ytm_sb, xntm, b2row, gelu,
                         precast=False):
        W1_ap = W1_l[l]
        W2_ap = W2_l[l]
        xntm_free = [None, None]
        ev_scat = None
        for tc in range(TC):
            par = tc % 2
            tr = None
            for dc in range(DC):
                tr = K.op("sync", lambda e, dc=dc, tc=tc, par=par: e.dma_start_transpose(
                    out=xntm[par][:, ts(dc, 128)], in_=xn_d[:, dc, ts(tc, 128)]),
                    waits=[xntm_free[par]], dma=True, dma_sem=f"d_xt{par}")
            ev_scat_p = K.op("gpsimd", lambda g, tc=tc, par=par: g.indirect_dma_start(
                out=xg_d, out_offset=IndirectOffsetOnAxis(tsl_sb[:, tc:tc + 1], 0),
                in_=xntm[par][:], in_offset=None),
                waits=[tr, state["tsl"]], dma=True, dma_sem=f"d_scat{par}")
            xntm_free[par] = ev_scat_p
            ev_scat = None
        a_evac = {}
        w1_free = [None] * 6
        w2_free = [None] * 6
        xg_free = [None, None]
        ytm_free = [None, None]
        b1_free = [None, None]
        b2_free = [None, None]
        ysl = [(p4[:, 0:512], [2]), (p4[:, 512:1024], [3]),
               (p4[:, 1024:1536], [4]), (p4[:, 1536:2048], [5]),
               (pbig[1][:, 0:512], [6]), (pbig[1][:, 512:1024], [7])]
        NSC = CAP // 128
        last_store = None
        xg_marks = [None] * E
        b1_marks = [None] * E
        b2_marks = [None] * E

        def emit_loads(e_):
            epar = e_ % 2
            m = None
            for dc in range(DC):
                m = K.op("sync", lambda e2, e_=e_, dc=dc, epar=epar: e2.dma_start_transpose(
                    out=xgfm[epar][:, ts(dc, CAP)], in_=xg_d[e_ * CAP:(e_ + 1) * CAP, ts(dc, 128)]),
                    waits=[xntm_free[0], xntm_free[1], xg_free[epar]],
                    dma=True, dma_sem=f"d_xg{epar}")
            xg_marks[e_] = m
            if l == 1:
                b1_marks[e_] = K.op("sync", lambda e, e_=e_, bp=epar: e.dma_start(
                    out=b1c_sb[bp][:], in_=b1c[e_]), waits=[b1_free[epar]],
                    dma=True, dma_sem=f"d_b{epar}")
                b2_marks[e_] = K.op("sync", lambda e, e_=e_, bp=epar: e.dma_start(
                    out=b2row[bp][:], in_=b2r[e_]), waits=[b2_free[epar]],
                    dma=True, dma_sem=f"d_b2{epar}")

        emit_loads(0)
        for e_ in range(E):
            epar = e_ % 2
            xg_m = xg_marks[e_]
            ev_b1 = b1_marks[e_]
            ev_b2 = b2_marks[e_]
            for seg in range(16):
                par4 = seg % 6
                w1src = (w1bf_d if precast else W1_ap)
                ev_w1 = K.op("gpsimd", lambda g, e_=e_, seg=seg, par4=par4, w1src=w1src: g.dma_start(
                    out=w1buf[par4][:].rearrange("p (dc h) -> p dc h", dc=DC),
                    in_=w1src[e_, :, ts(seg, 256)].rearrange("(dc p) h -> p dc h", p=128)),
                    waits=[w1_free[par4]] + ([state.get("precast_done"), state.get("precast_done2")] if precast else []),
                    dma=True, dma_sem=f"d_w1{par4}")
                for hc2 in range(2):
                    hc = seg * 2 + hc2
                    hb = hc % 2
                    for dc in range(DC):
                        mmk = K.mm([hb], lambda t, par4=par4, dc=dc, hc2=hc2, hb=hb, epar=epar: t.matmul(
                            pb[hb][:, 0:CAP],
                            w1buf[par4][:, dc * 256 + hc2 * 128: dc * 256 + hc2 * 128 + 128],
                            xgfm[epar][:, dc * CAP: dc * CAP + CAP],
                            start=(dc == 0), stop=(dc == DC - 1)),
                            extra=([ev_w1, xg_m] if dc == 0 else []),
                            first=(dc == 0), last=(dc == DC - 1))
                    if hc2 == 1:
                        w1_free[par4] = mmk
                    hsl = hact_sp[:, hc * CAP: hc * CAP + CAP]
                    if gelu:
                        a_evac[hc] = K.rd("scalar", [hb], lambda s, h_=hsl, hb=hb: s.activation(
                            h_, pb[hb][:, 0:CAP], AF.Gelu_apprx_tanh))
                    else:
                        a_evac[hc] = K.rd("scalar", [hb], lambda s, h_=hsl, hb=hb, hc=hc, bp=epar: s.activation(
                            h_, pb[hb][:, 0:CAP], AF.Relu, bias=b1c_sb[bp][:, hc:hc + 1]),
                            extra=[ev_b1])
            xg_free[epar] = (pe, pe.n)
            if l == 1:
                b1_free[epar] = (sc, sc.n)
            if e_ + 1 < E:
                emit_loads(e_ + 1)
            for hc in range(HC):
                w2p = hc % 6
                w2src = (w2bf_d if precast else W2_ap)
                ev_w2 = K.op("gpsimd", lambda g, e_=e_, hc=hc, w2p=w2p, w2src=w2src: g.dma_start(
                    out=w2buf[w2p][:], in_=w2src[e_, ts(hc, 128), :]),
                    waits=[w2_free[w2p]] + ([state.get("precast_done"), state.get("precast_done2")] if precast else []),
                    dma=True, dma_sem=f"d_w2{w2p}")
                for sc4 in range(NSC):
                    for half in range(2):
                        yi = sc4 * 2 + half
                        mmk = K.mm(ysl[yi][1], lambda t, hc=hc, sc4=sc4, half=half, w2p=w2p, yi=yi: t.matmul(
                            ysl[yi][0],
                            hact_sp[:, hc * CAP + sc4 * 128: hc * CAP + sc4 * 128 + 128],
                            w2buf[w2p][:, half * 512: half * 512 + 512],
                            start=(hc == 0), stop=(hc == HC - 1 and l == 0)),
                            extra=([ev_w2, a_evac[hc]] if (sc4 == 0 and half == 0) else [ev_w2] if half == 0 else []),
                            first=(hc == 0), last=(hc == HC - 1 and l == 0),
                            force_inc=(sc4 == NSC - 1 and half == 1))
                w2_free[w2p] = mmk
            if l == 1:
                for sc4 in range(NSC):
                    for half in range(2):
                        yi = sc4 * 2 + half
                        K.mm(ysl[yi][1], lambda t, sc4=sc4, half=half, yi=yi, bp=epar: t.matmul(
                            ysl[yi][0], ones_sb[0:1, :], b2row[bp][0:1, half * 512: half * 512 + 512],
                            start=False, stop=True),
                            extra=[ev_b2] if yi == 0 else [], first=False, last=True)
            for sc4 in range(NSC):
                for half in range(2):
                    yi = sc4 * 2 + half
                    K.rd("vector", ysl[yi][1], lambda vv, sc4=sc4, half=half, yi=yi, epar=epar: vv.tensor_copy(
                        ytm_sb[epar][:, sc4 * 1024 + half * 512: sc4 * 1024 + half * 512 + 512],
                        ysl[yi][0]), extra=[ytm_free[epar]])
            ev_yv = (v, v.n)
            last_store = K.op("scalar", lambda e2, e_=e_, epar=epar: e2.dma_start(
                out=ytm_d[e_ * CAP:(e_ + 1) * CAP, :].rearrange("(sc p) d -> p sc d", p=128),
                in_=ytm_sb[epar][:].rearrange("p (sc d) -> p sc d", sc=NSC)),
                waits=[ev_yv], dma=True, dma_sem=f"d_yt{epar}")
            ytm_free[epar] = last_store
            if l == 1:
                b2_free[epar] = (v, v.n)
        ev_gat = None
        gst_free = [None, None]
        for tc in range(TC):
            gpar = tc % 2
            gm = K.op("gpsimd", lambda g, tc=tc, gpar=gpar: g.indirect_dma_start(
                out=xntm[gpar][:], out_offset=None,
                in_=ytm_d, in_offset=IndirectOffsetOnAxis(tsl_sb[:, tc:tc + 1], 0)),
                waits=[ytm_free[0], ytm_free[1], gst_free[gpar]],
                dma=True, dma_sem=f"d_gat{gpar}")
            ev_gat = K.op("scalar", lambda e2, tc=tc, gpar=gpar: e2.dma_start(
                out=gtm_d[tc * 128:(tc + 1) * 128, :], in_=xntm[gpar][:]),
                waits=[gm], dma=True, dma_sem=f"d_gst{gpar}")
            gst_free[gpar] = ev_gat
        state["gst0"], state["gst1"] = gst_free[0], gst_free[1]
        ev_sg = None
        for dc in range(DC):
            ev_sg = K.op("sync", lambda e2, dc=dc: e2.dma_start_transpose(
                out=sg[:, dc * T:(dc + 1) * T], in_=gtm_d[:, ts(dc, 128)]),
                waits=[ev_gat, state.get("gst0"), state.get("gst1")], dma=True, dma_sem="d_sg")
        state[("ffn_v", l)] = ev_sg

    def phase_attn(xn_sb, av_sb, wpT_buf, attnT2, Qb2, Kb2, Vb2, qkv_bf, rrep2,
                   nmcols, rrow, x0t):
        ev_xn = K.op("sync", lambda e: e.dma_start(
            out=xn_sb[:].rearrange("p (dc t) -> p dc t", dc=DC), in_=xn_d[:, :, :]), dma=True)
        ev_g = None
        for i in range(DC * 4):
            ev_g = K.op("scalar", lambda s, i=i: s.activation(
                sg[:, ts(i, 512)], sg[:, ts(i, 512)], AF.Silu),
                waits=[state[("ffn_v", 0)]])
        for hd in range(2):
            for tc in range(TC):
                K.op("vector", lambda vv, hd=hd, tc=tc: vv.memset(
                    Vb2[hd][:, tc * 65 + 64: tc * 65 + 65], 1.0))
        ev_vones = (v, v.n)

        qkv_w = [None, None]
        qkv_wfree = [None, None]
        kones_free = [None, None]
        x0_free = [None, None]
        prev_av_tt = [None]
        tr_mark = [None, None]
        for hp in range(NH // 2):
            heads = (2 * hp, 2 * hp + 1)
            ev_qk = [None, None]
            ev_kones = [None, None]
            ev_v = [None, None]
            row64 = [[None] * TQ, [None] * TQ]
            for hd in range(2):
                h = heads[hd]
                for kk in range(3):
                    qkv_w[hd] = K.op("gpsimd", lambda g, h=h, hd=hd, kk=kk: g.dma_start(
                        out=qkv_bf[0][:, ts(kk, 1024)].rearrange(
                            "p (dc x) -> p dc x", dc=DC)[:, :, hd * HD:(hd + 1) * HD],
                        in_=Wqkv[kk * 16 + h].rearrange("(dc p) hd2 -> p dc hd2", p=128)),
                        waits=[qkv_wfree[0], qkv_wfree[1]], dma=True, dma_sem=f"d_qkv{hd}")
            ev_pairw = (K.ctr["d_qkv0"], None)
            for which, koff in (("q", 0), ("k", 1)):
                for tau in range(TQ):
                    qpar = tau % 2
                    for dc in range(DC):
                        K.mm([qpar], lambda t, dc=dc, tau=tau, koff=koff, qpar=qpar: t.matmul(
                            pb[qpar][:, 0:512],
                            qkv_bf[0][:, koff * 1024 + dc * 128: koff * 1024 + dc * 128 + 128],
                            xn_sb[:, dc * T + tau * 512: dc * T + tau * 512 + 512],
                            start=(dc == 0), stop=(dc == DC - 1)),
                            extra=([qkv_w[0], qkv_w[1], ev_xn] if dc == 0 else []),
                            first=(dc == 0), last=(dc == DC - 1))
                    for hd in range(2):
                        buf = (Qb2 if which == "q" else Kb2)[hd]
                        bi = (0 if which == "q" else 16) + heads[hd]
                        K.rd("vector", [qpar], lambda vv, buf=buf, tau=tau, bi=bi, qpar=qpar, hd=hd: vv.tensor_scalar(
                            buf[0:HD, ts(tau, 512)], pb[qpar][hd * HD:(hd + 1) * HD, 0:512],
                            bqkvT_sb[:, bi:bi + 1], None, op0=ALU.add),
                            extra=[ev_consts] + ([kones_free[hd]] if which == "k" else []))
            ev_qk = [(v, v.n), (v, v.n)]
            ev_kones = [None, None]
            for hd in range(2):
                ev_kones[hd] = K.op("vector", lambda vv, Kb=Kb2[hd]: vv.memset(Kb[64:65, :], 1.0),
                                    waits=[ev_qk[hd]])
            for tc in range(TC):
                vpar = tc % 2
                for dc in range(DC):
                    K.mm([vpar], lambda t, dc=dc, tc=tc, vpar=vpar: t.matmul(
                        pb[vpar][:, 0:128],
                        xn_sb[:, dc * T + tc * 128: dc * T + tc * 128 + 128],
                        qkv_bf[0][:, 2 * 1024 + dc * 128: 2 * 1024 + dc * 128 + 128],
                        start=(dc == 0), stop=(dc == DC - 1)),
                        extra=([qkv_w[0], qkv_w[1], ev_xn] if dc == 0 else []),
                        first=(dc == 0), last=(dc == DC - 1))
                for hd in range(2):
                    K.rd("vector", [vpar], lambda vv, tc=tc, vpar=vpar, hd=hd, Vb=Vb2[hd]: vv.tensor_copy(
                        Vb[:, tc * 65: tc * 65 + 64], pb[vpar][:, hd * HD:(hd + 1) * HD]),
                        extra=[ev_vones])
            ev_v = [(v, v.n), (v, v.n)]
            qkv_wfree[0] = (pe, pe.n)
            qkv_wfree[1] = (pe, pe.n)

            def emit_p1_scores(hd, qg, sp):
                Qb, Kb = Qb2[hd], Kb2[hd]
                for qt4 in range(4):
                    qt = qg * 4 + qt4
                    for half, bks in ((0, [2, 3]), (1, [4, 5])):
                        for kg in range(2):
                            K.mm(bks, lambda t, qt=qt, half=half, kg=kg, Qb=Qb, Kb=Kb: t.matmul(
                                p4[:, ts(half * 2 + kg, 512)], Qb[0:HD, ts(qt, 128)],
                                Kb[0:HD, ts(half * 2 + kg, 512)], start=True, stop=True),
                                extra=([ev_qk[hd]] if kg == 0 else []),
                                first=(kg == 0), last=(kg == 1))
                        K.rd("vector", bks, lambda vv, half=half: vv.reduce_max(
                            nmcols[:, half:half + 1], p4[:, half * 1024:half * 1024 + 1024],
                            axis=AX.X, negate=True))
                    col = 2 + (sp % 2) * 4 + qt4
                    K.op("vector", lambda vv, col=col: vv.tensor_reduce(
                        nmcols[:, col:col + 1], nmcols[:, 0:2], AX.X, ALU.min),
                        waits=[tr_mark[sp % 2]])

            def emit_p1_finish(hd, qg, sp):
                Qb = Qb2[hd]
                for qt4 in range(4):
                    qt = qg * 4 + qt4
                    col = 2 + (sp % 2) * 4 + qt4
                    mt = K.mm([7], lambda t, col=col: t.transpose(
                        pbig[1][0:1, 512:640], nmcols[:, col:col + 1], ident_sb[:]),
                        extra=[(v, v.n)])
                    row64[hd][qg] = K.rd("scalar", [7], lambda s, qt=qt, Qb=Qb: s.copy(
                        Qb[64:65, ts(qt, 128)], pbig[1][0:1, 512:640]), extra=[ev_kones[hd]])
                tr_mark[sp % 2] = (pe, pe.n)

            emit_p1_scores(0, 0, 0)
            emit_p1_finish(0, 0, 0)
            for step in range(2 * TQ):
                hd, qg = step % 2, step // 2
                h = heads[hd]
                Qb, Kb, Vb = Qb2[hd], Kb2[hd], Vb2[hd]
                attnT = attnT2[hd]
                rrep_sb = rrep2[hd]
                if step + 1 < 2 * TQ:
                    emit_p1_scores((step + 1) % 2, (step + 1) // 2, step + 1)
                exp_marks = []
                for kc in range(TC):
                    spar = kc % 2
                    K.mm([spar], lambda t, kc=kc, qg=qg, spar=spar, Qb=Qb, Kb=Kb: t.matmul(
                        pb[spar][:, 0:512], Kb[0:65, ts(kc, 128)], Qb[0:65, ts(qg, 512)],
                        start=True, stop=True),
                        extra=([row64[hd][qg], ev_kones[hd]] if kc == 0 else []))
                    exp_marks.append(K.rd("scalar", [spar], lambda s, kc=kc, spar=spar, attnT=attnT: s.activation(
                        attnT[:, ts(kc, 512)], pb[spar][:, 0:512], AF.Exp, scale=8.0)))
                for kc in range(TC):
                    K.mm([6], lambda t, kc=kc, attnT=attnT, Vb=Vb: t.matmul(
                        pbig[1][0:65, 0:512], Vb[:, kc * 65: kc * 65 + 65], attnT[:, ts(kc, 512)],
                        start=(kc == 0), stop=(kc == TC - 1)),
                        extra=[exp_marks[kc]] + ([ev_v[hd]] if kc == 0 else []),
                        first=(kc == 0), last=(kc == TC - 1))
                e1 = K.rd("vector", [6], lambda vv: vv.reciprocal(rrow[:], pbig[1][64:65, 0:512]))
                mmr = K.mm([7], lambda t: t.matmul(pbig[1][0:64, 512:1024], ones_sb[0:1, 0:HD],
                                                   rrow[:], start=True, stop=True), extra=[e1])
                ev_rr = K.rd("vector", [7], lambda vv, rrep_sb=rrep_sb: vv.tensor_copy(
                    rrep_sb[:], pbig[1][0:64, 512:1024]), extra=[prev_av_tt[0]])
                cc, ph = h // 2, h % 2
                bv_col = bqkvT_sb[:, 32 + h: 33 + h]
                K.rd("vector", [6], lambda vv, rrep_sb=rrep_sb: vv.tensor_tensor(
                    ffn_tmp[0:HD, :], pbig[1][0:HD, 0:512], rrep_sb[:], op=ALU.mult),
                    extra=[ev_rr])
                prev_av_tt[0] = K.op("vector", lambda vv, cc=cc, ph=ph, qg=qg, bv=bv_col: vv.tensor_scalar(
                    av_sb[ph * HD: ph * HD + HD, cc * T + qg * 512: cc * T + qg * 512 + 512],
                    ffn_tmp[0:HD, :], bv, None, op0=ALU.add))
                if step + 1 < 2 * TQ:
                    emit_p1_finish((step + 1) % 2, (step + 1) // 2, step + 1)
            kones_free[0] = (pe, pe.n)
            kones_free[1] = (pe, pe.n)
        ev_av = (v, v.n)
        x0r = x0.rearrange("(dc p) t -> p dc t", p=128)
        wp_free = [None, None]
        for pc in range(DC):
            wpar = pc % 2
            ev_wp = K.op("gpsimd", lambda g, pc=pc, wpar=wpar: g.dma_start(
                out=wpT_buf[wpar][:].rearrange("p (cc q) -> p cc q", cc=DC),
                in_=WpT[:, ts(pc, 128)].rearrange("(cc p) q -> p cc q", p=128)),
                waits=[wp_free[wpar]], dma=True, dma_sem=f"d_wp{wpar}")
            for tau in range(TQ):
                spar = (pc * TQ + tau) % 2
                ev_x0 = K.op("sync", lambda e, pc=pc, tau=tau, spar=spar: e.dma_start(
                    out=x0t[spar][:], in_=x0r[:, pc, ts(tau, 512)]),
                    waits=[x0_free[spar]], dma=True, dma_sem=f"d_x{spar}")
                for cc in range(DC):
                    K.mm([spar], lambda t, cc=cc, tau=tau, spar=spar, wpar=wpar: t.matmul(
                        pb[spar][:, 0:512],
                        wpT_buf[wpar][:, ts(cc, 128)],
                        av_sb[:, cc * T + tau * 512: cc * T + tau * 512 + 512],
                        start=(cc == 0), stop=(cc == DC - 1)),
                        extra=([ev_av, ev_wp] if cc == 0 else []),
                        first=(cc == 0), last=(cc == DC - 1))
                K.rd("vector", [spar], lambda vv, pc=pc, tau=tau, spar=spar: vv.scalar_tensor_tensor(
                    ffn_tmp[:], pb[spar][:, 0:512], bp_sb[:, pc:pc + 1],
                    sg[:, pc * T + tau * 512: pc * T + tau * 512 + 512],
                    op0=ALU.add, op1=ALU.mult), extra=[ev_g])
                ev_xt = K.op("vector", lambda vv, spar=spar: vv.tensor_tensor(
                    x0t[spar][:], x0t[spar][:], ffn_tmp[:], op=ALU.add), waits=[ev_x0])
                x0_free[spar] = K.op("sync", lambda e, pc=pc, tau=tau, spar=spar: e.dma_start(
                    out=x_d[:, pc, ts(tau, 512)], in_=x0t[spar][:]), waits=[ev_xt],
                    dma=True, dma_sem=f"d_xs{spar}")
            wp_free[wpar] = (pe, pe.n)

    def phase_l1small(xn_sb, pre, xr_bf, oow_bf, vpow_bf, opw_buf, vpiw_bf,
                      sq_t, gsil_t, s1_t, x1t, vqs):
        (params_sb, opb_sb, om_u, om_ut, om, om_n, om2, om3, Gt, G_bf, vq_row, vq_sq,
         vq_n, cb_sq, cb_n, cb_sc, vqn_rep, vq_gc, vq_ind, code_row, code_col,
         omix_col, vqmix_col, vqc_cols) = vqs
        ev_xn = K.op("sync", lambda e: e.dma_start(
            out=xn_sb[:].rearrange("p (dc t) -> p dc t", dc=DC), in_=xn_d[:, :, :]), dma=True)
        ev_oow = K.op("gpsimd", lambda g: g.dma_start(out=oow_bf[:], in_=oowT), dma=True)
        ev_vpow = K.op("gpsimd", lambda g: g.dma_start(out=vpow_bf[:], in_=vpowT), dma=True)
        ev_vpiw = K.op("sync", lambda e: e.dma_start(
            out=vpiw_bf[:].rearrange("p (dc c) -> p dc c", dc=DC),
            in_=vpiwT.rearrange("(dc p) c -> p dc c", p=128)), dma=True)
        ev_hm = state["hm"]
        ev_opb = K.op("sync", lambda e: e.dma_start(out=opb_sb[:], in_=opb), dma=True)
        opw_free = [None, None, None]
        for dcb in range(DC):
            par = dcb % 3
            ev_opw = K.op("gpsimd", lambda g, dcb=dcb, par=par: g.dma_start(
                out=opw_buf[par][:], in_=opwT[ts(dcb, 128), :]), waits=[opw_free[par]],
                dma=True, dma_sem=f"d_opw{par}")
            for seg_ in range(4):
                mk = K.mm([seg_], lambda t, dcb=dcb, seg_=seg_, par=par: t.matmul(
                    pb[seg_][0:1, 0:504], hm_bf[:, dcb:dcb + 1], opw_buf[par][:, ts(seg_, 504)],
                    start=(dcb == 0), stop=(dcb == DC - 1)),
                    extra=[ev_opw, ev_hm], first=(dcb == 0), last=True)
            opw_free[par] = mk
        for seg_ in range(4):
            K.rd("scalar", [seg_], lambda s, seg_=seg_: s.copy(
                params_sb[0:1, ts(seg_, 504)], pb[seg_][0:1, 0:504]))
        ev_p = K.op("vector", lambda vv: vv.tensor_tensor(
            params_sb[:], params_sb[:], opb_sb[:], op=ALU.add),
            waits=[(sc, sc.n), ev_opb])
        ev_mz = K.op("vector", lambda vv: vv.memset(om_u[:], 0.0))
        off = 0
        ev_tri = None
        for r in range(63):
            ln_ = 63 - r
            ev_tri = K.op("sync", lambda e, r=r, off=off, ln_=ln_: e.dma_start(
                out=om_u[r:r + 1, r + 1:64], in_=params_sb[0:1, off:off + ln_]),
                waits=[ev_p, ev_mz], dma=True)
            off += ln_
        mt = K.mm([4], lambda t: t.transpose(pbig[0][0:64, 0:64], om_u[:], ident_sb[0:64, 0:64]),
                  extra=[ev_tri])
        ev_ut = K.rd("scalar", [4], lambda s: s.copy(om_ut[:], pbig[0][0:64, 0:64]))
        e_om = K.op("vector", lambda vv: vv.tensor_tensor(om[:], om_u[:], om_ut[:], op=ALU.subtract),
                    waits=[ev_ut, ev_tri])
        e_nom = K.op("vector", lambda vv: vv.tensor_tensor(om_n[:], om_ut[:], om_u[:], op=ALU.subtract))
        m2 = K.mm([4], lambda t: t.matmul(pbig[0][0:64, 64:128], om_n[:], om[:], start=True, stop=True),
                  extra=[e_nom])
        ev_o2 = K.rd("scalar", [4], lambda s: s.copy(om2[:], pbig[0][0:64, 64:128]))
        m3 = K.mm([4], lambda t: t.matmul(pbig[0][0:64, 128:192], om2[:], om[:], start=True, stop=True),
                  extra=[ev_o2])
        ev_o3 = K.rd("scalar", [4], lambda s: s.copy(om3[:], pbig[0][0:64, 128:192]))
        K.op("vector", lambda vv: vv.tensor_tensor(Gt[:], om[:], ident_sb[0:64, 0:64], op=ALU.add),
             waits=[ev_o3])
        K.op("vector", lambda vv: vv.scalar_tensor_tensor(Gt[:], om2[:], 0.5, Gt[:],
                                                          op0=ALU.mult, op1=ALU.add))
        K.op("vector", lambda vv: vv.scalar_tensor_tensor(Gt[:], om3[:], 1.0 / 6.0, Gt[:],
                                                          op0=ALU.mult, op1=ALU.add))
        ev_G = K.op("vector", lambda vv: vv.tensor_copy(G_bf[:], Gt[:]))
        ev_xr = None
        for tau in range(TQ):
            mm_ = K.mm([6, 7], lambda t, tau=tau: t.matmul(
                pbig[1][0:64, 0:512], G_bf[:], xn_sb[0:64, ts(tau, 512)], start=True, stop=True),
                extra=[ev_G, ev_xn])
            ev_xr = K.rd("scalar", [6, 7], lambda s, tau=tau: s.copy(xr_bf[0:64, ts(tau, 512)],
                                                                     pbig[1][0:64, 0:512]))
        for dcb in range(DC):
            K.mm([5], lambda t, dcb=dcb: t.matmul(
                pbig[0][0:1, 512:576], hm_cols[:, dcb:dcb + 1], vpiw_bf[:, ts(dcb, HD)],
                start=(dcb == 0), stop=(dcb == DC - 1)),
                extra=[ev_vpiw, ev_hm], first=(dcb == 0), last=(dcb == DC - 1))
        ev_vq0 = K.rd("vector", [5], lambda vv: vv.scalar_tensor_tensor(
            vq_row[:], pbig[0][0:1, 512:576], 1.0 / T, vpib_sb[:],
            op0=ALU.mult, op1=ALU.add))
        ev_sq = K.op("scalar", lambda s: s.activation(vq_sq[:], vq_row[:], AF.Square),
                     waits=[ev_vq0])
        e1 = K.op("vector", lambda vv: vv.tensor_reduce(vq_n[0:1, 0:1], vq_sq[:], AX.X, ALU.add),
                  waits=[ev_sq])
        e2 = K.op("scalar", lambda s: s.activation(vq_n[0:1, 1:2], vq_n[0:1, 0:1], AF.Sqrt),
                  waits=[e1])
        K.op("vector", lambda vv: vv.tensor_scalar(vq_n[0:1, 2:3], vq_n[0:1, 1:2], 1e-12, None, op0=ALU.max),
             waits=[e2])
        K.op("vector", lambda vv: vv.reciprocal(vq_n[0:1, 3:4], vq_n[0:1, 2:3]))
        ev_vqn = K.op("vector", lambda vv: vv.tensor_scalar(vq_row[:], vq_row[:], vq_n[0:1, 3:4], None, op0=ALU.mult))
        ev_cbsq = K.op("scalar", lambda s: s.activation(cb_sq[:], cb_sb[:], AF.Square),
                       waits=[ev_consts])
        K.op("vector", lambda vv: vv.tensor_reduce(cb_n[:, 0:1], cb_sq[:, 0:64], AX.X, ALU.add),
             waits=[ev_cbsq])
        e3 = K.op("vector", lambda vv: vv.tensor_reduce(cb_n[:, 1:2], cb_sq[:, 64:128], AX.X, ALU.add))
        e4 = K.op("scalar", lambda s: s.activation(cb_n[:, 2:4], cb_n[:, 0:2], AF.Sqrt),
                  waits=[e3])
        K.op("vector", lambda vv: vv.tensor_scalar(cb_n[:, 2:4], cb_n[:, 2:4], 1e-12, None, op0=ALU.max),
             waits=[e4])
        ev_rcb = K.op("vector", lambda vv: vv.reciprocal(cb_n[:, 4:6], cb_n[:, 2:4]))
        mmn = K.mm([5], lambda t: t.matmul(pbig[0][:, 576:640], ones_sb[0:1, :], vq_row[:],
                                           start=True, stop=True), extra=[ev_vqn])
        ev_vqr = K.rd("scalar", [5], lambda s: s.copy(vqn_rep[:], pbig[0][:, 576:640]))
        K.op("vector", lambda vv: vv.tensor_tensor(cb_sc[:, 0:64], cb_sb[:, 0:64], vqn_rep[:], op=ALU.mult),
             waits=[ev_vqr])
        K.op("vector", lambda vv: vv.tensor_tensor(cb_sc[:, 64:128], cb_sb[:, 64:128], vqn_rep[:], op=ALU.mult))
        K.op("vector", lambda vv: vv.tensor_reduce(cb_n[:, 6:7], cb_sc[:, 0:64], AX.X, ALU.add))
        K.op("vector", lambda vv: vv.tensor_reduce(cb_n[:, 7:8], cb_sc[:, 64:128], AX.X, ALU.add))
        ev_scr = K.op("vector", lambda vv: vv.tensor_tensor(cb_n[:, 6:8], cb_n[:, 6:8], cb_n[:, 4:6], op=ALU.mult),
                      waits=[ev_rcb])
        ev_m1 = K.op("vector", lambda vv: vv.tensor_reduce(cb_n[:, 8:9], cb_n[:, 6:8], AX.X, ALU.max))
        mtx = K.mm([4], lambda t: t.transpose(pbig[0][0:1, 0:128], cb_n[:, 8:9], ident_sb[:]),
                   extra=[ev_m1])
        ev_mrow = K.rd("scalar", [4], lambda s: s.copy(vq_n[0:1, 8:136], pbig[0][0:1, 0:128]))
        ev_gmax = K.op("vector", lambda vv: vv.tensor_reduce(vq_n[0:1, 4:5], vq_n[0:1, 8:136], AX.X, ALU.max),
                       waits=[ev_mrow])
        mmg = K.mm([4], lambda t: t.matmul(pbig[0][:, 128:129], ones_sb[0:1, :], vq_n[0:1, 4:5],
                                           start=True, stop=True), extra=[ev_gmax])
        ev_gcol = K.rd("scalar", [4], lambda s: s.copy(vq_gc[:], pbig[0][:, 128:129]))
        ev_ind2 = K.op("vector", lambda vv: vv.tensor_scalar(vq_ind[:], cb_n[:, 6:8], vq_gc[:], None, op0=ALU.is_equal),
                       waits=[ev_gcol, ev_scr])
        for j in range(2):
            K.mm([4], lambda t, j=j: t.matmul(pbig[0][0:1, 192:256], vq_ind[:, j:j + 1],
                                              cb_sb[:, ts(j, 64)], start=(j == 0), stop=(j == 1)),
                 extra=[ev_ind2], first=(j == 0), last=(j == 1))
        ev_coder = K.rd("scalar", [4], lambda s: s.copy(code_row[:], pbig[0][0:1, 192:256]))
        mtc = K.mm([4], lambda t: t.transpose(pbig[0][0:64, 256:257], code_row[:], ident_sb[0:1, 0:1]),
                   extra=[ev_coder])
        ev_codec = K.rd("scalar", [4], lambda s: s.copy(code_col[:], pbig[0][0:64, 256:257]))
        for i, dst in ((0, omix_col), (1, vqmix_col)):
            K.mm([4], lambda t, i=i: t.matmul(pbig[0][:, 300 + i:301 + i], ones_sb[0:1, :],
                                              mixes_sb[0:1, i:i + 1], start=True, stop=True),
                 extra=[ev_consts])
            K.rd("scalar", [4], lambda s, dst=dst, i=i: s.copy(dst[:], pbig[0][:, 300 + i:301 + i]))
        ev_mix = (sc, sc.n)
        for pc in range(DC):
            K.mm([4], lambda t, pc=pc: t.matmul(pbig[0][:, 320 + pc:321 + pc],
                                                vpow_bf[:, ts(pc, 128)], code_col[:],
                                                start=True, stop=True),
                 extra=[ev_vpow, ev_codec], first=(pc == 0), last=(pc == DC - 1))
        K.rd("vector", [4], lambda vv: vv.tensor_tensor(
            vqc_cols[:], pbig[0][:, 320:328], vpobc_sb[:], op=ALU.add), extra=[ev_mix])
        ev_vqc = K.op("vector", lambda vv: vv.tensor_scalar(vqc_cols[:], vqc_cols[:], vqmix_col[:], None, op0=ALU.mult))
        lie_free = [None, None]
        x1_free = [None, None]
        ss_read = None
        def emit_prenorm(tau):
            for pc in range(DC):
                lpar = pc % 2
                mmk = K.mm([lpar], lambda t, pc=pc, tau=tau, lpar=lpar: t.matmul(
                    pb[lpar][:, 0:512], oow_bf[:, ts(pc, 128)], xr_bf[0:64, ts(tau, 512)],
                    start=True, stop=True), extra=[ev_oow, lie_free[lpar]])
                K.rd("vector", [lpar], lambda vv, pc=pc, lpar=lpar: vv.tensor_scalar(
                    ffn_tmp[:], pb[lpar][:, 0:512], oobc_sb[:, pc:pc + 1], omix_col[:],
                    op0=ALU.add, op1=ALU.mult))
                prt = pre[:, pc * T + tau * 512: pc * T + tau * 512 + 512]
                ev_pre = K.op("vector", lambda vv, pc=pc, tau=tau, prt=prt: vv.scalar_tensor_tensor(
                    prt, ffn_tmp[:], vqc_cols[:, pc:pc + 1],
                    xn_sb[:, pc * T + tau * 512: pc * T + tau * 512 + 512],
                    op0=ALU.add, op1=ALU.add))
                lie_free[lpar] = ev_pre
                ev_sq = K.op("scalar", lambda s, prt=prt: s.activation(sq_t[:], prt, AF.Square),
                             waits=[ev_pre, state.get("l1_sq_free")])
                state["l1_sq_free"] = K.mm(
                    [2], lambda t, pc=pc: t.matmul(pb[2][0:1, 0:512], ones_sb[:, 0:1], sq_t[:],
                                                   start=(pc == 0), stop=(pc == DC - 1)),
                    extra=[ev_sq], first=(pc == 0), last=True)

        emit_prenorm(0)
        for tau in range(TQ):
            e1 = K.rd("vector", [2], lambda vv: vv.tensor_scalar(
                rowC[0:1, 0:512], pb[2][0:1, 0:512], 1.0 / D, RMS_EPS, op0=ALU.mult, op1=ALU.add))
            e2 = K.op("scalar", lambda s: s.activation(rowC[0:1, 0:512], rowC[0:1, 0:512],
                                                       AF.Sqrt), waits=[e1])
            e3 = K.op("vector", lambda vv: vv.reciprocal(rowC[0:1, 0:512], rowC[0:1, 0:512]),
                      waits=[e2])
            if tau + 1 < TQ:
                emit_prenorm(tau + 1)
            K.mm([3], lambda t: t.matmul(pb[3][:, 0:512], ones_sb[0:1, :], rowC[0:1, 0:512],
                                         start=True, stop=True), extra=[e3])
            for pc in range(DC):
                xpar = pc % 2
                prt = pre[:, pc * T + tau * 512: pc * T + tau * 512 + 512]
                ev_xl = K.op("sync", lambda e, pc=pc, tau=tau, xpar=xpar: e.dma_start(
                    out=x1t[xpar][:], in_=x_d[:, pc, ts(tau, 512)]),
                    waits=[x1_free[xpar]], dma=True, dma_sem=f"d_x{xpar}")
                ev_sil = K.op("scalar", lambda s, pc=pc, tau=tau: s.activation(
                    gsil_t[:], sg[:, pc * T + tau * 512: pc * T + tau * 512 + 512], AF.Silu),
                    waits=[state[("ffn_v", 1)], state.get("l1_gsil_free")])
                ev_s1 = K.rd("vector", [3], lambda vv, prt=prt: vv.tensor_tensor(
                    s1_t[:], prt, pb[3][:, 0:512], op=ALU.mult))
                ev_s1b = K.op("vector", lambda vv, pc=pc: vv.scalar_tensor_tensor(
                    s1_t[:], s1_t[:], nwc_sb[:, pc:pc + 1], gsil_t[:], op0=ALU.mult, op1=ALU.mult),
                    waits=[ev_sil])
                state["l1_gsil_free"] = ev_s1b
                ev_fin = K.op("vector", lambda vv, xpar=xpar: vv.tensor_tensor(
                    x1t[xpar][:], x1t[xpar][:], s1_t[:], op=ALU.add), waits=[ev_xl])
                x1_free[xpar] = K.op("sync", lambda e, pc=pc, tau=tau, xpar=xpar: e.dma_start(
                    out=out_fm[:, pc, ts(tau, 512)], in_=x1t[xpar][:]), waits=[ev_fin],
                    dma=True, dma_sem=f"d_xs{xpar}")

    with nc.sbuf_tensor("xs0", [128, DC * T], F32) as xs0, \
         nc.sbuf_tensor("xn0", [128, DC * T], BF16) as xn0, \
         nc.sbuf_tensor("zero_sb", [128, 1024], BF16) as zero_sb, \
         nc.sbuf_tensor("lnsq0", [128, 512], F32) as lnsq0, \
         nc.sbuf_tensor("lnsq1", [128, 512], F32) as lnsq1, \
         nc.sbuf_tensor("rowA0", [1, 512], F32) as rowA0, \
         nc.sbuf_tensor("rowB0", [1, 512], F32) as rowB0:
        ev_zmem = K.op("vector", lambda vv: vv.memset(zero_sb[:], 0.0))
        for i in range(E * CAP // 128):
            K.op("sync", lambda e, i=i: e.dma_start(
                out=xg_d[i * 128:(i + 1) * 128, :], in_=zero_sb[:]), waits=[ev_zmem], dma=True)
        phase_ln(0, x0.rearrange("(dc p) t -> p dc t", p=128), xs0, xn0, False, (lnsq0, lnsq1), rowA0, rowB0)
        K.bar()

    with contextlib.ExitStack() as stf0:
        sbf = lambda n, sh, dt, td=False: stf0.enter_context(
            nc.sbuf_tensor(n, sh, dt, is_transpose_dest=td))
        xgfm0 = (sbf("xgfm00", [128, DC * CAP], BF16, True), sbf("xgfm01", [128, DC * CAP], BF16, True))
        hact0 = sbf("hact0", [128, HC * CAP], BF16)
        w1a = tuple(sbf(f"w1a{i}", [128, DC * 256], BF16) for i in range(6))
        w2a = tuple(sbf(f"w2a{i}", [128, 1024], BF16) for i in range(6))
        ytma = (sbf("ytma0", [128, (CAP // 128) * 1024], BF16), sbf("ytma1", [128, (CAP // 128) * 1024], BF16))
        xntma = (sbf("xntma0", [128, 1024], BF16, True), sbf("xntma1", [128, 1024], BF16, True))
        b2ra = (sbf("b2ra0", [1, D], F32), sbf("b2ra1", [1, D], F32))
        phase_ffn_sparse(0, xgfm0, hact0, w1a, w2a, ytma, xntma, b2ra, True)
        K.bar()
    if sg_d is not None:
        K.op("sync", lambda e: e.dma_start(out=sg_d, in_=sg[:]), dma=True)
        K.bar()

    with contextlib.ExitStack() as st3:
        sb3 = lambda n, sh, dt, td=False: st3.enter_context(
            nc.sbuf_tensor(n, sh, dt, is_transpose_dest=td))
        xna = sb3("xna", [128, DC * T], BF16)
        av_sb = sb3("av", [128, DC * T], BF16)
        wpT0 = sb3("wpT0", [128, DC * 128], BF16)
        wpT1 = sb3("wpT1", [128, DC * 128], BF16)
        attnTa = sb3("attnTa", [128, TC * 512], BF16)
        attnTb = sb3("attnTb", [128, TC * 512], BF16)
        Qba = sb3("Qba", [128, T], BF16)
        Qbb = sb3("Qbb", [128, T], BF16)
        Kba = sb3("Kba", [128, T], BF16)
        Kbb = sb3("Kbb", [128, T], BF16)
        Vba = sb3("Vba", [128, TC * 65], BF16)
        Vbb = sb3("Vbb", [128, TC * 65], BF16)
        qkvb0 = sb3("qkvb0", [128, 3 * 1024], BF16)
        qkvb1 = qkvb0
        rrepa = sb3("rrepa", [HD, 512], F32)
        rrepb = sb3("rrepb", [HD, 512], F32)
        nmcols = sb3("nmcols", [128, 10], F32)
        rrow = sb3("rrow", [1, 512], F32)
        x0t0 = sb3("x0t0", [128, 512], F32)
        x0t1 = sb3("x0t1", [128, 512], F32)
        phase_attn(xna, av_sb, (wpT0, wpT1), (attnTa, attnTb), (Qba, Qbb),
                   (Kba, Kbb), (Vba, Vbb), (qkvb0, qkvb1), (rrepa, rrepb),
                   nmcols, rrow, (x0t0, x0t1))
        K.bar()

    with nc.sbuf_tensor("xs1", [128, DC * T], F32) as xs1, \
         nc.sbuf_tensor("xn1", [128, DC * T], BF16) as xn1, \
         nc.sbuf_tensor("lnsq0b", [128, 512], F32) as lnsq0b, \
         nc.sbuf_tensor("lnsq1b", [128, 512], F32) as lnsq1b, \
         nc.sbuf_tensor("rowA1", [1, 512], F32) as rowA1, \
         nc.sbuf_tensor("rowB1", [1, 512], F32) as rowB1:
        phase_ln(1, x_d, xs1, xn1, True, (lnsq0b, lnsq1b), rowA1, rowB1)
        K.bar()

    with contextlib.ExitStack() as stf1:
        sbf = lambda n, sh, dt, td=False: stf1.enter_context(
            nc.sbuf_tensor(n, sh, dt, is_transpose_dest=td))
        xgfm1 = (sbf("xgfm10", [128, DC * CAP], BF16, True), sbf("xgfm11", [128, DC * CAP], BF16, True))
        hact1 = sbf("hact1", [128, HC * CAP], BF16)
        w1b = tuple(sbf(f"w1b{i}", [128, DC * 256], BF16) for i in range(6))
        w2b = tuple(sbf(f"w2b{i}", [128, 1024], BF16) for i in range(6))
        ytmb = (sbf("ytmb0", [128, (CAP // 128) * 1024], BF16), sbf("ytmb1", [128, (CAP // 128) * 1024], BF16))
        xntmb = (sbf("xntmb0", [128, 1024], BF16, True), sbf("xntmb1", [128, 1024], BF16, True))
        b2rb = (sbf("b2rb0", [1, D], F32), sbf("b2rb1", [1, D], F32))
        phase_ffn_sparse(1, xgfm1, hact1, w1b, w2b, ytmb, xntmb, b2rb, False, precast=False)
        K.bar()
    if sg_d1 is not None:
        K.op("sync", lambda e: e.dma_start(out=sg_d1, in_=sg[:]), dma=True)
        K.bar()

    with contextlib.ExitStack() as st6:
        sb6 = lambda n, sh, dt: st6.enter_context(nc.sbuf_tensor(n, sh, dt))
        xnb = sb6("xnb", [128, DC * T], BF16)
        pre = sb6("pre", [128, DC * T], BF16)
        xr_bf = sb6("xr", [64, T], BF16)
        oow_bf = sb6("oowb", [HD, D], BF16)
        vpow_bf = sb6("vpowb", [HD, D], BF16)
        opwb0 = sb6("opwb0", [128, 2016], BF16)
        opwb1 = sb6("opwb1", [128, 2016], BF16)
        opwb2 = sb6("opwb2", [128, 2016], BF16)
        vpiw_bf = sb6("vpiwb", [128, DC * HD], F32)
        params_sb = sb6("params", [1, 2016], F32)
        om_u = sb6("om_u", [64, 64], F32)
        om_ut = sb6("om_ut", [64, 64], F32)
        om = sb6("om", [64, 64], F32)
        om_n = sb6("om_n", [64, 64], F32)
        om2 = sb6("om2", [64, 64], F32)
        om3 = sb6("om3", [64, 64], F32)
        Gt = sb6("Gt", [64, 64], F32)
        G_bf = sb6("G_bf", [64, 64], BF16)
        vq_row = sb6("vq_row", [1, 64], F32)
        vq_sq = sb6("vq_sq", [1, 64], F32)
        vq_n = sb6("vq_n", [1, 256], F32)
        cb_sq = sb6("cb_sq", [128, 128], F32)
        cb_n = sb6("cb_n", [128, 9], F32)
        cb_sc = sb6("cb_sc", [128, 128], F32)
        vqn_rep = sb6("vqn_rep", [128, 64], F32)
        vq_gc = sb6("vq_gc", [128, 1], F32)
        vq_ind = sb6("vq_ind", [128, 2], F32)
        code_row = sb6("code_row", [1, 64], F32)
        code_col = sb6("code_col", [64, 1], BF16)
        omix_col = sb6("omix", [128, 1], F32)
        vqmix_col = sb6("vqmix", [128, 1], F32)
        vqc_cols = sb6("vqc_cols", [128, DC], F32)
        sq_t = sb6("sq_t", [128, 512], F32)
        gsil_t = sb6("gsil_t", [128, 512], F32)
        s1_t = sb6("s1_t", [128, 512], F32)
        x1t0 = sb6("x1t0", [128, 512], F32)
        x1t1 = sb6("x1t1", [128, 512], F32)
        vqs = (params_sb, opb_sb, om_u, om_ut, om, om_n, om2, om3, Gt, G_bf, vq_row,
               vq_sq, vq_n, cb_sq, cb_n, cb_sc, vqn_rep, vq_gc, vq_ind,
               code_row, code_col, omix_col, vqmix_col, vqc_cols)
        phase_l1small(xnb, pre, xr_bf, oow_bf, vpow_bf, (opwb0, opwb1, opwb2), vpiw_bf,
                      sq_t, gsil_t, s1_t, (x1t0, x1t1), vqs)

    K.bar()

    with nc.Block() as block:
        K.replay(block)
    stk.close()
    return nc


_NC_CACHE = {}


def get_nc(dbg=()):
    key = tuple(sorted(dbg))
    if key not in _NC_CACHE:
        _NC_CACHE[key] = build(key)
    return _NC_CACHE[key]


def host_inputs(inputs):
    f = np.float32
    a = lambda x: np.ascontiguousarray(x, dtype=f)
    g = {}
    g["ident"] = np.eye(128, dtype=f)
    g["ones"] = np.ones((128, 128), dtype=f)
    g["Wqkv"] = a(inputs["L0_attn_Wqkv"])
    bqkv = np.asarray(inputs["L0_attn_bqkv"], dtype=f)
    g["bqkvT"] = a(bqkv.T)
    g["bqkv_flat"] = a(bqkv.reshape(1, -1))
    g["WpT"] = a(np.asarray(inputs["L0_attn_Wp"], dtype=f).T)
    g["bp_cols"] = a(np.asarray(inputs["L0_attn_bp"], dtype=f).reshape(8, 128).T)
    g["Wg0"] = a(inputs["L0_ffn_Wg"])
    g["bg0"] = a(np.asarray(inputs["L0_ffn_bg"], dtype=f).reshape(1, -1))
    g["W1_0"] = a(inputs["L0_ffn_W1"])
    g["W2_0"] = a(inputs["L0_ffn_W2"])
    g["Wg1"] = a(np.asarray(inputs["L1_ffn_gate_w"], dtype=f).T)
    g["bg1"] = a(np.asarray(inputs["L1_ffn_gate_b"], dtype=f).reshape(1, -1))
    g["W1_1"] = a(np.asarray(inputs["L1_ffn_w1"], dtype=f).transpose(0, 2, 1))
    g["W2_1"] = a(np.asarray(inputs["L1_ffn_w2"], dtype=f).transpose(0, 2, 1))
    g["b1c"] = a(np.asarray(inputs["L1_ffn_b1"], dtype=f).reshape(8, 32, 128).transpose(0, 2, 1))
    g["b2c"] = a(np.asarray(inputs["L1_ffn_b2"], dtype=f).reshape(8, 8, 128).transpose(0, 2, 1))
    g["opwT"] = a(np.asarray(inputs["L1_attn_omega_proj_w"], dtype=f).T)
    g["opb"] = a(np.asarray(inputs["L1_attn_omega_proj_b"], dtype=f).reshape(1, -1))
    g["oowT"] = a(np.asarray(inputs["L1_attn_omega_out_w"], dtype=f).T)
    g["oobc"] = a(np.asarray(inputs["L1_attn_omega_out_b"], dtype=f).reshape(8, 128).T)
    g["vpiwT"] = a(np.asarray(inputs["L1_attn_vq_proj_in_w"], dtype=f).T)
    g["vpib"] = a(np.asarray(inputs["L1_attn_vq_proj_in_b"], dtype=f).reshape(1, -1))
    g["vpowT"] = a(np.asarray(inputs["L1_attn_vq_proj_out_w"], dtype=f).T)
    g["vpobc"] = a(np.asarray(inputs["L1_attn_vq_proj_out_b"], dtype=f).reshape(8, 128).T)
    g["lt128"] = np.triu(np.ones((128, 128), dtype=f))
    g["ecm1"] = np.tile((np.arange(8, dtype=f) * 384.0 - 1.0)[None, :], (128, 1))
    g["b2r"] = a(np.asarray(inputs["L1_ffn_b2"], dtype=f).reshape(8, 1, 1024))
    cb = np.asarray(inputs["L1_attn_codebook"], dtype=f)
    g["cbp"] = a(np.concatenate([cb[0:128], cb[128:256]], axis=1))
    g["mixes"] = np.array([[float(inputs["L1_attn_omega_mix"]),
                            float(inputs["L1_attn_vq_mix"])]], dtype=f)
    g["nwc"] = a(np.asarray(inputs["L1_attn_norm_w"], dtype=f).reshape(8, 128).T)
    return g


def kernel(**inputs):
    nc = get_nc()
    shared = host_inputs(inputs)
    x = np.asarray(inputs["L0_attn_x"], dtype=np.float32)
    in_maps = []
    for b in range(NCORES):
        m = dict(shared)
        m["x0"] = np.ascontiguousarray(x[b].T)
        in_maps.append(m)
    res = run_bass_kernel_spmd(nc, in_maps, core_ids=list(range(NCORES)))
    outs = []
    for b in range(NCORES):
        o_ = np.asarray(res.results[b]["out_fm"])
        outs.append(o_.transpose(1, 0, 2).reshape(D, T).T)
    return np.stack(outs, axis=0)
